# version 07
# baseline (speedup 1.0000x reference)
import numpy as np
import concourse.bass as bass
import concourse.mybir as mybir
from concourse.bass_utils import run_bass_kernel_spmd
from concourse.bass_core_rust import IndirectOffsetOnAxis

F32 = mybir.dt.float32
I32 = mybir.dt.int32
BF16 = mybir.dt.bfloat16
AF = mybir.ActivationFunctionType
ALU = mybir.AluOpType
AX = mybir.AxisListType

D = 1024
NDC = 8
NH = 16
HD = 64
I_DIM = 2048
NIC = 16
NE = 8
EPS = 1e-5


class Ctr:
    def __init__(self, nc, name):
        self.h = nc.alloc_semaphore(name)
        self.v = 0
        self.name = name


class Prog:
    ENGINES = ("sync", "gpsimd", "tensor", "vector", "scalar")

    def __init__(self, nc):
        self.nc = nc
        self.ops = {e: [] for e in self.ENGINES}

    def ctr(self, name):
        return Ctr(self.nc, name)

    def add(self, eng, emit, waits=(), ctr=None, n=1):
        self.ops[eng].append((emit, [w for w in waits if w is not None and w[1] is not None and w[1] > 0],
                              ctr, n))
        if ctr is None:
            return None
        ctr.v += n
        return ctr.v

    def dma(self, eng, out, in_, waits=(), ctr=None, **kw):
        return self.add(eng, lambda e, o=out, i=in_, kw=kw: e.dma_start(o, i, **kw),
                        waits, ctr, 16)

    def emit_all(self):
        with self.nc.Block() as block:
            for eng in self.ENGINES:
                ops = self.ops[eng]

                def runner(engobj, ops=ops):
                    last_wait = {}
                    for emit, waits, ctr, n in ops:
                        for (c, v) in waits:
                            if last_wait.get(id(c), 0) >= v:
                                continue
                            engobj.wait_ge(c.h, v)
                            last_wait[id(c)] = v
                        instr = emit(engobj)
                        if ctr is not None:
                            instr.then_inc(ctr.h, n)
                getattr(block, eng)(runner)


def build_program(T=4096, S=2048, dbg=(), stop_after_postproc=0):
    nseq = T // S
    STC = S // 128
    NQT = S // 512
    NKC = S // 128
    TH = T // 2
    NHALF = 2
    HTC = TH // 128
    NTT = TH // 512

    nc = bass.Bass("TRN2", target_bir_lowering=False)
    p = Prog(nc)

    IN = lambda n, s: nc.dram_tensor(n, s, F32, kind="ExternalInput").ap()
    x0_h = IN("x0", [T, D])
    decay_h = IN("decay", [1, 1])
    b_h = IN("b_row", [1, D]); gamma_h = IN("gamma_row", [1, D]); beta_h = IN("beta_row", [1, D])
    ln1w_h = IN("ln1w_row", [1, D]); ln2w_h = IN("ln2w_row", [1, D])
    wT_h = IN("wT", [D, D])
    wqT_h = IN("wqT", [D, D]); wkT_h = IN("wkT", [D, D])
    wvT_h = IN("wvT", [D, D]); woT_h = IN("woT", [D, D])
    gateT_h = IN("gateT", [D, NE])
    w1T_h = IN("w1T", [NE, D, I_DIM]); w3T_h = IN("w3T", [NE, D, I_DIM])
    w2T_h = IN("w2T", [NE, I_DIM, D])
    out_h = nc.dram_tensor("out", [T, D], F32, kind="ExternalOutput").ap()

    def scratch(name, shape, dt):
        kind = "ExternalOutput" if name in dbg else "Internal"
        return nc.dram_tensor(name, shape, dt, kind=kind).ap()

    x1_h = scratch("x1_s", [T, D], F32)
    z_h = scratch("z_s", [T, D], F32)
    f_h = scratch("f_s", [T + 128, D], F32)
    qt_h = scratch("qt_s", [nseq, D, S], BF16)
    kt_h = scratch("kt_s", [nseq, D, S], BF16)
    v_h = scratch("v_s", [T, D], BF16)
    aot_h = scratch("aot_s", [nseq, D, S], BF16)
    r2t_h = scratch("r2t_s", [NDC, 128, T], BF16)
    r2tm_h = scratch("r2tm_s", [T + 128, D], BF16)
    wq3_h = scratch("wq3_s", [24, T], F32)
    NT_SP = (2 * (T // 2)) // 512 + NE
    flat0_h = scratch("flat0_s", [NT_SP * 512, 2], F32)
    flat1_h = scratch("flat1_s", [NT_SP * 512, 2], F32)
    flat_hs = (flat0_h, flat1_h)
    stage_h = scratch("stage_s", [2, 6, T // 2], F32)
    wcol_dbg_h = nc.dram_tensor("wcol_s", [128, 32], I32,
                                kind="ExternalOutput" if "wcol_s" in dbg
                                else "Internal").ap()
    pq_dbg_h = scratch("pq_s", [2, 8, 16], F32)
    brow_dbg_h = scratch("brow_s", [2, 1, 32], F32)
    w1c_h = scratch("w1c_s", [NE * 128, NDC * I_DIM], BF16)
    w3c_h = scratch("w3c_s", [NE * 128, NDC * I_DIM], BF16)
    w2c_h = scratch("w2c_s", [NE * 128, NIC * D], BF16)
    we_h = scratch("we_s", [T // 128, 128, NE], F32)
    pow_h = scratch("pow_s", [1, 256], F32)
    yln_h = scratch("yln_s", [T, D], BF16)
    st_h = scratch("st_s", [T, D], BF16)
    yb_h = scratch("yb_s", [T, D], F32)

    sb = lambda n, s, d_: nc.alloc_sbuf_tensor(n, s, d_)
    b_b = sb("b_b", [128, D], F32)
    gam_b = sb("gam_b", [128, D], F32)
    bet_b = sb("bet_b", [128, D], F32)
    l1w_b = sb("l1w_b", [128, D], F32)
    l2w_b = sb("l2w_b", [128, D], F32)
    eps_c = sb("eps_c", [128, 1], F32)
    ones_bf = sb("ones_bf", [128, 128], BF16)
    ones_f = sb("ones_f", [128, 128], F32)
    ident_bf = sb("ident_bf", [128, 128], BF16)
    ident_f = sb("ident_f", [128, 128], F32)
    powx = sb("powx", [1, 256], F32)
    a_sc = sb("a_sc", [1, 1], F32)
    apow_b = sb("apow_b", [128, 128], F32)
    a128_c = sb("a128_c", [128, 1], F32)
    ltri = sb("ltri", [128, 128], BF16)
    gate_w = sb("gate_w", [128, NDC * NE], F32)
    we_all = sb("we_all", [128, (T // 128) * NE], F32)
    stat = sb("stat", [128, 96], F32)

    ra0, _ = nc.bump_sbuf(65536)
    AT = nc.alloc_sbuf_tensor_at
    wq_w = AT("wq_w", [128, NDC * D], BF16, offset=ra0)
    wk_w = AT("wk_w", [128, NDC * D], BF16, offset=ra0 + 16384)
    wv_w = AT("wv_w", [128, NDC * D], BF16, offset=ra0 + 32768)
    wo_w = AT("wo_w", [128, NDC * D], BF16, offset=ra0)
    wt_w = AT("wt_w", [128, NDC * D], BF16, offset=ra0 + 49152)
    w1s = AT("w1s", [128, NDC * I_DIM], BF16, offset=ra0)
    w3s = AT("w3s", [128, NDC * I_DIM], BF16, offset=ra0 + 32768)
    rb0, _ = nc.bump_sbuf(NDC * S * 2)
    r1t_s = AT("r1t_s", [128, NDC * S], BF16, offset=rb0)
    g_t = AT("g_t", [128, NIC * 512], BF16, offset=rb0)
    ysc = AT("ysc", [128, 2 * D], F32, offset=rb0 + 16384)
    r2g = AT("r2g", [128, 4 * D], BF16, offset=rb0 + 24576)
    rc0, _ = nc.bump_sbuf(53248)
    off = rc0
    def at(name, shape, dt):
        nonlocal off
        t = AT(name, shape, dt, offset=off)
        nbytes = int(np.prod(shape[1:])) * (2 if dt == BF16 else 4)
        off += (nbytes + 31) & ~31
        return t
    x0_t = at("x0_t", [128, 2 * D], F32)
    yln_t = at("yln_t", [128, D], BF16)
    st_t = at("st_t", [128, 2 * D], BF16)
    carry = at("carry", [128, 2 * NDC], F32)
    ybuf = at("ybuf", [128, D], F32)
    gbuf = at("gbuf", [128, D], F32)
    sqs = at("sqs", [128, D], F32)
    x1b = at("x1b", [128, 2 * D], F32)
    zb = at("zb", [128, 2 * D], F32)
    r1tmp = at("r1tmp", [128, D], F32)
    assert off - rc0 <= 49152, off - rc0
    off = rc0
    aoin = at("aoin", [128, 2 * NDC * 128], BF16)
    zin = at("zin", [128, 2 * D], F32)
    hb = at("hb", [128, D], F32)
    r2tmp = at("r2tmp", [128, D], F32)
    r2f = at("r2f", [128, D], F32)
    r2tf = at("r2tf", [128, NDC * 128], F32)
    r2tb = at("r2tb", [128, NDC * 256], BF16)
    logit = at("logit", [128, NE], F32)
    gs1 = at("gs1", [128, NE], F32)
    gs2 = at("gs2", [128, NE], F32)
    sq4 = at("sq4", [128, D], F32)
    wq24 = at("wq24", [128, 24], F32)
    wq24t = at("wq24t", [24, 128], F32)
    r2b_st = at("r2b_st", [128, 2 * D], BF16)
    assert off - rc0 <= 53248, off - rc0
    off = rc0
    w2s = at("w2s", [128, NIC * D], BF16)
    convst = AT("convst", [128, 2 * 1024], BF16, offset=rc0 + 49152)
    r2gT = at("r2gT", [128, NDC * 512], BF16)
    silu_t = at("silu_t", [128, 2 * 512], BF16)
    pairblk = at("pairblk", [128, 2 * 8], F32)
    posi6 = at("posi6", [128, 2 * 4], I32)
    assert off - rc0 <= 53248, off - rc0
    rd0, _ = nc.bump_sbuf(35712)
    off = rd0
    ybufB = at("ybufB", [128, D], F32)
    gbufB = at("gbufB", [128, D], F32)
    sqsB = at("sqsB", [128, D], F32)
    r1tmpB = at("r1tmpB", [128, D], F32)
    r1bB = AT("r1bB", [128, D], BF16, offset=rd0 + 24704)
    ylnB = AT("ylnB", [128, D], BF16, offset=rd0 + 26752)
    assert off - rd0 <= 35712, off - rd0
    off = rd0
    pstage = at("pstage", [128, 2 * 512], BF16)
    vstage = at("vstage", [128, 2 * 512], BF16)
    assert off - rd0 <= 35712, off - rd0
    off = rd0
    qt2 = at("qt2", [128, 2 * S], BF16)
    kt2 = at("kt2", [128, 2 * S], BF16)
    vh2 = at("vh2", [128, 2 * NKC * 130], BF16)
    mcols = at("mcols", [128, 32], F32)
    mrow_t = at("mrow_t", [16, 128], BF16)
    negm = at("negm", [1, 2048], BF16)
    pt_t = at("pt_t", [128, 2 * 512], BF16)
    recb = at("recb", [128, 512], F32)
    lrow = AT("lrow", [128, 512], F32, offset=off - 2048)
    ao_st = at("ao_st", [128, 512], BF16)
    assert off - rd0 <= 35712, off - rd0
    prow_a = AT("prow_a", [8, TH], F32, offset=rd0)
    prow_b = AT("prow_b", [8, TH], F32, offset=rd0 + 8192)
    prow_c = AT("prow_c", [8, TH], F32, offset=rd0 + 16384)
    prow_d = AT("prow_d", [8, TH], F32, offset=rb0)
    etab_i = AT("etab_i", [1, 2 * 16], I32, offset=rd0 + 28672)
    wcol_f = AT("wcol_f", [128, 16], F32, offset=rd0 + 33312)
    wcol_i = AT("wcol_i", [128, 2 * 16], I32, offset=rd0 + 33376)
    etab_f = AT("etab_f", [1, 16], F32, offset=rd0 + 28800)
    iota128 = AT("iota128", [128, 1], F32, offset=rd0 + 28864)
    srow6 = AT("srow6", [1, 512], F32, offset=rd0 + 28896)
    pq = AT("pq", [8, 16], F32, offset=rd0 + 30944)
    brow = AT("brow", [1, 32], F32, offset=rd0 + 31008)
    jrow8 = AT("jrow8", [8, 16], F32, offset=rd0 + 31136)
    val2 = AT("val2", [128, 2], F32, offset=rd0 + 31200)
    posi = AT("posi", [128, 1], I32, offset=rd0 + 31232)
    zraw = AT("zraw", [128, D], BF16, offset=rd0 + 31264)
    r1b = AT("r1b", [128, D], BF16, offset=rd0 + 33504)
    off = rd0
    fin = at("fin", [128, 2 * D], F32)
    x1in = at("x1in", [128, 2 * D], F32)
    oub = at("oub", [128, 2 * D], F32)
    sq7 = at("sq7", [128, D], F32)
    assert off - rd0 <= 35712, off - rd0

    psA = nc.alloc_psum_tensor("psA", [128, 2048], F32)
    psB = nc.alloc_psum_tensor("psB", [128, 1024], F32)
    psC = nc.alloc_psum_tensor("psC", [128, 512], F32)
    psD = nc.alloc_psum_tensor("psD", [128, 1024], BF16)

    c_ld = p.ctr("c_ld")
    c_st = p.ctr("c_st")
    c_gw = p.ctr("c_gw")
    c_ga = p.ctr("c_ga")
    c_ga1 = p.ctr("c_ga1")
    c_cv = p.ctr("c_cv")
    c_pe = p.ctr("c_pe")
    c_dv = p.ctr("c_dv")
    c_ac = p.ctr("c_ac")

    SY, GP, TE, VE, SC = "sync", "gpsimd", "tensor", "vector", "scalar"


    def ln_stats(src, n_feat, cols, waits, sq_scratch):
        sq_scratch = sq_scratch[:]
        s0, s1, mu, var, rstd, nmr = (stat[:, cols + i:cols + i + 1] for i in range(6))
        t0 = p.add(VE, lambda e, s=src, o=s0: e.reduce_sum(o, s, axis=AX.X), waits, c_dv)
        t_sq = p.add(SC, lambda e, s=src, o=sq_scratch, a=s1: e.activation(
            o, s, AF.Square, accum_out=a), waits, c_ac)
        t1 = p.add(VE, lambda e, i=s0, o=mu: e.tensor_scalar(o, i, 1.0 / n_feat, None, op0=ALU.mult),
                   [(c_ac, t_sq), (c_dv, t0)], c_dv)
        t2 = p.add(VE, lambda e, i=mu, o=nmr: e.tensor_scalar(o, i, i, None, op0=ALU.mult),
                   [(c_dv, t1)], c_dv)
        t_var = p.add(VE, lambda e, i=s1, o=var, m=nmr: e.tensor_scalar(
            o, i, 1.0 / n_feat, m, op0=ALU.mult, op1=ALU.subtract), [(c_dv, t2)], c_dv)
        t_ln = p.add(SC, lambda e, i=var, o=rstd: e.activation(o, i, AF.Ln, bias=eps_c[:, 0:1]),
                     [(c_dv, t_var)], c_ac)
        t_ex = p.add(SC, lambda e, i=rstd, o=rstd: e.activation(o, i, AF.Exp, scale=-0.5),
                     [(c_ac, t_ln)], c_ac)
        t_nmr = p.add(VE, lambda e, i=mu, r=rstd, o=nmr: e.scalar_tensor_tensor(
            o, i, -1.0, r, op0=ALU.mult, op1=ALU.mult), [(c_ac, t_ex), (c_dv, t_var)], c_dv)
        return t_nmr, rstd, nmr

    def ln_stats_gen(src, n_feat, cols, waits, sq_scratch, res):
        sq_scratch = sq_scratch[:]
        s0, s1, mu, var, rstd, nmr = (stat[:, cols + i:cols + i + 1] for i in range(6))
        t0 = p.add(VE, lambda e, s=src, o=s0: e.reduce_sum(o, s, axis=AX.X), waits, c_dv)
        yield
        t_sq = p.add(SC, lambda e, s=src, o=sq_scratch, a=s1: e.activation(
            o, s, AF.Square, accum_out=a), waits, c_ac)
        yield
        t1 = p.add(VE, lambda e, i=s0, o=mu: e.tensor_scalar(o, i, 1.0 / n_feat, None, op0=ALU.mult),
                   [(c_ac, t_sq), (c_dv, t0)], c_dv)
        yield
        t2 = p.add(VE, lambda e, i=mu, o=nmr: e.tensor_scalar(o, i, i, None, op0=ALU.mult),
                   [(c_dv, t1)], c_dv)
        yield
        t_var = p.add(VE, lambda e, i=s1, o=var, m=nmr: e.tensor_scalar(
            o, i, 1.0 / n_feat, m, op0=ALU.mult, op1=ALU.subtract), [(c_dv, t2)], c_dv)
        yield
        t_ln = p.add(SC, lambda e, i=var, o=rstd: e.activation(o, i, AF.Ln, bias=eps_c[:, 0:1]),
                     [(c_dv, t_var)], c_ac)
        yield
        t_ex = p.add(SC, lambda e, i=rstd, o=rstd: e.activation(o, i, AF.Exp, scale=-0.5),
                     [(c_ac, t_ln)], c_ac)
        yield
        t_nmr = p.add(VE, lambda e, i=mu, r=rstd, o=nmr: e.scalar_tensor_tensor(
            o, i, -1.0, r, op0=ALU.mult, op1=ALU.mult), [(c_ac, t_ex), (c_dv, t_var)], c_dv)
        res["t"], res["rstd"], res["nmr"] = t_nmr, rstd, nmr
        yield

    def rms_stats_gen(src, n_feat, cols, waits, sq_scratch, res):
        sq_scratch = sq_scratch[:]
        s1, msq, rr = (stat[:, cols + i:cols + i + 1] for i in range(3))
        t_sq = p.add(SC, lambda e, s=src, o=sq_scratch, a=s1: e.activation(
            o, s, AF.Square, accum_out=a), waits, c_ac)
        yield
        t_m = p.add(VE, lambda e, i=s1, o=msq: e.tensor_scalar(o, i, 1.0 / n_feat, None,
                                                               op0=ALU.mult),
                    [(c_ac, t_sq)], c_dv)
        yield
        t_ln = p.add(SC, lambda e, i=msq, o=rr: e.activation(o, i, AF.Ln, bias=eps_c[:, 0:1]),
                     [(c_dv, t_m)], c_ac)
        yield
        t_rr = p.add(SC, lambda e, i=rr, o=rr: e.activation(o, i, AF.Exp, scale=-0.5),
                     [(c_ac, t_ln)], c_ac)
        res["t"], res["rr"] = t_rr, rr
        yield

    def rms_stats(src, n_feat, cols, waits, sq_scratch):
        sq_scratch = sq_scratch[:]
        s1, msq, rr = (stat[:, cols + i:cols + i + 1] for i in range(3))
        t_sq = p.add(SC, lambda e, s=src, o=sq_scratch, a=s1: e.activation(
            o, s, AF.Square, accum_out=a), waits, c_ac)
        t_m = p.add(VE, lambda e, i=s1, o=msq: e.tensor_scalar(o, i, 1.0 / n_feat, None,
                                                               op0=ALU.mult),
                    [(c_ac, t_sq)], c_dv)
        t_ln = p.add(SC, lambda e, i=msq, o=rr: e.activation(o, i, AF.Ln, bias=eps_c[:, 0:1]),
                     [(c_dv, t_m)], c_ac)
        t_rr = p.add(SC, lambda e, i=rr, o=rr: e.activation(o, i, AF.Exp, scale=-0.5),
                     [(c_ac, t_ln)], c_ac)
        return t_rr, rr

    for dst, srch in ((b_b, b_h), (gam_b, gamma_h), (bet_b, beta_h),
                      (l1w_b, ln1w_h), (l2w_b, ln2w_h)):
        p.dma(SY, dst[:], bass.AP(srch.tensor, 0, [[0, 128], [1, D]]), (), c_ld)
    t_ld_a = p.dma(SY, a_sc[:], decay_h, (), c_ld)

    for dc in range(NDC):
        p.dma(GP, wt_w[:, dc * D:(dc + 1) * D], wT_h[dc * 128:(dc + 1) * 128, :], (), c_gw)
    for dc in range(NDC):
        t_gw_wt = p.dma(GP, gate_w[:, dc * NE:(dc + 1) * NE],
                        gateT_h[dc * 128:(dc + 1) * 128, :], (), c_gw)

    p.add(GP, lambda e: e.memset(ident_bf[:], 0.0))
    p.add(GP, lambda e: e.affine_select(out=ident_bf[:], in_=ident_bf[:],
                                        compare_op=ALU.not_equal, fill=1.0, base=0,
                                        pattern=[[-1, 128]], channel_multiplier=1))
    p.add(GP, lambda e: e.memset(ident_f[:], 0.0))
    p.add(GP, lambda e: e.affine_select(out=ident_f[:], in_=ident_f[:],
                                        compare_op=ALU.not_equal, fill=1.0, base=0,
                                        pattern=[[-1, 128]], channel_multiplier=1))
    p.add(GP, lambda e: e.memset(ones_bf[:], 1.0))
    for pb in range(2):
        for kc in range(NKC):
            for hoff in (64, 129):
                p.add(GP, lambda e, c=pb * (NKC * 130) + kc * 130 + hoff:
                      e.memset(vh2[:, c:c + 1], 1.0))
    t_gp_done = p.add(GP, lambda e: e.memset(ones_f[:], 1.0), (), c_gw)

    t_a = p.add(SC, lambda e: e.activation(a_sc[:], a_sc[:], AF.Sigmoid),
                [(c_ld, t_ld_a)], c_ac)
    p.add(VE, lambda e: e.memset(eps_c[:], EPS))

    p.add(VE, lambda e: e.memset(powx[:], 0.0), [(c_ac, t_a)])
    t_ms1 = p.add(VE, lambda e: e.memset(powx[0:1, 127:128], 1.0), (), c_dv)
    t_pow = t_ms1
    t_pow = None
    for k in range(1, 129):
        t_pow = p.add(VE, lambda e, k=k: e.tensor_scalar(
            powx[0:1, 127 + k:128 + k], powx[0:1, 126 + k:127 + k], a_sc[0:1, 0:1],
            None, op0=ALU.mult), [(c_dv, t_pow)], c_dv)
    t_pwst = p.dma(SY, pow_h, powx[0:1, :], [(c_dv, t_pow)], c_st)
    p.dma(SY, apow_b[:], bass.AP(pow_h.tensor, 128, [[0, 128], [1, 128]]),
          [(c_st, t_pwst)], c_ld)
    t_a128 = p.dma(SY, a128_c[:], bass.AP(pow_h.tensor, 255, [[0, 128], [1, 1]]), (), c_ld)
    t_cv_in = {}
    cvu = 0
    t_cv_st = {-1: 0, -2: 0}
    for e_ in range(NE):
        for dc in range(NDC):
            for wsrc, wdst in ((w1T_h, w1c_h), (w3T_h, w3c_h)):
                for hc in range(2):
                    pbv = cvu % 2
                    t_ci = p.dma(GP, convst[:, pbv * 1024:(pbv + 1) * 1024],
                                 wsrc[e_, dc * 128:(dc + 1) * 128,
                                      hc * 1024:(hc + 1) * 1024],
                                 [(c_gw, t_gw_wt), (c_cv, t_cv_st[cvu - 2])], c_cv)
                    t_cv_st[cvu] = p.dma(GP,
                        bass.AP(wdst.tensor,
                                e_ * 128 * NDC * I_DIM + dc * I_DIM + hc * 1024,
                                [[NDC * I_DIM, 128], [1, 1024]]),
                        convst[:, pbv * 1024:(pbv + 1) * 1024],
                        [(c_cv, t_ci)], c_cv)
                    cvu += 1
        for ic in range(NIC):
            pbv = cvu % 2
            t_ci = p.dma(GP, convst[:, pbv * 1024:(pbv + 1) * 1024],
                         w2T_h[e_, ic * 128:(ic + 1) * 128, :],
                         [(c_cv, t_cv_st[cvu - 2])], c_cv)
            t_cv_st[cvu] = p.dma(GP,
                bass.AP(w2c_h.tensor, e_ * 128 * NIC * D + ic * D,
                        [[NIC * D, 128], [1, D]]),
                convst[:, pbv * 1024:(pbv + 1) * 1024],
                [(c_cv, t_ci)], c_cv)
            cvu += 1
    t_conv_done = t_cv_st[cvu - 1]
    p.conv_done_token = t_conv_done

    t_ltri = None
    for srow in range(128):
        t_ltri = p.dma(GP, ltri[srow:srow + 1, :],
                       bass.AP(pow_h.tensor, 127 - srow, [[1, 1], [1, 128]]),
                       [(c_st, t_pwst)] if srow == 0 else (), c_gw)

    tok_x1_st = [0] * (T // 128)
    tok_z_st = [0] * (T // 128)
    tok_fh_st = [0] * (T // 128)
    tok_r2_st = {}
    tok_we = [0] * (T // 128)
    p4_last_pe = 0
    p4_last_r2st = 0

    for seq in range(nseq):
        w_wait = [(c_pe, p4_last_pe)] if seq > 0 else []
        t_gw_qkv = None
        for w_sbuf, w_ap in ((wq_w, wqT_h), (wk_w, wkT_h), (wv_w, wvT_h)):
            for dc in range(NDC):
                t_gw_qkv = p.dma(GP, w_sbuf[:, dc * D:(dc + 1) * D],
                                 w_ap[dc * 128:(dc + 1) * 128, :],
                                 w_wait if dc == 0 else (), c_gw)

        p1_w = [(c_pe, c_pe.v), (c_dv, c_dv.v), (c_ac, c_ac.v), (c_st, c_st.v)]
        t_carry0 = p.add(VE, lambda e: e.memset(carry[:, NDC:2 * NDC], 0.0),
                         [(c_ld, t_a128), (c_gw, t_ltri)], c_dv)

        dbg_tok = {}
        t_dv_evac = {-1: t_carry0}
        t_pe_scan = {-1: 0, -2: 0}
        t_dv_yb = {-1: 0}
        t_dv_x1add = {-1: 0, -2: 0}
        t_pe_r1t = {-1: 0, -2: 0}
        t_dv_r1t = {-1: 0, -2: 0}
        t_ac_r1s = {-1: 0}

        site_last = {}

        def p1_gen(tc):
            pb = tc % 2
            gtc = seq * STC + tc
            r0 = gtc * 128
            ybuf_p = ybuf if pb == 0 else ybufB
            gbuf_p = gbuf if pb == 0 else gbufB
            sqs_p = sqs if pb == 0 else sqsB
            r1tmp_p = r1tmp if pb == 0 else r1tmpB
            r1b_p = r1b if pb == 0 else r1bB
            yln_p = yln_t if pb == 0 else ylnB
            xo = x0_t[:, pb * D:(pb + 1) * D]
            t_ldx = p.dma(SY, xo, x0_h[r0:r0 + 128, :],
                          [(c_dv, t_dv_x1add[tc - 2])] + (p1_w if tc < 2 else []), c_ld)
            yield
            res = {}
            yield from ln_stats_gen(xo, D, 0 + 8 * pb,
                                    [(c_ld, t_ldx), (c_ac, site_last.get(pb, 0))] +
                                    (p1_w if tc < 2 else []), sqs_p, res)
            t_nmr, rstd, nmr = res["t"], res["rstd"], res["nmr"]
            t_yln = p.add(SC, lambda e, x=xo, r=rstd, m=nmr, o=yln_p: e.activation(
                o[:], x, AF.Identity, scale=r, bias=m),
                [(c_dv, t_nmr), (c_pe, t_pe_scan[tc - 2])], c_ac)
            site_last[pb] = t_yln
            yield
            t_scan = None
            for dc in range(NDC):
                t_scan = p.add(TE, lambda e, dc=dc, o=yln_p: e.matmul(
                    psA[:, dc * 128:(dc + 1) * 128], o[:, dc * 128:(dc + 1) * 128],
                    ltri[:], start=True, stop=True),
                    [(c_ac, t_yln), (c_dv, t_dv_evac[tc - 1]), (c_gw, t_ltri)]
                    if dc == 0 else (), c_pe if dc == NDC - 1 else None)
            t_pe_scan[tc] = t_scan
            yield
            po, pn = (1 - pb) * NDC, pb * NDC
            t_ev = None
            for dc in range(NDC):
                p.add(VE, lambda e, dc=dc, po=po, pn=pn: e.scalar_tensor_tensor(
                    carry[:, pn + dc:pn + dc + 1], a128_c[:], carry[:, po + dc:po + dc + 1],
                    psA[:, dc * 128 + 127:dc * 128 + 128], op0=ALU.mult, op1=ALU.add),
                    [(c_pe, t_scan)] if dc == 0 else ())
                t_ev = p.add(VE, lambda e, dc=dc, po=po, pb=pb: e.scalar_tensor_tensor(
                    st_t[:, pb * D + dc * 128:pb * D + (dc + 1) * 128], apow_b[:],
                    carry[:, po + dc:po + dc + 1], psA[:, dc * 128:(dc + 1) * 128],
                    op0=ALU.mult, op1=ALU.add), (), c_dv if dc == NDC - 1 else None)
                if dc == 3:
                    yield
            t_dv_evac[tc] = t_ev
            yield
            t_y = None
            for et in range(2):
                for dc in range(NDC):
                    t_y = p.add(TE, lambda e, et=et, dc=dc, pb=pb: e.matmul(
                        psA[:, 1024 + et * 512:1024 + (et + 1) * 512],
                        st_t[:, pb * D + dc * 128:pb * D + (dc + 1) * 128],
                        wt_w[:, dc * D + et * 512:dc * D + (et + 1) * 512],
                        start=(dc == 0), stop=(dc == NDC - 1)),
                        [(c_dv, t_ev), (c_dv, t_dv_yb[tc - 1]), (c_gw, t_gw_wt)]
                        if (et == 0 and dc == 0) else (),
                        c_pe if (et == 1 and dc == NDC - 1) else None)
            yield
            t_yb = p.add(VE, lambda e, o=ybuf_p: e.tensor_add(o[:], psA[:, 1024:2048], b_b[:]),
                         [(c_pe, t_y)], c_dv)
            t_dv_yb[tc] = t_yb
            yield
            res1 = {}
            yield from ln_stats_gen(ybuf_p[:], D, 16 + 8 * pb,
                                    [(c_dv, t_yb), (c_ac, site_last.get(16 + pb, 0))],
                                    sqs_p, res1)
            t_l1 = p.add(SC, lambda e, r=res1["rstd"], m=res1["nmr"], i=ybuf_p, o=sqs_p:
                         e.activation(o[:], i[:], AF.Identity, scale=r, bias=m),
                         [(c_dv, res1["t"])], c_ac)
            site_last[16 + pb] = t_l1
            yield
            t_g1 = p.add(VE, lambda e, i=sqs_p, o=gbuf_p: e.tensor_mul(o[:], i[:], gam_b[:]),
                         [(c_ac, t_l1)], c_dv)
            yield
            t_g2 = p.add(VE, lambda e, o=gbuf_p: e.tensor_add(o[:], o[:], bet_b[:]),
                         [(c_dv, t_g1)], c_dv)
            yield
            res2 = {}
            yield from ln_stats_gen(gbuf_p[:], D, 32 + 8 * pb,
                                    [(c_dv, t_g2), (c_ac, site_last.get(32 + pb, 0))],
                                    sqs_p, res2)
            t_l2 = p.add(SC, lambda e, r=res2["rstd"], m=res2["nmr"], i=gbuf_p, o=ybuf_p:
                         e.activation(o[:], i[:], AF.Identity, scale=r, bias=m),
                         [(c_dv, res2["t"])], c_ac)
            site_last[32 + pb] = t_l2
            yield
            t_x1a = p.add(VE, lambda e, x=xo, pb=pb, i=ybuf_p: e.tensor_add(
                x1b[:, pb * D:(pb + 1) * D], x, i[:]),
                [(c_ac, t_l2), (c_st, tok_x1_st[gtc - 2] if gtc >= 2 else 0)], c_dv)
            t_dv_x1add[tc] = t_x1a
            tok_x1_st[gtc] = p.dma(SY, x1_h[r0:r0 + 128, :], x1b[:, pb * D:(pb + 1) * D],
                                   [(c_dv, t_x1a)], c_st)
            yield
            resz = {}
            yield from ln_stats_gen(x1b[:, pb * D:(pb + 1) * D], D, 48 + 8 * pb,
                                    [(c_dv, t_x1a), (c_ac, site_last.get(48 + pb, 0))],
                                    sqs_p, resz)
            t_z = p.add(SC, lambda e, r=resz["rstd"], m=resz["nmr"], pb=pb: e.activation(
                zb[:, pb * D:(pb + 1) * D], x1b[:, pb * D:(pb + 1) * D], AF.Identity,
                scale=r, bias=m),
                [(c_dv, resz["t"]), (c_st, tok_z_st[gtc - 2] if gtc >= 2 else 0)], c_ac)
            site_last[48 + pb] = t_z
            tok_z_st[gtc] = p.dma(SY, z_h[r0:r0 + 128, :], zb[:, pb * D:(pb + 1) * D],
                                  [(c_ac, t_z)], c_st)
            yield
            resr = {}
            yield from rms_stats_gen(zb[:, pb * D:(pb + 1) * D], D, 64 + 4 * pb,
                                     [(c_ac, t_z), (c_ac, site_last.get(64 + pb, 0))],
                                     sqs_p, resr)
            t_r1s = p.add(SC, lambda e, rr=resr["rr"], pb=pb, o=r1tmp_p: e.activation(
                o[:], zb[:, pb * D:(pb + 1) * D], AF.Identity, scale=rr),
                [(c_ac, resr["t"]), (c_dv, t_dv_r1t[tc - 2])], c_ac)
            site_last[64 + pb] = t_r1s
            yield
            t_r1 = p.add(VE, lambda e, i=r1tmp_p, o=r1b_p: e.tensor_mul(o[:], i[:], l1w_b[:]),
                         [(c_ac, t_r1s), (c_pe, t_pe_r1t[tc - 2])], c_dv)
            yield
            t_tr = None
            for dc in range(NDC):
                t_tr = p.add(TE, lambda e, dc=dc, i=r1b_p: e.transpose(
                    psD[:, dc * 128:(dc + 1) * 128], i[:, dc * 128:(dc + 1) * 128],
                    ident_bf[:]),
                    [(c_dv, t_r1), (c_dv, t_dv_r1t[tc - 1]), (c_gw, t_gp_done)]
                    if dc == 0 else (), c_pe if dc == NDC - 1 else None)
            t_pe_r1t[tc] = t_tr
            yield
            t_r1t = None
            for dc in range(NDC):
                t_r1t = p.add(VE, lambda e, dc=dc, tc=tc: e.tensor_copy(
                    r1t_s[:, dc * S + tc * 128:dc * S + (tc + 1) * 128],
                    psD[:, dc * 128:(dc + 1) * 128]),
                    [(c_pe, t_tr)] if dc == 0 else (), c_dv if dc == NDC - 1 else None)
            t_dv_r1t[tc] = t_r1t
            yield

        LAG = 8
        act_g = []
        next_tc = 0
        steps_newest = 0
        while act_g or next_tc < STC:
            if next_tc < STC and (not act_g or (len(act_g) < 2 and steps_newest >= LAG)):
                act_g.append(p1_gen(next_tc))
                next_tc += 1
                steps_newest = 0
            for g in list(act_g):
                try:
                    next(g)
                    if g is act_g[-1]:
                        steps_newest += 1
                except StopIteration:
                    was_newest = g is act_g[-1]
                    act_g.remove(g)
                    if not was_newest:
                        pass

        p1_last_dv = t_dv_r1t[STC - 1]

        t_p2_dv = {-1: 0, -2: 0}
        t_p2_st = {-1: 0, -2: 0}
        unit = 0
        t_mm = None
        for w_sbuf, dst_h in ((wq_w, qt_h), (wk_w, kt_h)):
            for ec in range(NDC):
                for tt4 in range(NQT):
                    pp = unit % 2
                    for dc in range(NDC):
                        t_mm = p.add(TE, lambda e, dc=dc, ec=ec, tt4=tt4, pp=pp, w=w_sbuf: e.matmul(
                            psB[:, pp * 512:(pp + 1) * 512],
                            w[:, dc * D + ec * 128:dc * D + ec * 128 + 128],
                            r1t_s[:, dc * S + tt4 * 512:dc * S + (tt4 + 1) * 512],
                            start=(dc == 0), stop=(dc == NDC - 1)),
                            [(c_dv, p1_last_dv), (c_gw, t_gw_qkv),
                             (c_dv, t_p2_dv[unit - 2])] if dc == 0 else (),
                            c_pe if dc == NDC - 1 else None)
                    t_ev2 = p.add(VE, lambda e, pp=pp: e.tensor_copy(
                        pstage[:, pp * 512:(pp + 1) * 512], psB[:, pp * 512:(pp + 1) * 512]),
                        [(c_pe, t_mm), (c_st, t_p2_st[unit - 2])], c_dv)
                    t_p2_dv[unit] = t_ev2
                    t_p2_st[unit] = p.dma(
                        SY, dst_h[seq, ec * 128:(ec + 1) * 128, tt4 * 512:(tt4 + 1) * 512],
                        pstage[:, pp * 512:(pp + 1) * 512], [(c_dv, t_ev2)], c_st)
                    unit += 1
        last_qk_st = t_p2_st[unit - 1]
        vunit = 0
        t_v_dv = {-1: 0, -2: 0}
        t_v_st = {-1: 0, -2: 0}
        for tcq in range(STC):
            for et in range(2):
                pp = vunit % 2
                for dc in range(NDC):
                    t_mm = p.add(TE, lambda e, dc=dc, tcq=tcq, et=et, pp=pp: e.matmul(
                        psA[:, pp * 512:(pp + 1) * 512],
                        r1t_s[:, dc * S + tcq * 128:dc * S + (tcq + 1) * 128],
                        wv_w[:, dc * D + et * 512:dc * D + (et + 1) * 512],
                        start=(dc == 0), stop=(dc == NDC - 1)),
                        [(c_dv, t_v_dv[vunit - 2]), (c_dv, p1_last_dv)] if dc == 0 else (),
                        c_pe if dc == NDC - 1 else None)
                t_ev3 = p.add(VE, lambda e, pp=pp: e.tensor_copy(
                    vstage[:, pp * 512:(pp + 1) * 512], psA[:, pp * 512:(pp + 1) * 512]),
                    [(c_pe, t_mm), (c_st, t_v_st[vunit - 2])], c_dv)
                t_v_dv[vunit] = t_ev3
                t_v_st[vunit] = p.dma(
                    SY, v_h[seq * S + tcq * 128:seq * S + (tcq + 1) * 128,
                            et * 512:(et + 1) * 512],
                    vstage[:, pp * 512:(pp + 1) * 512], [(c_dv, t_ev3)], c_st)
                vunit += 1
        p2_last_pe = t_mm
        last_v_st = t_v_st[vunit - 1]
        t_gw_wo = None
        for dc in range(NDC):
            t_gw_wo = p.dma(GP, wo_w[:, dc * D:(dc + 1) * D],
                            woT_h[dc * 128:(dc + 1) * 128, :],
                            [(c_pe, p2_last_pe)] if dc == 0 else (), c_gw)

        p3_dv0 = c_dv.v
        qc_glob = [0]
        t_p1max = {-1: p3_dv0, -2: p3_dv0}
        t_pair_pe_done = {-1: 0, -2: 0}
        t_ao_dv = {}
        t_recb_dv = {}
        t_st_exp = {-1: 0, -2: 0}
        t_trm_prev_dv = p3_dv0
        t_trm_tok = {-1: 0, -2: 0}
        t_aost = {-1: 0}
        pair_tok = {}
        p1_last_tok = {}
        aou = 0
        t_pv = None

        def emit_pair_loads(hp):
            pqb = hp % 2
            wv_ = [(c_pe, t_pair_pe_done[hp - 2]), (c_st, last_qk_st), (c_st, last_v_st)]
            t_lq = p.dma(SY, qt2[:, pqb * S:(pqb + 1) * S],
                         qt_h[seq, hp * 128:(hp + 1) * 128, :], wv_, c_ld)
            t_lk = p.dma(SY, kt2[:, pqb * S:(pqb + 1) * S],
                         kt_h[seq, hp * 128:(hp + 1) * 128, :], (), c_ld)
            t_lv = None
            for sub in range(2):
                h2 = 2 * hp + sub
                hoff = 0 if sub == 0 else 65
                srcv = bass.AP(v_h.tensor, (seq * S) * D + h2 * HD,
                               [[D, 128], [128 * D, NKC], [1, HD]])
                dstv = vh2[:, pqb * NKC * 130:(pqb + 1) * NKC * 130].rearrange(
                    "q (kc w) -> q kc w", kc=NKC)[:, :, hoff:hoff + HD]
                t_lv = p.dma(SY, dstv, srcv, (), c_ld)
            pair_tok[hp] = (t_lq, t_lk, t_lv)

        def make_p1(h2):
            hp = h2 // 2
            pqb = hp % 2
            hsl = slice(0, 64) if h2 % 2 == 0 else slice(64, 128)
            mbase = (h2 % 2) * 16
            state = {"i": 0}

            def emit_unit():
                i = state["i"]; state["i"] += 1
                qc, kh = i // 2, i % 2
                qg = qc_glob[0]; qc_glob[0] += 1
                pp = qg % 2
                t_s1 = None
                for kn in range(2):
                    xw = ()
                    if kn == 0:
                        xw = [(c_ld, pair_tok[hp][1]), (c_dv, t_p1max[qg - 2])]
                        if i == 0:
                            xw = xw + [(c_pe, t_trm_tok[h2 - 2])]
                    t_s1 = p.add(TE, lambda e, qc=qc, kh=kh, kn=kn, pp=pp,
                                 pqb=pqb, hsl=hsl: e.matmul(
                        psA[:, pp * 1024 + kn * 512:pp * 1024 + (kn + 1) * 512],
                        qt2[hsl, pqb * S + qc * 128:pqb * S + (qc + 1) * 128],
                        kt2[hsl, pqb * S + kh * 1024 + kn * 512:
                            pqb * S + kh * 1024 + (kn + 1) * 512],
                        start=True, stop=True), xw, c_pe if kn == 1 else None)
                if kh == 0:
                    t_mx = p.add(VE, lambda e, qc=qc, pp=pp, mbase=mbase: e.reduce_max(
                        mcols[:, mbase + qc:mbase + qc + 1],
                        psA[:, pp * 1024:(pp + 1) * 1024],
                        axis=AX.X, negate=True), [(c_pe, t_s1)], c_dv)
                else:
                    t_tmp = p.add(VE, lambda e, pp=pp: e.reduce_max(
                        stat[:, 90:91], psA[:, pp * 1024:(pp + 1) * 1024],
                        axis=AX.X, negate=True), [(c_pe, t_s1)], c_dv)
                    t_mx = p.add(VE, lambda e, qc=qc, mbase=mbase: e.tensor_tensor(
                        mcols[:, mbase + qc:mbase + qc + 1],
                        mcols[:, mbase + qc:mbase + qc + 1], stat[:, 90:91],
                        op=ALU.min), [(c_dv, t_tmp)], c_dv)
                t_p1max[qg] = t_mx
                p1_last_tok[h2] = t_mx
            return emit_unit, state

        emit_pair_loads(0)
        p1_em = {0: make_p1(0)}
        for _ in range(2 * STC):
            p1_em[0][0]()

        for h_ in range(NH):
            hp = h_ // 2
            pqb = hp % 2
            hsl = slice(0, 64) if h_ % 2 == 0 else slice(64, 128)
            hoff = 0 if h_ % 2 == 0 else 65
            if h_ % 2 == 0 and hp + 1 < NH // 2:
                emit_pair_loads(hp + 1)
            if h_ + 1 < NH:
                p1_em[h_ + 1] = make_p1(h_ + 1)
            mbase = (h_ % 2) * 16
            t_trm = p.add(TE, lambda e, mbase=mbase: e.transpose(
                psC[0:16, 0:128], mcols[:, mbase:mbase + 16], ident_f[:]),
                [(c_dv, p1_last_tok[h_]), (c_dv, t_trm_prev_dv),
                 (c_dv, t_ao_dv.get((h_ - 1, NQT - 1), p3_dv0))], c_pe)
            t_trm_tok[h_] = t_trm
            t_mr = p.add(VE, lambda e: e.tensor_copy(mrow_t[:], psC[0:16, 0:128]),
                         [(c_pe, t_trm)], c_dv)
            t_trm_prev_dv = t_mr
            t_nm = p.dma(SY, negm[0:1, :], mrow_t[:, :], [(c_dv, t_mr)], c_ld)
            for qt in range(NQT):
                pend_pv = []

                def emit_pv(kc, pp2, first_w=()):
                    nonlocal t_pv
                    pvw = [(c_ac, t_st_exp[(h_ * NQT + qt) * NKC + kc])] + list(first_w)
                    t_pv = p.add(TE, lambda e, kc=kc, pqb=pqb, hoff=hoff, pp2=pp2: e.matmul(
                        psC[0:65, 0:512],
                        vh2[:, pqb * NKC * 130 + kc * 130 + hoff:
                            pqb * NKC * 130 + kc * 130 + hoff + 65],
                        pt_t[:, pp2 * 512:(pp2 + 1) * 512],
                        start=(kc == 0), stop=(kc == NKC - 1)),
                        pvw, c_pe if kc == NKC - 1 else None)

                for kc in range(NKC):
                    stu = (h_ * NQT + qt) * NKC + kc
                    pp2 = stu % 2
                    p.add(TE, lambda e, kc=kc, qt=qt, pp2=pp2, pqb=pqb, hsl=hsl: e.matmul(
                        psB[:, pp2 * 512:(pp2 + 1) * 512],
                        kt2[hsl, pqb * S + kc * 128:pqb * S + (kc + 1) * 128],
                        qt2[hsl, pqb * S + qt * 512:pqb * S + (qt + 1) * 512],
                        start=True, stop=False),
                        [(c_ld, t_nm), (c_ac, t_st_exp[stu - 2]),
                         (c_dv, t_recb_dv.get((h_, qt - 1),
                                              t_recb_dv.get((h_ - 1, NQT - 1), p3_dv0)))])
                    t_s2 = p.add(TE, lambda e, qt=qt, pp2=pp2: e.matmul(
                        psB[:, pp2 * 512:(pp2 + 1) * 512], ones_bf[0:1, :],
                        negm[0:1, qt * 512:(qt + 1) * 512], start=False, stop=True),
                        (), c_pe)
                    t_ex = p.add(SC, lambda e, pp2=pp2: e.activation(
                        pt_t[:, pp2 * 512:(pp2 + 1) * 512],
                        psB[:, pp2 * 512:(pp2 + 1) * 512], AF.Exp, scale=0.125),
                        [(c_pe, t_s2)], c_ac)
                    t_st_exp[stu] = t_ex
                    if kc % 2 == 1 and h_ + 1 < NH and p1_em[h_ + 1][1]["i"] < 2 * STC:
                        p1_em[h_ + 1][0]()
                    if kc > 0:
                        fw = ()
                        if kc == 1:
                            fw = [(c_ld, pair_tok[hp][2]), (c_dv, t_mr),
                                  (c_dv, t_ao_dv.get((h_, qt - 1),
                                                     t_ao_dv.get((h_ - 1, NQT - 1), p3_dv0)))]
                            emit_pv(0, (h_ * NQT + qt) * NKC % 2, fw)
                        else:
                            emit_pv(kc - 1, ((h_ * NQT + qt) * NKC + kc - 1) % 2)
                emit_pv(NKC - 1, ((h_ * NQT + qt) * NKC + NKC - 1) % 2)
                t_lc = p.add(VE, lambda e: e.tensor_copy(lrow[64:65, :], psC[64:65, 0:512]),
                             [(c_pe, t_pv)], c_dv)
                t_ln2 = p.add(SC, lambda e: e.activation(lrow[64:65, :], lrow[64:65, :],
                                                         AF.Ln), [(c_dv, t_lc)], c_ac)
                t_rc = p.add(SC, lambda e: e.activation(lrow[64:65, :], lrow[64:65, :],
                                                        AF.Exp, scale=-1.0),
                             [(c_ac, t_ln2)], c_ac)
                t_bc = p.add(TE, lambda e: e.matmul(
                    psB[0:64, 0:512], ones_f[64:65, 0:64], lrow[64:65, :],
                    start=True, stop=True),
                    [(c_ac, t_rc),
                     (c_ac, t_st_exp[(h_ * NQT + qt) * NKC + NKC - 1])], c_pe)
                t_rb = p.add(VE, lambda e: e.tensor_copy(recb[0:64, :], psB[0:64, 0:512]),
                             [(c_pe, t_bc)], c_dv)
                t_recb_dv[(h_, qt)] = t_rb
                t_ao = p.add(VE, lambda e: e.tensor_mul(
                    ao_st[0:64, :], psC[0:64, 0:512], recb[0:64, :]),
                    [(c_dv, t_rb), (c_st, t_aost[aou - 1])], c_dv)
                t_ao_dv[(h_, qt)] = t_ao
                t_aost[aou] = p.dma(SY, aot_h[seq, h_ * 64:(h_ + 1) * 64,
                                              qt * 512:(qt + 1) * 512],
                                    ao_st[0:64, :], [(c_dv, t_ao)], c_st)
                aou += 1
            t_pair_pe_done[hp] = t_pv
        p3_last_st = t_aost[aou - 1]

        p4_dv0 = c_dv.v
        t_p4_dv = {-1: 0, -2: 0}
        t_r2tb_dv = {-1: 0}
        tok_wq3 = {}
        t_wq3_prev = 0
        t_logit_dv = {-1: p4_dv0}
        t_hb_st = {-1: 0, -2: 0}
        t_r2_prev = 0
        t_r2s_prev = 0
        t_tr4_prev = 0
        t_gmm = None
        for tc in range(STC):
            pb = tc % 2
            gtc = seq * STC + tc
            r0 = gtc * 128
            src = bass.AP(aot_h.tensor, seq * D * S + tc * 128,
                          [[S, 128], [128 * S, NDC], [1, 128]])
            dsta = aoin[:, pb * NDC * 128:(pb + 1) * NDC * 128]
            t_lao = p.dma(SY, dsta.rearrange("q (dc w) -> q dc w", dc=NDC), src,
                          [(c_st, p3_last_st), (c_dv, t_p4_dv[tc - 2])], c_ld)
            t_lz = p.dma(SY, zin[:, pb * D:(pb + 1) * D], z_h[r0:r0 + 128, :],
                         [(c_st, tok_z_st[gtc])], c_ld)
            t_wo = None
            for et in range(2):
                for dc in range(NDC):
                    t_wo = p.add(TE, lambda e, et=et, dc=dc, pb=pb: e.matmul(
                        psA[:, pb * 1024 + et * 512:pb * 1024 + (et + 1) * 512],
                        aoin[:, pb * NDC * 128 + dc * 128:pb * NDC * 128 + (dc + 1) * 128],
                        wo_w[:, dc * D + et * 512:dc * D + (et + 1) * 512],
                        start=(dc == 0), stop=(dc == NDC - 1)),
                        [(c_ld, t_lao), (c_gw, t_gw_wo), (c_dv, t_p4_dv[tc - 2]),
                         (c_dv, p4_dv0)] if (et == 0 and dc == 0) else (),
                        c_pe if (et == 1 and dc == NDC - 1) else None)
            t_h = p.add(VE, lambda e, pb=pb: e.tensor_add(
                hb[:], psA[:, pb * 1024:(pb + 1) * 1024], zin[:, pb * D:(pb + 1) * D]),
                [(c_pe, t_wo), (c_ld, t_lz), (c_st, t_hb_st[tc - 1]),
                 (c_ac, t_r2s_prev)], c_dv)
            t_p4_dv[tc] = t_h
            t_hb_st[tc] = tok_fh_st[gtc] = p.dma(SY, f_h[r0:r0 + 128, :], hb[:],
                                                 [(c_dv, t_h)], c_st)
            t_rr2, rr2 = rms_stats(hb[:], D, 72 + 4 * pb,
                                   [(c_dv, t_h), (c_ac, t_r2s_prev)], sq4)
            t_r2s = p.add(SC, lambda e, rr=rr2: e.activation(
                r2tmp[:], hb[:], AF.Identity, scale=rr),
                [(c_ac, t_rr2), (c_dv, t_r2_prev)], c_ac)
            t_r2s_prev = t_r2s
            t_r2 = p.add(VE, lambda e: e.tensor_mul(r2f[:], r2tmp[:], l2w_b[:]),
                         [(c_ac, t_r2s), (c_pe, t_tr4_prev)], c_dv)
            t_r2_prev = t_r2
            t_tr4 = None
            for dc in range(NDC):
                t_tr4 = p.add(TE, lambda e, dc=dc: e.transpose(
                    psB[:, dc * 128:(dc + 1) * 128], r2f[:, dc * 128:(dc + 1) * 128],
                    ident_f[:]),
                    [(c_dv, t_r2), (c_dv, t_recb_dv.get((NH - 1, NQT - 1), p4_dv0))]
                    if dc == 0 else (), c_pe if dc == NDC - 1 else None)
            t_tr4_prev = t_tr4
            for dc in range(NDC):
                p.add(VE, lambda e, dc=dc: e.tensor_copy(
                    r2tf[:, dc * 128:(dc + 1) * 128], psB[:, dc * 128:(dc + 1) * 128]),
                    [(c_pe, t_tr4)] if dc == 0 else ())
            t_r2b = p.add(VE, lambda e, pb=pb: e.tensor_copy(
                r2b_st[:, pb * D:(pb + 1) * D], r2f[:]),
                [(c_dv, t_r2), (c_st, tok_r2_st.get(gtc - 2, 0))], c_dv)
            t_r2tb_dv[tc] = t_r2b
            tok_r2_st[gtc] = p.dma(SY, r2tm_h[r0:r0 + 128, :],
                                   r2b_st[:, pb * D:(pb + 1) * D],
                                   [(c_dv, t_r2b)], c_st)
            for dc in range(NDC):
                t_gmm = p.add(TE, lambda e, dc=dc: e.matmul(
                    psC[:, 0:NE], r2tf[:, dc * 128:(dc + 1) * 128],
                    gate_w[:, dc * NE:(dc + 1) * NE],
                    start=(dc == 0), stop=(dc == NDC - 1)),
                    [(c_dv, t_r2b), (c_dv, t_logit_dv[tc - 1]), (c_gw, t_gw_wt)]
                    if dc == 0 else (), c_pe if dc == NDC - 1 else None)
            g0, g1, g2, g3, g4 = (stat[:, 83 + i:84 + i] for i in range(5))
            t_lg = p.add(VE, lambda e: e.tensor_copy(logit[:], psC[:, 0:NE]),
                         [(c_pe, t_gmm)], c_dv)
            t_logit_dv[tc] = t_lg
            tg = p.add(VE, lambda e: e.reduce_max(g0, logit[:], axis=AX.X, negate=True),
                       [(c_dv, t_lg)], c_dv)
            tg = p.add(VE, lambda e: e.tensor_scalar(gs1[:], logit[:], g0, None, op0=ALU.add),
                       [(c_dv, tg)], c_dv)
            tg = p.add(VE, lambda e: e.tensor_scalar(gs2[:], gs1[:], 0.0, None, op0=ALU.is_ge),
                       [(c_dv, tg)], c_dv)
            tg = p.add(VE, lambda e: e.tensor_copy(wq24[:, 8:16], gs2[:]),
                       [(c_dv, tg), (c_st, tok_wq3.get(tc - 1, 0))], c_dv)
            tg = p.add(VE, lambda e: e.scalar_tensor_tensor(gs1[:], gs2[:], -1e30, logit[:],
                                                            op0=ALU.mult, op1=ALU.add),
                       [(c_dv, tg)], c_dv)
            tg = p.add(VE, lambda e: e.reduce_max(g1, gs1[:], axis=AX.X, negate=True),
                       [(c_dv, tg)], c_dv)
            t_d12 = p.add(VE, lambda e: e.tensor_sub(g2, g1, g0), [(c_dv, tg)], c_dv)
            t_sg = p.add(SC, lambda e: e.activation(g3, g2, AF.Sigmoid),
                         [(c_dv, t_d12)], c_ac)
            tg = p.add(VE, lambda e: e.tensor_scalar(g4, g3, -1.0, 1.0, op0=ALU.mult, op1=ALU.add),
                       [(c_ac, t_sg)], c_dv)
            tg = p.add(VE, lambda e: e.tensor_scalar(gs1[:], gs1[:], g1, None, op0=ALU.add),
                       [(c_dv, tg)], c_dv)
            tg = p.add(VE, lambda e: e.tensor_scalar(gs1[:], gs1[:], 0.0, None, op0=ALU.is_ge),
                       [(c_dv, tg)], c_dv)
            tg = p.add(VE, lambda e: e.tensor_copy(wq24[:, 16:24], gs1[:]),
                       [(c_dv, tg)], c_dv)
            tg = p.add(VE, lambda e: e.tensor_scalar(gs2[:], gs2[:], g3, None, op0=ALU.mult),
                       [(c_dv, tg)], c_dv)
            t_we = p.add(VE, lambda e, gtc=gtc: e.scalar_tensor_tensor(
                we_all[:, gtc * NE:(gtc + 1) * NE], gs1[:], g4, gs2[:],
                op0=ALU.mult, op1=ALU.add), [(c_dv, tg)], c_dv)
            tok_we[gtc] = t_we
            if "we_s" in dbg:
                p.dma(SY, we_h[gtc, :, :], we_all[:, gtc * NE:(gtc + 1) * NE],
                      [(c_dv, t_we)], c_st)
            t_wc = p.add(VE, lambda e, gtc=gtc: e.tensor_copy(
                wq24[:, 0:8], we_all[:, gtc * NE:(gtc + 1) * NE]), [(c_dv, t_we)], c_dv)
            t_trq = p.add(TE, lambda e: e.transpose(psC[0:24, 128:256], wq24[:], ident_f[:]),
                          [(c_dv, t_wc), (c_dv, t_wq3_prev)], c_pe)
            t_wq3_prev = p.add(VE, lambda e: e.tensor_copy(wq24t[:], psC[0:24, 128:256]),
                               [(c_pe, t_trq)], c_dv)
            t_logit_dv[tc] = t_wq3_prev
            tok_wq3[tc] = p.dma(SY, wq3_h[:, gtc * 128:(gtc + 1) * 128], wq24t[:],
                                [(c_dv, t_wq3_prev)], c_st)
        p4_last_pe = t_gmm
        p4_last_r2st = tok_r2_st[seq * STC + STC - 1]
        p4_last_wq3 = tok_wq3[STC - 1]

    TS = 512
    NT_SP = (2 * TH) // TS + NE
    STRIDE_E = D * I_DIM



    p6_w = [(c_pe, c_pe.v), (c_dv, c_dv.v), (c_ac, c_ac.v), (c_st, c_st.v),
            (c_gw, c_gw.v)]

    t_zr = p.add(VE, lambda e: e.memset(zraw[:], 0.0), p6_w, c_dv)
    t_zr_st = p.dma(SY, r2tm_h[T:T + 128, :], zraw[:], [(c_dv, t_zr)], c_st)
    t_v2a = p.add(VE, lambda e: e.memset(val2[:, 0:1], float(T)), [(c_dv, t_zr)], c_dv)
    t_v2b = p.add(VE, lambda e: e.memset(val2[:, 1:2], 0.0), [(c_dv, t_v2a)], c_dv)
    t_fil = {}
    for half in range(2):
        tk = p.dma(SY, flat_hs[half][0:128, :], val2[:, 0:2], [(c_dv, t_v2b)], c_st)
        n = 128
        while n < NT_SP * TS:
            m = min(n, NT_SP * TS - n)
            tk = p.dma(SY, flat_hs[half][n:n + m, :], flat_hs[half][0:m, :],
                       [(c_st, tk)], c_st)
            n += m
        t_fil[half] = tk
    t_io = p.add(GP, lambda e: e.iota(iota128[:, 0:1], [[1, 1]], channel_multiplier=1,
                                      allow_small_or_imprecise_dtypes=True), p6_w, c_gw)
    t_jr0 = p.add(GP, lambda e: e.iota(jrow8[:, :], [[1, 16]], channel_multiplier=0,
                                       allow_small_or_imprecise_dtypes=True), (), c_gw)

    if stop_after_postproc == 1:
        p.add(SY, lambda e: e.nop(), [(c_st, c_st.v), (c_ga, c_ga.v), (c_ga1, c_ga1.v), (c_cv, c_cv.v), (c_ld, c_ld.v),
                                      (c_gw, c_gw.v), (c_dv, c_dv.v), (c_pe, c_pe.v),
                                      (c_ac, c_ac.v)])
        p.emit_all()
        return nc
    t_scat = {}
    t_etab = {}
    t_jr = t_jr0
    for half in range(2):
        hw0 = [(c_st, p4_last_wq3), (c_st, t_zr_st)] + p6_w
        t_la = p.dma(SY, prow_a[:, :], wq3_h[0:8, half * TH:(half + 1) * TH], hw0, c_ld)
        t_le1 = p.dma(SY, prow_b[:, :], wq3_h[8:16, half * TH:(half + 1) * TH], (), c_ld)
        t_le2 = p.dma(SY, prow_c[:, :], wq3_h[16:24, half * TH:(half + 1) * TH], (), c_ld)
        t_m = p.add(VE, lambda e: e.tensor_add(prow_c[:, :], prow_b[:, :], prow_c[:, :]),
                    [(c_ld, t_le2)] + p6_w, c_dv)
        t_s = p.add(VE, lambda e: e.tensor_tensor_scan(prow_d[:, :], prow_c[:, :],
                                                       prow_c[:, :], 0.0,
                                                       op0=ALU.add, op1=ALU.bypass),
                    [(c_dv, t_m)], c_dv)
        t_n = p.add(VE, lambda e: e.tensor_copy(pq[:, 0:1], prow_d[:, TH - 1:TH]),
                    [(c_dv, t_s)], c_dv)
        t_tc = p.add(VE, lambda e: e.tensor_scalar(pq[:, 1:2], pq[:, 0:1], 0.0, None,
                                                   op0=ALU.is_gt), [(c_dv, t_n)], c_dv)
        for m in range(1, 4):
            t_g5 = p.add(VE, lambda e, m=m: e.tensor_scalar(
                pq[:, 2:3], pq[:, 0:1], float(TS * m), None, op0=ALU.is_gt),
                [(c_dv, t_tc)], c_dv)
            t_tc = p.add(VE, lambda e: e.tensor_add(pq[:, 1:2], pq[:, 1:2], pq[:, 2:3]),
                         [(c_dv, t_g5)], c_dv)
        t_pad = p.add(VE, lambda e: e.tensor_scalar(pq[:, 4:5], pq[:, 1:2], float(TS),
                                                    None, op0=ALU.mult),
                      [(c_dv, t_tc)], c_dv)
        t_fl = None
        for ee in range(8):
            t_fl = p.dma(SY, brow[0:1, ee:ee + 1], pq[ee:ee + 1, 4:5],
                         [(c_dv, t_pad)] if ee == 0 else (), c_ld)
        t_inc = p.add(VE, lambda e: e.tensor_tensor_scan(
            brow[0:1, 8:16], brow[0:1, 0:8], brow[0:1, 0:8], 0.0,
            op0=ALU.add, op1=ALU.bypass), [(c_ld, t_fl)], c_dv)
        t_bas = p.add(VE, lambda e: e.tensor_sub(brow[0:1, 16:24], brow[0:1, 8:16],
                                                 brow[0:1, 0:8]), [(c_dv, t_inc)], c_dv)
        t_ct = p.add(VE, lambda e: e.tensor_scalar(brow[0:1, 24:32], brow[0:1, 8:16],
                                                   1.0 / TS, None, op0=ALU.mult),
                     [(c_dv, t_bas)], c_dv)
        t_bc2 = None
        for ee in range(8):
            p.dma(SY, pq[ee:ee + 1, 5:6], brow[0:1, 16 + ee:17 + ee],
                  [(c_dv, t_ct)] if ee == 0 else (), c_ld)
            t_bc2 = p.dma(SY, pq[ee:ee + 1, 6:7], brow[0:1, 24 + ee:25 + ee], (), c_ld)
        t_ge = p.add(VE, lambda e: e.tensor_scalar(jrow8[:, :], jrow8[:, :], pq[:, 6:7],
                                                   None, op0=ALU.is_ge),
                     [(c_ld, t_bc2), (c_gw, t_jr)], c_dv)
        t_emm = p.add(TE, lambda e: e.matmul(psC[0:1, 256:272], ones_f[0:8, 0:1],
                                             jrow8[:, :], start=True, stop=True),
                      [(c_dv, t_ge)] + p6_w, c_pe)
        t_ecp = p.add(VE, lambda e: e.tensor_copy(etab_f[:, :], psC[0:1, 256:272]),
                      [(c_pe, t_emm)], c_dv)
        t_ecl = p.add(VE, lambda e: e.tensor_scalar(etab_f[:, :], etab_f[:, :],
                                                    7.0, None, op0=ALU.min),
                      [(c_dv, t_ecp)], c_dv)
        t_esc = p.add(VE, lambda e: e.tensor_scalar(etab_f[:, :], etab_f[:, :],
                                                    128.0, None, op0=ALU.mult),
                      [(c_dv, t_ecl)], c_dv)
        t_eb = p.add(TE, lambda e: e.matmul(psC[:, 272:288], ones_f[0:1, 0:128],
                                            etab_f[0:1, :], start=True, stop=True),
                     [(c_dv, t_esc)], c_pe)
        t_ei = p.add(VE, lambda e, half=half: e.tensor_scalar(
            wcol_f[:, :], psC[:, 272:288], iota128[:, 0:1], None,
            op0=ALU.add), [(c_pe, t_eb), (c_gw, t_io)], c_dv)
        t_ei = p.add(VE, lambda e, half=half: e.tensor_copy(
            wcol_i[:, half * 16:(half + 1) * 16], wcol_f[:, :]), [(c_dv, t_ei)], c_dv)
        t_etab[half] = t_ei
        if "wcol_s" in dbg:
            p.dma(SY, wcol_dbg_h[:, half * 16:(half + 1) * 16],
                  wcol_i[:, half * 16:(half + 1) * 16], [(c_dv, t_ei)], c_st)
            p.dma(SY, pq_dbg_h[half], pq[:, :], [(c_dv, t_ei)], c_st)
            p.dma(SY, brow_dbg_h[half], brow[:, :], [(c_dv, t_ei)], c_st)
        t_jr = p.add(GP, lambda e: e.iota(jrow8[:, :], [[1, 16]], channel_multiplier=0,
                                          allow_small_or_imprecise_dtypes=True),
                     [(c_pe, t_emm)], c_gw)
        t_dp = p.add(VE, lambda e: e.tensor_scalar(prow_d[:, :], prow_d[:, :],
                                                   pq[:, 5:6], None, op0=ALU.add),
                     [(c_dv, t_ei)], c_dv)
        tok_pos = {}
        tok_wee = {}
        t_mmP = 0
        for lst in range(2):
            rows = slice(8, 16) if lst == 0 else slice(16, 24)
            t_lb = p.dma(SY, prow_b[:, :], wq3_h[rows, half * TH:(half + 1) * TH],
                         [(c_dv, t_dp)], c_ld)
            t_c1 = p.add(VE, lambda e: e.tensor_mul(prow_c[:, :], prow_a[:, :],
                                                    prow_b[:, :]),
                         [(c_ld, t_lb), (c_pe, t_mmP)], c_dv)
            for seg in range(TH // 512):
                t_mmW = p.add(TE, lambda e, seg=seg: e.matmul(
                    psC[0:1, 0:512], ones_f[0:8, 0:1],
                    prow_c[:, seg * 512:(seg + 1) * 512], start=True, stop=True),
                    [(c_dv, t_c1)] if seg == 0 else [(c_dv, t_evW)], c_pe)
                t_evW = p.add(VE, lambda e: e.tensor_copy(srow6[:, :], psC[0:1, 0:512]),
                              [(c_pe, t_mmW)], c_dv)
                tok_wee[(lst, seg)] = p.dma(
                    SY, stage_h[half, lst * 2 + 1, seg * 512:(seg + 1) * 512],
                    srow6[:, :], [(c_dv, t_evW)], c_st)
            t_c2 = p.add(VE, lambda e: e.tensor_mul(prow_c[:, :], prow_d[:, :],
                                                    prow_b[:, :]),
                         [(c_pe, t_mmW)], c_dv)
            for seg in range(TH // 512):
                t_mmP = p.add(TE, lambda e, seg=seg: e.matmul(
                    psC[0:1, 0:512], ones_f[0:8, 0:1],
                    prow_c[:, seg * 512:(seg + 1) * 512], start=True, stop=True),
                    [(c_dv, t_c2)] if seg == 0 else [(c_dv, t_evP)], c_pe)
                t_evP = p.add(VE, lambda e: e.tensor_scalar(
                    srow6[:, :], psC[0:1, 0:512], -1.0, None, op0=ALU.add),
                    [(c_pe, t_mmP)], c_dv)
                tok_pos[(lst, seg)] = p.dma(
                    SY, stage_h[half, lst * 2, seg * 512:(seg + 1) * 512],
                    srow6[:, :], [(c_dv, t_evP)], c_st)
        t_sc = 0
        t_vprev = t_v2b
        for lst in range(2):
            for jj in range(TH // 128):
                seg = jj // 4
                t_lp = p.dma(SY, lrow[0:128, 0:1].rearrange("q w -> q w"),
                             bass.AP(stage_h.tensor,
                                     (half * 6 + lst * 2) * TH + jj * 128,
                                     [[1, 128], [1, 1]]),
                             [(c_st, tok_pos[(lst, seg)]), (c_gw, t_sc)], c_ld)
                t_lw = p.dma(SY, lrow[0:128, 1:2],
                             bass.AP(stage_h.tensor,
                                     (half * 6 + lst * 2 + 1) * TH + jj * 128,
                                     [[1, 128], [1, 1]]),
                             [(c_st, tok_wee[(lst, seg)])], c_ld)
                t_pi = p.add(VE, lambda e: e.tensor_copy(posi[:, 0:1], lrow[0:128, 0:1]),
                             [(c_ld, t_lw), (c_gw, t_sc)], c_dv)
                t_vv = p.add(VE, lambda e, base=float(half * TH + jj * 128): e.tensor_scalar(
                    val2[:, 0:1], iota128[:, 0:1], base, None, op0=ALU.add),
                    [(c_dv, t_pi), (c_gw, t_io)], c_dv)
                t_vw = p.add(VE, lambda e: e.tensor_copy(val2[:, 1:2], lrow[0:128, 1:2]),
                             [(c_dv, t_vv)], c_dv)
                t_sc = p.add(GP, lambda e, half=half: e.indirect_dma_start(
                    flat_hs[half][0:128, :], IndirectOffsetOnAxis(posi[:, 0:1], 0),
                    val2[:, 0:2], None),
                    [(c_dv, t_vw), (c_st, t_fil[half])], c_gw, 16)
        t_scat[half] = t_sc

    if stop_after_postproc:
        p.add(SY, lambda e: e.nop(), [(c_st, c_st.v), (c_ga, c_ga.v), (c_ga1, c_ga1.v), (c_cv, c_cv.v), (c_ld, c_ld.v),
                                      (c_gw, c_gw.v), (c_dv, c_dv.v), (c_pe, c_pe.v),
                                      (c_ac, c_ac.v)])
        p.emit_all()
        return nc

    t_uv_dv = {-1: 0, -2: 0}
    t_y_dv = {-1: 0, -2: 0}
    t_add = {-1: None, -2: None}
    t_tile_add_last = {}
    t_posi_dv = {}
    t_tile_y_done = {-1: 0}
    t_tile_uv_done = {-1: 0}
    t_out_st = {-2: 0, -1: 0}
    t_p7_dv = {-2: 0, -1: 0}
    t_add_final = {}
    p7_site = {}
    uvu = 0
    yu = 0
    addu = 0
    glob_j = 0

    def emit_p7(half2, tcl):
        gtc = half2 * HTC + tcl
        r0 = gtc * 128
        pb = tcl % 2
        k7 = half2 * HTC + tcl
        t_lf = p.dma(SY, fin[:, pb * D:(pb + 1) * D], f_h[r0:r0 + 128, :],
                     [t_add_final[half2], (c_dv, t_p7_dv[k7 - 2])], c_ld)
        t_lx = p.dma(SY, x1in[:, pb * D:(pb + 1) * D], x1_h[r0:r0 + 128, :],
                     [(c_st, tok_x1_st[gtc])], c_ld)
        t_nmrf, rstdf, nmrf = ln_stats(fin[:, pb * D:(pb + 1) * D], D, 0 + 8 * pb,
                                       [(c_ld, t_lf), (c_ac, p7_site.get(pb, 0))], sq7)
        t_lnf = p.add(SC, lambda e, r=rstdf, m=nmrf, pb=pb: e.activation(
            oub[:, pb * D:(pb + 1) * D], fin[:, pb * D:(pb + 1) * D], AF.Identity,
            scale=r, bias=m), [(c_dv, t_nmrf), (c_st, t_out_st[k7 - 2])], c_ac)
        p7_site[pb] = t_lnf
        t_o = p.add(VE, lambda e, pb=pb: e.tensor_add(
            oub[:, pb * D:(pb + 1) * D], oub[:, pb * D:(pb + 1) * D],
            x1in[:, pb * D:(pb + 1) * D]), [(c_ac, t_lnf), (c_ld, t_lx)], c_dv)
        t_p7_dv[k7] = t_o
        t_out_st[k7] = p.dma(SY, out_h[r0:r0 + 128, :], oub[:, pb * D:(pb + 1) * D],
                             [(c_dv, t_o)], c_st)

    for half in range(2):
        last_fh = tok_fh_st[(half + 1) * HTC - 1]
        last_r2 = tok_r2_st[(half + 1) * HTC - 1]
        for j in range(NT_SP):
            pjb = glob_j % 2
            t_pb = p.dma(SY, pairblk[:, pjb * 8:(pjb + 1) * 8].rearrange(
                "q (k c) -> q k c", k=4),
                bass.AP(flat_hs[half].tensor, j * TS * 2,
                        [[2, 128], [256, 4], [1, 2]]),
                [(c_gw, t_scat[half]), (c_dv, t_posi_dv.get(glob_j - 2, 0)),
                 t_tile_add_last.get(glob_j - 2)], c_ld)
            t_pi6 = None
            for k in range(4):
                t_pi6 = p.add(VE, lambda e, k=k, pjb=pjb: e.tensor_copy(
                    posi6[:, pjb * 4 + k:pjb * 4 + k + 1],
                    pairblk[:, pjb * 8 + 2 * k:pjb * 8 + 2 * k + 1]),
                    [(c_ld, t_pb)] if k == 0 else (), c_dv if k == 3 else None)
            t_posi_dv[glob_j] = t_pi6
            t_g4 = None
            for k in range(4):
                t_g4 = p.add(GP, lambda e, k=k, pjb=pjb: e.indirect_dma_start(
                    r2g[:, k * D:(k + 1) * D], None,
                    r2tm_h[0:128, :],
                    IndirectOffsetOnAxis(posi6[:, pjb * 4 + k:pjb * 4 + k + 1], 0)),
                    [(c_dv, t_pi6), (c_st, last_r2), (c_st, t_zr_st),
                     (c_pe, t_tile_uv_done[glob_j - 1])] if k == 0 else (),
                    c_gw, 16)
            t_gT = None
            t_trp = None
            for k in range(4):
                for dc in range(NDC):
                    t_trp = p.add(TE, lambda e, k=k, dc=dc: e.transpose(
                        psD[:, dc * 128:(dc + 1) * 128],
                        r2g[:, k * D + dc * 128:k * D + (dc + 1) * 128], ident_bf[:]),
                        ([(c_gw, t_g4)] + ([(c_dv, t_gT)] if t_gT else []))
                        if dc == 0 else (), c_pe if dc == NDC - 1 else None)
                for dc in range(NDC):
                    t_gT = p.add(VE, lambda e, k=k, dc=dc: e.tensor_copy(
                        r2gT[:, dc * 512 + k * 128:dc * 512 + (k + 1) * 128],
                        psD[:, dc * 128:(dc + 1) * 128]),
                        [(c_pe, t_trp)] if dc == 0 else (),
                        c_dv if dc == NDC - 1 else None)
            t_gT_tok = t_gT
            ew = [(c_dv, t_etab[half]), (c_cv, t_conv_done),
                  (c_pe, t_tile_uv_done[glob_j - 1])]
            t_wl = {}
            wc = wcol_i[:, half * 16 + j:half * 16 + j + 1]
            t_wl[(0, 1)] = p.add(GP, lambda e, wc=wc: e.indirect_dma_start(
                w1s[:, :], None, w1c_h[0:128, :], IndirectOffsetOnAxis(wc, 0)),
                ew, c_gw, 16)
            t_wl[(1, 1)] = p.add(GP, lambda e, wc=wc: e.indirect_dma_start(
                w3s[:, :], None, w3c_h[0:128, :], IndirectOffsetOnAxis(wc, 0)),
                (), c_gw, 16)
            t_wl[(2, 1)] = p.add(GP, lambda e, wc=wc: e.indirect_dma_start(
                w2s[:, :], None, w2c_h[0:128, :], IndirectOffsetOnAxis(wc, 0)),
                [(c_pe, t_tile_y_done[glob_j - 1])], c_gw, 16)
            t_g6 = None
            t_uvmm = None
            for ic in range(NIC):
                pp = uvu % 2
                ub = psB if pp == 0 else psA
                uw = ()
                if ic == 0:
                    uw = [(c_gw, t_wl[(1, 1)]), (c_dv, t_gT_tok),
                          (c_dv, t_uv_dv[uvu - 2])]
                elif ic == 8:
                    uw = [(c_gw, t_wl[(1, 1)]), (c_dv, t_uv_dv[uvu - 2])]
                else:
                    uw = [(c_dv, t_uv_dv[uvu - 2])]
                for dc in range(NDC):
                    p.add(TE, lambda e, ic=ic, dc=dc, ub=ub: e.matmul(
                        ub[:, 0:512],
                        w1s[:, dc * I_DIM + ic * 128:dc * I_DIM + (ic + 1) * 128],
                        r2gT[:, dc * 512:(dc + 1) * 512],
                        start=(dc == 0), stop=(dc == NDC - 1)),
                        uw if dc == 0 else ())
                for dc in range(NDC):
                    t_uvmm = p.add(TE, lambda e, ic=ic, dc=dc, ub=ub: e.matmul(
                        ub[:, 512:1024],
                        w3s[:, dc * I_DIM + ic * 128:dc * I_DIM + (ic + 1) * 128],
                        r2gT[:, dc * 512:(dc + 1) * 512],
                        start=(dc == 0), stop=(dc == NDC - 1)),
                        (), c_pe if dc == NDC - 1 else None)
                t_si = p.add(SC, lambda e, pp=pp, ub=ub: e.activation(
                    silu_t[:, pp * 512:(pp + 1) * 512], ub[:, 0:512], AF.Silu),
                    [(c_pe, t_uvmm)], c_ac)
                gw_ = [(c_ac, t_si)]
                if ic == 0:
                    gw_ += [(c_pe, t_tile_y_done[glob_j - 1])]
                t_g6 = p.add(VE, lambda e, pp=pp, ub=ub, ic=ic: e.tensor_mul(
                    g_t[:, ic * 512:(ic + 1) * 512],
                    silu_t[:, pp * 512:(pp + 1) * 512], ub[:, 512:1024]), gw_, c_dv)
                t_uv_dv[uvu] = t_g6
                uvu += 1
            t_tile_uv_done[glob_j] = t_uvmm
            t_ymm = None
            for tch in range(4):
                t_ye = None
                for ot in range(2):
                    yp = yu % 2
                    yw = ()
                    for ic in range(NIC):
                        if ic == 0:
                            yw = [(c_dv, t_g6), (c_gw, t_wl[(2, 1)]),
                                  (c_dv, t_y_dv[yu - 2])]
                        else:
                            yw = ()
                        t_ymm = p.add(TE, lambda e, ic=ic, tch=tch, ot=ot, yp=yp: e.matmul(
                            psA[:, 1024 + yp * 512:1024 + (yp + 1) * 512],
                            g_t[:, ic * 512 + tch * 128:ic * 512 + tch * 128 + 128],
                            w2s[:, ic * D + ot * 512:ic * D + (ot + 1) * 512],
                            start=(ic == 0), stop=(ic == NIC - 1)),
                            yw, c_pe if ic == NIC - 1 else None)
                    t_ye = p.add(VE, lambda e, tch=tch, ot=ot, yp=yp, pjb=pjb:
                                 e.tensor_scalar(
                        ysc[:, (tch % 2) * D + ot * 512:(tch % 2) * D + (ot + 1) * 512],
                        psA[:, 1024 + yp * 512:1024 + (yp + 1) * 512],
                        pairblk[:, pjb * 8 + 2 * tch + 1:pjb * 8 + 2 * tch + 2],
                        None, op0=ALU.mult),
                        [(c_pe, t_ymm), t_add[addu - 2]], c_dv)
                    t_y_dv[yu] = t_ye
                    yu += 1
                t_ad = p.add(GP, lambda e, tch=tch, pjb=pjb: e.indirect_dma_start(
                    f_h[0:128, :],
                    IndirectOffsetOnAxis(posi6[:, pjb * 4 + tch:pjb * 4 + tch + 1], 0),
                    ysc[:, (tch % 2) * D:(tch % 2 + 1) * D], None,
                    compute_op=ALU.add),
                    [(c_dv, t_ye), t_add[addu - 1], (c_st, last_fh)],
                    c_ga if half == 0 else c_ga1, 16)
                t_add[addu] = (c_ga if half == 0 else c_ga1, t_ad)
                addu += 1
            t_tile_add_last[glob_j] = t_add[addu - 1]
            t_tile_y_done[glob_j] = t_ymm
            if half == 1 and j < HTC:
                emit_p7(0, j)
            glob_j += 1
        t_add_final[half] = t_add[addu - 1]

    for tcl in range(HTC):
        emit_p7(1, tcl)

    p.add(SY, lambda e: e.nop(), [(c_st, c_st.v), (c_ga, c_ga.v), (c_ga1, c_ga1.v), (c_cv, c_cv.v), (c_ld, c_ld.v),
                                  (c_gw, c_gw.v)])
    p.emit_all()
    return nc

def host_inputs(L0_attn_Z, L0_attn_decay, L0_attn_W, L0_attn_b, L0_attn_gamma,
                L0_attn_beta, L0_ffn_ln1_w, L0_ffn_ln2_w, L0_ffn_wq, L0_ffn_wk,
                L0_ffn_wv, L0_ffn_wo, L0_ffn_gate_w, L0_ffn_w1, L0_ffn_w2, L0_ffn_w3,
                n_cores=8):
    f = np.float32
    x = np.ascontiguousarray(np.asarray(L0_attn_Z, f))
    B, S_, D_ = x.shape
    per = B // n_cores
    common = dict(
        decay=np.asarray(L0_attn_decay, f).reshape(1, 1),
        b_row=np.ascontiguousarray(np.asarray(L0_attn_b, f).reshape(1, D_)),
        gamma_row=np.ascontiguousarray(np.asarray(L0_attn_gamma, f).reshape(1, D_)),
        beta_row=np.ascontiguousarray(np.asarray(L0_attn_beta, f).reshape(1, D_)),
        ln1w_row=np.ascontiguousarray(np.asarray(L0_ffn_ln1_w, f).reshape(1, D_)),
        ln2w_row=np.ascontiguousarray(np.asarray(L0_ffn_ln2_w, f).reshape(1, D_)),
        wT=np.ascontiguousarray(np.asarray(L0_attn_W, f).T),
        wqT=np.ascontiguousarray(np.asarray(L0_ffn_wq, f).T),
        wkT=np.ascontiguousarray(np.asarray(L0_ffn_wk, f).T),
        wvT=np.ascontiguousarray(np.asarray(L0_ffn_wv, f).T),
        woT=np.ascontiguousarray(np.asarray(L0_ffn_wo, f).T),
        gateT=np.ascontiguousarray(np.asarray(L0_ffn_gate_w, f).T),
        w1T=np.ascontiguousarray(np.asarray(L0_ffn_w1, f).transpose(0, 2, 1)),
        w3T=np.ascontiguousarray(np.asarray(L0_ffn_w3, f).transpose(0, 2, 1)),
        w2T=np.ascontiguousarray(np.asarray(L0_ffn_w2, f).transpose(0, 2, 1)),
    )
    in_maps = [dict(common, x0=np.ascontiguousarray(
        x[c * per:(c + 1) * per].reshape(per * S_, D_))) for c in range(n_cores)]
    return in_maps, (B, S_, D_, per)


def kernel(**inputs):
    in_maps, (B, S_, D_, per) = host_inputs(**inputs)
    nc = build_program(T=per * S_, S=S_)
    res = run_bass_kernel_spmd(nc, in_maps, core_ids=list(range(8)))
    out = np.concatenate([res.results[c]["out"] for c in range(8)], axis=0)
    return out.reshape(B, S_, D_)


# version 12
# speedup vs baseline: 1.5171x; 1.5171x over previous
import numpy as np
import concourse.bass as bass
import concourse.mybir as mybir
from concourse.bass_utils import run_bass_kernel_spmd
from concourse.bass_core_rust import IndirectOffsetOnAxis

F32 = mybir.dt.float32
I32 = mybir.dt.int32
BF16 = mybir.dt.bfloat16
AF = mybir.ActivationFunctionType
ALU = mybir.AluOpType
AX = mybir.AxisListType

D = 1024
NDC = 8
NH = 16
HD = 64
I_DIM = 2048
NIC = 16
NE = 8
EPS = 1e-5


class Ctr:
    def __init__(self, nc, name):
        self.h = nc.alloc_semaphore(name)
        self.v = 0
        self.name = name


class Prog:
    ENGINES = ("sync", "gpsimd", "tensor", "vector", "scalar")

    def __init__(self, nc):
        self.nc = nc
        self.ops = {e: [] for e in self.ENGINES}

    def ctr(self, name):
        return Ctr(self.nc, name)

    def add(self, eng, emit, waits=(), ctr=None, n=1):
        self.ops[eng].append((emit, [w for w in waits if w is not None and w[1] is not None and w[1] > 0],
                              ctr, n))
        if ctr is None:
            return None
        ctr.v += n
        return ctr.v

    def dma(self, eng, out, in_, waits=(), ctr=None, **kw):
        return self.add(eng, lambda e, o=out, i=in_, kw=kw: e.dma_start(o, i, **kw),
                        waits, ctr, 16)

    def emit_all(self):
        with self.nc.Block() as block:
            for eng in self.ENGINES:
                ops = self.ops[eng]

                def runner(engobj, ops=ops):
                    last_wait = {}
                    for emit, waits, ctr, n in ops:
                        for (c, v) in waits:
                            if last_wait.get(id(c), 0) >= v:
                                continue
                            engobj.wait_ge(c.h, v)
                            last_wait[id(c)] = v
                        instr = emit(engobj)
                        if ctr is not None:
                            instr.then_inc(ctr.h, n)
                getattr(block, eng)(runner)


def build_program(T=4096, S=2048, dbg=(), stop_after_postproc=0):
    nseq = T // S
    STC = S // 128
    NQT = S // 512
    NKC = S // 128
    TH = T // 2
    NHALF = 2
    HTC = TH // 128
    NTT = TH // 512

    nc = bass.Bass("TRN2", target_bir_lowering=False)
    p = Prog(nc)

    IN = lambda n, s: nc.dram_tensor(n, s, F32, kind="ExternalInput").ap()
    x0_h = IN("x0", [T, D])
    decay_h = IN("decay", [1, 1])
    b_h = IN("b_row", [1, D]); gamma_h = IN("gamma_row", [1, D]); beta_h = IN("beta_row", [1, D])
    ln1w_h = IN("ln1w_row", [1, D]); ln2w_h = IN("ln2w_row", [1, D])
    wT_h = IN("wT", [D, D])
    wqT_h = IN("wqT", [D, D]); wkT_h = IN("wkT", [D, D])
    wvT_h = IN("wvT", [D, D]); woT_h = IN("woT", [D, D])
    gateT_h = IN("gateT", [D, NE])
    w1T_h = IN("w1T", [NE, D, I_DIM]); w3T_h = IN("w3T", [NE, D, I_DIM])
    w2T_h = IN("w2T", [NE, I_DIM, D])
    out_h = nc.dram_tensor("out", [T, D], F32, kind="ExternalOutput").ap()

    def scratch(name, shape, dt):
        kind = "ExternalOutput" if name in dbg else "Internal"
        return nc.dram_tensor(name, shape, dt, kind=kind).ap()

    x1_h = scratch("x1_s", [T, D], F32)
    z_h = scratch("z_s", [T, D], F32)
    f_h = scratch("f_s", [T + 128, D], F32)
    qt_h = scratch("qt_s", [nseq, D, S], BF16)
    kt_h = scratch("kt_s", [nseq, D, S], BF16)
    v_h = scratch("v_s", [T, D], BF16)
    aot_h = scratch("aot_s", [nseq, D, S], BF16)
    r2t_h = scratch("r2t_s", [NDC, 128, T], BF16)
    r2tm_h = scratch("r2tm_s", [T + 128, D], BF16)
    wq3_h = scratch("wq3_s", [24, T], F32)
    NT_SP = (2 * T) // 512 + NE
    flat0_h = scratch("flat0_s", [NT_SP * 512, 2], F32)
    flat_hs = (flat0_h,)
    stage_h = scratch("stage_s", [4, T], F32)
    wcol_dbg_h = nc.dram_tensor("wcol_s", [128, 32], I32,
                                kind="ExternalOutput" if "wcol_s" in dbg
                                else "Internal").ap()
    pq_dbg_h = scratch("pq_s", [2, 8, 16], F32)
    brow_dbg_h = scratch("brow_s", [2, 1, 32], F32)
    w1c_h = scratch("w1c_s", [NE * 128, NDC * I_DIM], BF16)
    w3c_h = scratch("w3c_s", [NE * 128, NDC * I_DIM], BF16)
    w2c_h = scratch("w2c_s", [NE * 128, NIC * D], BF16)
    we_h = scratch("we_s", [T // 128, 128, NE], F32)
    pow_h = scratch("pow_s", [1, 256], F32)
    yln_h = scratch("yln_s", [T, D], BF16)
    st_h = scratch("st_s", [T, D], BF16)
    yb_h = scratch("yb_s", [T, D], F32)

    sb = lambda n, s, d_: nc.alloc_sbuf_tensor(n, s, d_)
    b_b = sb("b_b", [128, D], F32)
    gam_b = sb("gam_b", [128, D], F32)
    bet_b = sb("bet_b", [128, D], F32)
    l1w_b = sb("l1w_b", [128, D], F32)
    l2w_b = sb("l2w_b", [128, D], F32)
    eps_c = sb("eps_c", [128, 1], F32)
    ones_bf = sb("ones_bf", [128, 128], BF16)
    ones_f = sb("ones_f", [128, 128], F32)
    ident_bf = sb("ident_bf", [128, 128], BF16)
    ident_f = sb("ident_f", [128, 128], F32)
    powx = sb("powx", [1, 256], F32)
    a_sc = sb("a_sc", [1, 1], F32)
    apow_b = sb("apow_b", [128, 128], F32)
    a128_c = sb("a128_c", [128, 1], F32)
    ltri = sb("ltri", [128, 128], BF16)
    gate_w = sb("gate_w", [128, NDC * NE], F32)
    we_all = sb("we_all", [128, (T // 128) * NE], F32)
    stat = sb("stat", [128, 96], F32)

    ra0, _ = nc.bump_sbuf(65536)
    AT = nc.alloc_sbuf_tensor_at
    wq_w = AT("wq_w", [128, NDC * D], BF16, offset=ra0)
    wk_w = AT("wk_w", [128, NDC * D], BF16, offset=ra0 + 16384)
    wv_w = AT("wv_w", [128, NDC * D], BF16, offset=ra0 + 32768)
    wo_w = AT("wo_w", [128, NDC * D], BF16, offset=ra0)
    wt_w = AT("wt_w", [128, NDC * D], BF16, offset=ra0 + 49152)
    w1s = AT("w1s", [128, NDC * I_DIM], BF16, offset=ra0)
    w3s = AT("w3s", [128, NDC * I_DIM], BF16, offset=ra0 + 32768)
    rb0, _ = nc.bump_sbuf(NDC * S * 2)
    r1t_s = AT("r1t_s", [128, NDC * S], BF16, offset=rb0)
    g_t = AT("g_t", [128, NIC * 512], BF16, offset=rb0)
    ysc = AT("ysc", [128, 2 * D], F32, offset=rb0 + 16384)
    r2g = AT("r2g", [128, 4 * D], BF16, offset=rb0 + 24576)
    rc0, _ = nc.bump_sbuf(53248)
    off = rc0
    def at(name, shape, dt):
        nonlocal off
        t = AT(name, shape, dt, offset=off)
        nbytes = int(np.prod(shape[1:])) * (2 if dt == BF16 else 4)
        off += (nbytes + 31) & ~31
        return t
    x0_t = at("x0_t", [128, 2 * D], F32)
    yln_t = at("yln_t", [128, D], BF16)
    st_t = at("st_t", [128, 2 * D], BF16)
    carry = at("carry", [128, 2 * NDC], F32)
    ybuf = at("ybuf", [128, D], F32)
    gbuf = at("gbuf", [128, D], F32)
    sqs = at("sqs", [128, D], F32)
    x1b = at("x1b", [128, 2 * D], F32)
    zb = at("zb", [128, 2 * D], F32)
    r1tmp = at("r1tmp", [128, D], F32)
    assert off - rc0 <= 49152, off - rc0
    off = rc0
    aoin = at("aoin", [128, 2 * NDC * 128], BF16)
    zin = at("zin", [128, 2 * D], F32)
    hb = at("hb", [128, D], F32)
    r2tmp = at("r2tmp", [128, D], F32)
    r2f = at("r2f", [128, D], F32)
    r2tf = at("r2tf", [128, NDC * 128], F32)
    r2tb = at("r2tb", [128, NDC * 256], BF16)
    logit = at("logit", [128, NE], F32)
    gs1 = at("gs1", [128, NE], F32)
    gs2 = at("gs2", [128, NE], F32)
    sq4 = at("sq4", [128, D], F32)
    wq24 = at("wq24", [128, 24], F32)
    wq24t = at("wq24t", [24, 128], F32)
    r2b_st = at("r2b_st", [128, 2 * D], BF16)
    assert off - rc0 <= 53248, off - rc0
    off = rc0
    w2s = at("w2s", [128, NIC * D], BF16)
    convst = AT("convst", [128, 2 * 1024], BF16, offset=rc0 + 49152)
    r2gT = at("r2gT", [128, NDC * 512], BF16)
    silu_t = at("silu_t", [128, 2 * 512], BF16)
    pairblk = at("pairblk", [128, 2 * 8], F32)
    posi6 = at("posi6", [128, 2 * 4], I32)
    assert off - rc0 <= 53248, off - rc0
    rd0, _ = nc.bump_sbuf(35712)
    off = rd0
    ybufB = at("ybufB", [128, D], F32)
    gbufB = at("gbufB", [128, D], F32)
    sqsB = at("sqsB", [128, D], F32)
    r1tmpB = at("r1tmpB", [128, D], F32)
    r1bB = AT("r1bB", [128, D], BF16, offset=rd0 + 24704)
    ylnB = AT("ylnB", [128, D], BF16, offset=rd0 + 26752)
    assert off - rd0 <= 35712, off - rd0
    off = rd0
    pstage = at("pstage", [128, 2 * 512], BF16)
    vstage = at("vstage", [128, 2 * 512], BF16)
    assert off - rd0 <= 35712, off - rd0
    off = rd0
    qt2 = at("qt2", [128, 2 * S], BF16)
    kt2 = at("kt2", [128, 2 * S], BF16)
    vh2 = at("vh2", [128, 2 * NKC * 130], BF16)
    mcols = at("mcols", [128, 32], F32)
    mrow_t = at("mrow_t", [16, 128], BF16)
    negm = at("negm", [1, 2048], BF16)
    pt_t = at("pt_t", [128, 2 * 512], BF16)
    recb = at("recb", [128, 512], F32)
    lrow = AT("lrow", [128, 512], F32, offset=off - 2048)
    ao_st = at("ao_st", [128, 512], BF16)
    assert off - rd0 <= 35712, off - rd0
    prow_a = AT("prow_a", [8, T], F32, offset=ra0)
    prow_b = AT("prow_b", [8, T], F32, offset=ra0 + 16384)
    prow_c = AT("prow_c", [8, T], F32, offset=ra0 + 32768)
    prow_d = AT("prow_d", [8, T], F32, offset=rb0)
    jrow8 = AT("jrow8", [8, NT_SP], F32, offset=rd0 + 28672)
    etab_f = AT("etab_f", [1, NT_SP], F32, offset=rd0 + 28768)
    iota128 = AT("iota128", [128, 1], F32, offset=rd0 + 28864)
    srow6 = AT("srow6", [1, 512], F32, offset=rd0 + 28896)
    pq = AT("pq", [8, 16], F32, offset=rd0 + 30944)
    brow = AT("brow", [1, 32], F32, offset=rd0 + 31008)
    val2 = AT("val2", [128, 2], F32, offset=rd0 + 31200)
    posi = AT("posi", [128, 1], I32, offset=rd0 + 31232)
    zraw = AT("zraw", [128, D], BF16, offset=rd0 + 31264)
    wcol_f = AT("wcol_f", [128, NT_SP], F32, offset=rd0 + 33312)
    wcol_i = AT("wcol_i", [128, NT_SP], I32, offset=rd0 + 33408)
    r1b = AT("r1b", [128, D], BF16, offset=rd0 + 33504)
    off = rd0
    fin = at("fin", [128, 2 * D], F32)
    x1in = at("x1in", [128, 2 * D], F32)
    oub = at("oub", [128, 2 * D], F32)
    sq7 = at("sq7", [128, D], F32)
    assert off - rd0 <= 35712, off - rd0

    psA = nc.alloc_psum_tensor("psA", [128, 2048], F32)
    psB = nc.alloc_psum_tensor("psB", [128, 1024], F32)
    psC = nc.alloc_psum_tensor("psC", [128, 512], F32)
    psD = nc.alloc_psum_tensor("psD", [128, 1024], BF16)

    c_ld = p.ctr("c_ld")
    c_st = p.ctr("c_st")
    c_gw = p.ctr("c_gw")
    c_ga = p.ctr("c_ga")
    c_ga1 = p.ctr("c_ga1")
    c_cv = p.ctr("c_cv")
    c_pe = p.ctr("c_pe")
    c_dv = p.ctr("c_dv")
    c_ac = p.ctr("c_ac")

    SY, GP, TE, VE, SC = "sync", "gpsimd", "tensor", "vector", "scalar"


    def ln_stats(src, n_feat, cols, waits, sq_scratch):
        sq_scratch = sq_scratch[:]
        s0, s1, mu, var, rstd, nmr = (stat[:, cols + i:cols + i + 1] for i in range(6))
        t0 = p.add(VE, lambda e, s=src, o=s0: e.reduce_sum(o, s, axis=AX.X), waits, c_dv)
        t_sq = p.add(SC, lambda e, s=src, o=sq_scratch, a=s1: e.activation(
            o, s, AF.Square, accum_out=a), waits, c_ac)
        t1 = p.add(VE, lambda e, i=s0, o=mu: e.tensor_scalar(o, i, 1.0 / n_feat, None, op0=ALU.mult),
                   [(c_ac, t_sq), (c_dv, t0)], c_dv)
        t2 = p.add(VE, lambda e, i=mu, o=nmr: e.tensor_scalar(o, i, i, None, op0=ALU.mult),
                   [(c_dv, t1)], c_dv)
        t_var = p.add(VE, lambda e, i=s1, o=var, m=nmr: e.tensor_scalar(
            o, i, 1.0 / n_feat, m, op0=ALU.mult, op1=ALU.subtract), [(c_dv, t2)], c_dv)
        t_ln = p.add(SC, lambda e, i=var, o=rstd: e.activation(o, i, AF.Ln, bias=eps_c[:, 0:1]),
                     [(c_dv, t_var)], c_ac)
        t_ex = p.add(SC, lambda e, i=rstd, o=rstd: e.activation(o, i, AF.Exp, scale=-0.5),
                     [(c_ac, t_ln)], c_ac)
        t_nmr = p.add(VE, lambda e, i=mu, r=rstd, o=nmr: e.scalar_tensor_tensor(
            o, i, -1.0, r, op0=ALU.mult, op1=ALU.mult), [(c_ac, t_ex), (c_dv, t_var)], c_dv)
        return t_nmr, rstd, nmr

    def ln_stats_gen(src, n_feat, cols, waits, sq_scratch, res):
        sq_scratch = sq_scratch[:]
        s0, s1, mu, var, rstd, nmr = (stat[:, cols + i:cols + i + 1] for i in range(6))
        t0 = p.add(VE, lambda e, s=src, o=s0: e.reduce_sum(o, s, axis=AX.X), waits, c_dv)
        yield
        t_sq = p.add(SC, lambda e, s=src, o=sq_scratch, a=s1: e.activation(
            o, s, AF.Square, accum_out=a), waits, c_ac)
        yield
        t1 = p.add(VE, lambda e, i=s0, o=mu: e.tensor_scalar(o, i, 1.0 / n_feat, None, op0=ALU.mult),
                   [(c_ac, t_sq), (c_dv, t0)], c_dv)
        yield
        t2 = p.add(VE, lambda e, i=mu, o=nmr: e.tensor_scalar(o, i, i, None, op0=ALU.mult),
                   [(c_dv, t1)], c_dv)
        yield
        t_var = p.add(VE, lambda e, i=s1, o=var, m=nmr: e.tensor_scalar(
            o, i, 1.0 / n_feat, m, op0=ALU.mult, op1=ALU.subtract), [(c_dv, t2)], c_dv)
        yield
        t_ln = p.add(SC, lambda e, i=var, o=rstd: e.activation(o, i, AF.Ln, bias=eps_c[:, 0:1]),
                     [(c_dv, t_var)], c_ac)
        yield
        t_ex = p.add(SC, lambda e, i=rstd, o=rstd: e.activation(o, i, AF.Exp, scale=-0.5),
                     [(c_ac, t_ln)], c_ac)
        yield
        t_nmr = p.add(VE, lambda e, i=mu, r=rstd, o=nmr: e.scalar_tensor_tensor(
            o, i, -1.0, r, op0=ALU.mult, op1=ALU.mult), [(c_ac, t_ex), (c_dv, t_var)], c_dv)
        res["t"], res["rstd"], res["nmr"] = t_nmr, rstd, nmr
        yield

    def rms_stats_gen(src, n_feat, cols, waits, sq_scratch, res):
        sq_scratch = sq_scratch[:]
        s1, msq, rr = (stat[:, cols + i:cols + i + 1] for i in range(3))
        t_sq = p.add(SC, lambda e, s=src, o=sq_scratch, a=s1: e.activation(
            o, s, AF.Square, accum_out=a), waits, c_ac)
        yield
        t_m = p.add(VE, lambda e, i=s1, o=msq: e.tensor_scalar(o, i, 1.0 / n_feat, None,
                                                               op0=ALU.mult),
                    [(c_ac, t_sq)], c_dv)
        yield
        t_ln = p.add(SC, lambda e, i=msq, o=rr: e.activation(o, i, AF.Ln, bias=eps_c[:, 0:1]),
                     [(c_dv, t_m)], c_ac)
        yield
        t_rr = p.add(SC, lambda e, i=rr, o=rr: e.activation(o, i, AF.Exp, scale=-0.5),
                     [(c_ac, t_ln)], c_ac)
        res["t"], res["rr"] = t_rr, rr
        yield

    def rms_stats(src, n_feat, cols, waits, sq_scratch):
        sq_scratch = sq_scratch[:]
        s1, msq, rr = (stat[:, cols + i:cols + i + 1] for i in range(3))
        t_sq = p.add(SC, lambda e, s=src, o=sq_scratch, a=s1: e.activation(
            o, s, AF.Square, accum_out=a), waits, c_ac)
        t_m = p.add(VE, lambda e, i=s1, o=msq: e.tensor_scalar(o, i, 1.0 / n_feat, None,
                                                               op0=ALU.mult),
                    [(c_ac, t_sq)], c_dv)
        t_ln = p.add(SC, lambda e, i=msq, o=rr: e.activation(o, i, AF.Ln, bias=eps_c[:, 0:1]),
                     [(c_dv, t_m)], c_ac)
        t_rr = p.add(SC, lambda e, i=rr, o=rr: e.activation(o, i, AF.Exp, scale=-0.5),
                     [(c_ac, t_ln)], c_ac)
        return t_rr, rr

    for dst, srch in ((b_b, b_h), (gam_b, gamma_h), (bet_b, beta_h),
                      (l1w_b, ln1w_h), (l2w_b, ln2w_h)):
        p.dma(SY, dst[:], bass.AP(srch.tensor, 0, [[0, 128], [1, D]]), (), c_ld)
    t_ld_a = p.dma(SY, a_sc[:], decay_h, (), c_ld)

    for dc in range(NDC):
        p.dma(GP, wt_w[:, dc * D:(dc + 1) * D], wT_h[dc * 128:(dc + 1) * 128, :], (), c_gw)
    for dc in range(NDC):
        t_gw_wt = p.dma(GP, gate_w[:, dc * NE:(dc + 1) * NE],
                        gateT_h[dc * 128:(dc + 1) * 128, :], (), c_gw)

    p.add(GP, lambda e: e.memset(ident_bf[:], 0.0))
    p.add(GP, lambda e: e.affine_select(out=ident_bf[:], in_=ident_bf[:],
                                        compare_op=ALU.not_equal, fill=1.0, base=0,
                                        pattern=[[-1, 128]], channel_multiplier=1))
    p.add(GP, lambda e: e.memset(ident_f[:], 0.0))
    p.add(GP, lambda e: e.affine_select(out=ident_f[:], in_=ident_f[:],
                                        compare_op=ALU.not_equal, fill=1.0, base=0,
                                        pattern=[[-1, 128]], channel_multiplier=1))
    p.add(GP, lambda e: e.memset(ones_bf[:], 1.0))
    for pb in range(2):
        for kc in range(NKC):
            for hoff in (64, 129):
                p.add(GP, lambda e, c=pb * (NKC * 130) + kc * 130 + hoff:
                      e.memset(vh2[:, c:c + 1], 1.0))
    t_gp_done = p.add(GP, lambda e: e.memset(ones_f[:], 1.0), (), c_gw)

    t_a = p.add(SC, lambda e: e.activation(a_sc[:], a_sc[:], AF.Sigmoid),
                [(c_ld, t_ld_a)], c_ac)
    p.add(VE, lambda e: e.memset(eps_c[:], EPS))

    p.add(VE, lambda e: e.memset(powx[:], 0.0), [(c_ac, t_a)])
    t_ms1 = p.add(VE, lambda e: e.memset(powx[0:1, 127:128], 1.0), (), c_dv)
    t_pow = t_ms1
    t_pow = None
    for k in range(1, 129):
        t_pow = p.add(VE, lambda e, k=k: e.tensor_scalar(
            powx[0:1, 127 + k:128 + k], powx[0:1, 126 + k:127 + k], a_sc[0:1, 0:1],
            None, op0=ALU.mult), [(c_dv, t_pow)], c_dv)
    t_pwst = p.dma(SY, pow_h, powx[0:1, :], [(c_dv, t_pow)], c_st)
    p.dma(SY, apow_b[:], bass.AP(pow_h.tensor, 128, [[0, 128], [1, 128]]),
          [(c_st, t_pwst)], c_ld)
    t_a128 = p.dma(SY, a128_c[:], bass.AP(pow_h.tensor, 255, [[0, 128], [1, 1]]), (), c_ld)
    t_ltri = None
    for srow in range(128):
        t_ltri = p.dma(GP, ltri[srow:srow + 1, :],
                       bass.AP(pow_h.tensor, 127 - srow, [[1, 1], [1, 128]]),
                       [(c_st, t_pwst)] if srow == 0 else (), c_gw)
    t_gw_qkv0 = None
    for w_sbuf, w_ap in ((wq_w, wqT_h), (wk_w, wkT_h), (wv_w, wvT_h)):
        for dc in range(NDC):
            t_gw_qkv0 = p.dma(GP, w_sbuf[:, dc * D:(dc + 1) * D],
                              w_ap[dc * 128:(dc + 1) * 128, :], (), c_gw)
    t_cv_in = {}
    cvu = 0
    t_cv_st = {-1: 0, -2: 0}
    for e_ in range(NE):
        for dc in range(NDC):
            for wsrc, wdst in ((w1T_h, w1c_h), (w3T_h, w3c_h)):
                for hc in range(2):
                    pbv = cvu % 2
                    t_ci = p.dma(GP, convst[:, pbv * 1024:(pbv + 1) * 1024],
                                 wsrc[e_, dc * 128:(dc + 1) * 128,
                                      hc * 1024:(hc + 1) * 1024],
                                 [(c_gw, t_gw_wt), (c_cv, t_cv_st[cvu - 2])], c_cv)
                    t_cv_st[cvu] = p.dma(GP,
                        bass.AP(wdst.tensor,
                                e_ * 128 * NDC * I_DIM + dc * I_DIM + hc * 1024,
                                [[NDC * I_DIM, 128], [1, 1024]]),
                        convst[:, pbv * 1024:(pbv + 1) * 1024],
                        [(c_cv, t_ci)], c_cv)
                    cvu += 1
        for ic in range(NIC):
            pbv = cvu % 2
            t_ci = p.dma(GP, convst[:, pbv * 1024:(pbv + 1) * 1024],
                         w2T_h[e_, ic * 128:(ic + 1) * 128, :],
                         [(c_cv, t_cv_st[cvu - 2])], c_cv)
            t_cv_st[cvu] = p.dma(GP,
                bass.AP(w2c_h.tensor, e_ * 128 * NIC * D + ic * D,
                        [[NIC * D, 128], [1, D]]),
                convst[:, pbv * 1024:(pbv + 1) * 1024],
                [(c_cv, t_ci)], c_cv)
            cvu += 1
    t_conv_done = t_cv_st[cvu - 1]
    p.conv_done_token = t_conv_done

    tok_x1_st = [0] * (T // 128)
    tok_z_st = [0] * (T // 128)
    tok_fh_st = [0] * (T // 128)
    tok_r2_st = {}
    tok_we = [0] * (T // 128)
    p4_last_pe = 0
    p4_last_r2st = 0

    for seq in range(nseq):
        w_wait = [(c_pe, p4_last_pe)] if seq > 0 else []
        if seq == 0:
            t_gw_qkv = t_gw_qkv0
        else:
            t_gw_qkv = None
            for w_sbuf, w_ap in ((wq_w, wqT_h), (wk_w, wkT_h), (wv_w, wvT_h)):
                for dc in range(NDC):
                    t_gw_qkv = p.dma(GP, w_sbuf[:, dc * D:(dc + 1) * D],
                                     w_ap[dc * 128:(dc + 1) * 128, :],
                                     w_wait if dc == 0 else (), c_gw)

        p1_w = [(c_pe, c_pe.v), (c_dv, c_dv.v), (c_ac, c_ac.v), (c_st, c_st.v)]
        t_carry0 = p.add(VE, lambda e: e.memset(carry[:, NDC:2 * NDC], 0.0),
                         [(c_ld, t_a128), (c_gw, t_ltri)], c_dv)

        dbg_tok = {}
        t_dv_evac = {-1: t_carry0}
        t_pe_scan = {-1: 0, -2: 0}
        t_dv_yb = {-1: 0}
        t_dv_x1add = {-1: 0, -2: 0}
        t_pe_r1t = {-1: 0, -2: 0}
        t_dv_r1t = {-1: 0, -2: 0}
        t_ac_r1s = {-1: 0}

        site_last = {}

        def p1_gen(tc):
            pb = tc % 2
            gtc = seq * STC + tc
            r0 = gtc * 128
            ybuf_p = ybuf if pb == 0 else ybufB
            gbuf_p = gbuf if pb == 0 else gbufB
            sqs_p = sqs if pb == 0 else sqsB
            r1tmp_p = r1tmp if pb == 0 else r1tmpB
            r1b_p = r1b if pb == 0 else r1bB
            yln_p = yln_t if pb == 0 else ylnB
            xo = x0_t[:, pb * D:(pb + 1) * D]
            t_ldx = p.dma(SY, xo, x0_h[r0:r0 + 128, :],
                          [(c_dv, t_dv_x1add[tc - 2])] + (p1_w if tc < 2 else []), c_ld)
            yield
            res = {}
            yield from ln_stats_gen(xo, D, 0 + 8 * pb,
                                    [(c_ld, t_ldx), (c_ac, site_last.get(pb, 0))] +
                                    (p1_w if tc < 2 else []), sqs_p, res)
            t_nmr, rstd, nmr = res["t"], res["rstd"], res["nmr"]
            t_yln = p.add(SC, lambda e, x=xo, r=rstd, m=nmr, o=yln_p: e.activation(
                o[:], x, AF.Identity, scale=r, bias=m),
                [(c_dv, t_nmr), (c_pe, t_pe_scan[tc - 2])], c_ac)
            site_last[pb] = t_yln
            yield
            t_scan = None
            for dc in range(NDC):
                t_scan = p.add(TE, lambda e, dc=dc, o=yln_p: e.matmul(
                    psA[:, dc * 128:(dc + 1) * 128], o[:, dc * 128:(dc + 1) * 128],
                    ltri[:], start=True, stop=True),
                    [(c_ac, t_yln), (c_dv, t_dv_evac[tc - 1]), (c_gw, t_ltri)]
                    if dc == 0 else (), c_pe if dc == NDC - 1 else None)
            t_pe_scan[tc] = t_scan
            yield
            po, pn = (1 - pb) * NDC, pb * NDC
            t_ev = None
            for dc in range(NDC):
                p.add(VE, lambda e, dc=dc, po=po, pn=pn: e.scalar_tensor_tensor(
                    carry[:, pn + dc:pn + dc + 1], a128_c[:], carry[:, po + dc:po + dc + 1],
                    psA[:, dc * 128 + 127:dc * 128 + 128], op0=ALU.mult, op1=ALU.add),
                    [(c_pe, t_scan)] if dc == 0 else ())
                t_ev = p.add(VE, lambda e, dc=dc, po=po, pb=pb: e.scalar_tensor_tensor(
                    st_t[:, pb * D + dc * 128:pb * D + (dc + 1) * 128], apow_b[:],
                    carry[:, po + dc:po + dc + 1], psA[:, dc * 128:(dc + 1) * 128],
                    op0=ALU.mult, op1=ALU.add), (), c_dv if dc == NDC - 1 else None)
                if dc == 3:
                    yield
            t_dv_evac[tc] = t_ev
            yield
            t_y = None
            for et in range(2):
                for dc in range(NDC):
                    t_y = p.add(TE, lambda e, et=et, dc=dc, pb=pb: e.matmul(
                        psA[:, 1024 + et * 512:1024 + (et + 1) * 512],
                        st_t[:, pb * D + dc * 128:pb * D + (dc + 1) * 128],
                        wt_w[:, dc * D + et * 512:dc * D + (et + 1) * 512],
                        start=(dc == 0), stop=(dc == NDC - 1)),
                        [(c_dv, t_ev), (c_dv, t_dv_yb[tc - 1]), (c_gw, t_gw_wt)]
                        if (et == 0 and dc == 0) else (),
                        c_pe if (et == 1 and dc == NDC - 1) else None)
            yield
            t_yb = p.add(VE, lambda e, o=ybuf_p: e.tensor_add(o[:], psA[:, 1024:2048], b_b[:]),
                         [(c_pe, t_y)], c_dv)
            t_dv_yb[tc] = t_yb
            yield
            res1 = {}
            yield from ln_stats_gen(ybuf_p[:], D, 16 + 8 * pb,
                                    [(c_dv, t_yb), (c_ac, site_last.get(16 + pb, 0))],
                                    sqs_p, res1)
            t_l1 = p.add(SC, lambda e, r=res1["rstd"], m=res1["nmr"], i=ybuf_p, o=sqs_p:
                         e.activation(o[:], i[:], AF.Identity, scale=r, bias=m),
                         [(c_dv, res1["t"])], c_ac)
            site_last[16 + pb] = t_l1
            yield
            t_g1 = p.add(VE, lambda e, i=sqs_p, o=gbuf_p: e.tensor_mul(o[:], i[:], gam_b[:]),
                         [(c_ac, t_l1)], c_dv)
            yield
            t_g2 = p.add(VE, lambda e, o=gbuf_p: e.tensor_add(o[:], o[:], bet_b[:]),
                         [(c_dv, t_g1)], c_dv)
            yield
            res2 = {}
            yield from ln_stats_gen(gbuf_p[:], D, 32 + 8 * pb,
                                    [(c_dv, t_g2), (c_ac, site_last.get(32 + pb, 0))],
                                    sqs_p, res2)
            t_l2 = p.add(SC, lambda e, r=res2["rstd"], m=res2["nmr"], i=gbuf_p, o=ybuf_p:
                         e.activation(o[:], i[:], AF.Identity, scale=r, bias=m),
                         [(c_dv, res2["t"])], c_ac)
            site_last[32 + pb] = t_l2
            yield
            t_x1a = p.add(VE, lambda e, x=xo, pb=pb, i=ybuf_p: e.tensor_add(
                x1b[:, pb * D:(pb + 1) * D], x, i[:]),
                [(c_ac, t_l2), (c_st, tok_x1_st[gtc - 2] if gtc >= 2 else 0)], c_dv)
            t_dv_x1add[tc] = t_x1a
            tok_x1_st[gtc] = p.dma(SY, x1_h[r0:r0 + 128, :], x1b[:, pb * D:(pb + 1) * D],
                                   [(c_dv, t_x1a)], c_st)
            yield
            resz = {}
            yield from ln_stats_gen(x1b[:, pb * D:(pb + 1) * D], D, 48 + 8 * pb,
                                    [(c_dv, t_x1a), (c_ac, site_last.get(48 + pb, 0))],
                                    sqs_p, resz)
            t_z = p.add(SC, lambda e, r=resz["rstd"], m=resz["nmr"], pb=pb: e.activation(
                zb[:, pb * D:(pb + 1) * D], x1b[:, pb * D:(pb + 1) * D], AF.Identity,
                scale=r, bias=m),
                [(c_dv, resz["t"]), (c_st, tok_z_st[gtc - 2] if gtc >= 2 else 0)], c_ac)
            site_last[48 + pb] = t_z
            tok_z_st[gtc] = p.dma(SY, z_h[r0:r0 + 128, :], zb[:, pb * D:(pb + 1) * D],
                                  [(c_ac, t_z)], c_st)
            yield
            resr = {}
            yield from rms_stats_gen(zb[:, pb * D:(pb + 1) * D], D, 64 + 4 * pb,
                                     [(c_ac, t_z), (c_ac, site_last.get(64 + pb, 0))],
                                     sqs_p, resr)
            t_r1s = p.add(SC, lambda e, rr=resr["rr"], pb=pb, o=r1tmp_p: e.activation(
                o[:], zb[:, pb * D:(pb + 1) * D], AF.Identity, scale=rr),
                [(c_ac, resr["t"]), (c_dv, t_dv_r1t[tc - 2])], c_ac)
            site_last[64 + pb] = t_r1s
            yield
            t_r1 = p.add(VE, lambda e, i=r1tmp_p, o=r1b_p: e.tensor_mul(o[:], i[:], l1w_b[:]),
                         [(c_ac, t_r1s), (c_pe, t_pe_r1t[tc - 2])], c_dv)
            yield
            t_tr = None
            for dc in range(NDC):
                t_tr = p.add(TE, lambda e, dc=dc, i=r1b_p: e.transpose(
                    psD[:, dc * 128:(dc + 1) * 128], i[:, dc * 128:(dc + 1) * 128],
                    ident_bf[:]),
                    [(c_dv, t_r1), (c_dv, t_dv_r1t[tc - 1]), (c_gw, t_gp_done)]
                    if dc == 0 else (), c_pe if dc == NDC - 1 else None)
            t_pe_r1t[tc] = t_tr
            yield
            t_r1t = None
            for dc in range(NDC):
                t_r1t = p.add(VE, lambda e, dc=dc, tc=tc: e.tensor_copy(
                    r1t_s[:, dc * S + tc * 128:dc * S + (tc + 1) * 128],
                    psD[:, dc * 128:(dc + 1) * 128]),
                    [(c_pe, t_tr)] if dc == 0 else (), c_dv if dc == NDC - 1 else None)
            t_dv_r1t[tc] = t_r1t
            yield

        p2_state = {"unit": 0, "st": {-1: 0, -2: 0}, "sst": {-1: 0, -2: 0},
                    "mm": None}

        def emit_p2_block(tt4):
            for w_sbuf, dst_h in ((wq_w, qt_h), (wk_w, kt_h)):
                for ec in range(NDC):
                    unit = p2_state["unit"]
                    pp = unit % 2
                    t_mm = None
                    for dc in range(NDC):
                        t_mm = p.add(TE, lambda e, dc=dc, ec=ec, tt4=tt4, pp=pp,
                                     w=w_sbuf: e.matmul(
                            psB[:, pp * 512:(pp + 1) * 512],
                            w[:, dc * D + ec * 128:dc * D + ec * 128 + 128],
                            r1t_s[:, dc * S + tt4 * 512:dc * S + (tt4 + 1) * 512],
                            start=(dc == 0), stop=(dc == NDC - 1)),
                            [(c_dv, t_dv_r1t[4 * tt4 + 3]), (c_gw, t_gw_qkv),
                             (c_dv, p2_state["st"][unit - 2])] if dc == 0 else (),
                            c_pe if dc == NDC - 1 else None)
                    p2_state["mm"] = t_mm
                    t_ev2 = p.add(VE, lambda e, pp=pp: e.tensor_copy(
                        pt_t[:, pp * 512:(pp + 1) * 512],
                        psB[:, pp * 512:(pp + 1) * 512]),
                        [(c_pe, t_mm), (c_st, p2_state["sst"][unit - 2])], c_dv)
                    p2_state["st"][unit] = t_ev2
                    p2_state["sst"][unit] = p.dma(
                        SY, dst_h[seq, ec * 128:(ec + 1) * 128, tt4 * 512:(tt4 + 1) * 512],
                        pt_t[:, pp * 512:(pp + 1) * 512], [(c_dv, t_ev2)], c_st)
                    p2_state["unit"] = unit + 1

        LAG = 8
        act_g = []
        next_tc = 0
        steps_newest = 0
        while act_g or next_tc < STC:
            if next_tc < STC and (not act_g or (len(act_g) < 2 and steps_newest >= LAG)):
                act_g.append((next_tc, p1_gen(next_tc)))
                next_tc += 1
                steps_newest = 0
            for ent in list(act_g):
                tc_, g = ent
                try:
                    next(g)
                    if ent is act_g[-1]:
                        steps_newest += 1
                except StopIteration:
                    act_g.remove(ent)
                    if tc_ % 4 == 3:
                        emit_p2_block(tc_ // 4)

        p1_last_dv = t_dv_r1t[STC - 1]
        last_qk_st = p2_state["sst"][p2_state["unit"] - 1]
        vunit = 0
        t_v_dv = {-1: 0, -2: 0}
        t_v_st = {-1: 0, -2: 0}
        for tcq in range(STC):
            for et in range(2):
                pp = vunit % 2
                for dc in range(NDC):
                    t_mm = p.add(TE, lambda e, dc=dc, tcq=tcq, et=et, pp=pp: e.matmul(
                        psA[:, pp * 512:(pp + 1) * 512],
                        r1t_s[:, dc * S + tcq * 128:dc * S + (tcq + 1) * 128],
                        wv_w[:, dc * D + et * 512:dc * D + (et + 1) * 512],
                        start=(dc == 0), stop=(dc == NDC - 1)),
                        [(c_dv, t_v_dv[vunit - 2]), (c_dv, p1_last_dv)] if dc == 0 else (),
                        c_pe if dc == NDC - 1 else None)
                t_ev3 = p.add(VE, lambda e, pp=pp: e.tensor_copy(
                    vstage[:, pp * 512:(pp + 1) * 512], psA[:, pp * 512:(pp + 1) * 512]),
                    [(c_pe, t_mm), (c_st, t_v_st[vunit - 2])], c_dv)
                t_v_dv[vunit] = t_ev3
                t_v_st[vunit] = p.dma(
                    SY, v_h[seq * S + tcq * 128:seq * S + (tcq + 1) * 128,
                            et * 512:(et + 1) * 512],
                    vstage[:, pp * 512:(pp + 1) * 512], [(c_dv, t_ev3)], c_st)
                vunit += 1
        p2_last_pe = t_mm
        last_v_st = t_v_st[vunit - 1]
        t_gw_wo = None
        for dc in range(NDC):
            t_gw_wo = p.dma(GP, wo_w[:, dc * D:(dc + 1) * D],
                            woT_h[dc * 128:(dc + 1) * 128, :],
                            [(c_pe, p2_state["mm"]), (c_pe, p2_last_pe)]
                            if dc == 0 else (), c_gw)

        p3_dv0 = c_dv.v
        t_pair_pe_done = {-1: 0, -2: 0}
        t_ao_dv = {}
        t_recb_dv = {}
        t_st_exp = {-1: 0, -2: 0}
        t_aost = {-1: 0}
        pair_tok = {}
        aou = 0
        t_pv = None

        def emit_pair_loads(hp):
            pqb = hp % 2
            wv_ = [(c_pe, t_pair_pe_done[hp - 2]), (c_st, last_qk_st), (c_st, last_v_st)]
            t_lq = p.dma(SY, qt2[:, pqb * S:(pqb + 1) * S],
                         qt_h[seq, hp * 128:(hp + 1) * 128, :], wv_, c_ld)
            t_lk = p.dma(SY, kt2[:, pqb * S:(pqb + 1) * S],
                         kt_h[seq, hp * 128:(hp + 1) * 128, :], (), c_ld)
            t_lv = None
            for sub in range(2):
                h2 = 2 * hp + sub
                hoff = 0 if sub == 0 else 65
                srcv = bass.AP(v_h.tensor, (seq * S) * D + h2 * HD,
                               [[D, 128], [128 * D, NKC], [1, HD]])
                dstv = vh2[:, pqb * NKC * 130:(pqb + 1) * NKC * 130].rearrange(
                    "q (kc w) -> q kc w", kc=NKC)[:, :, hoff:hoff + HD]
                t_lv = p.dma(SY, dstv, srcv, (), c_ld)
            pair_tok[hp] = (t_lq, t_lk, t_lv)

        emit_pair_loads(0)
        for h_ in range(NH):
            hp = h_ // 2
            pqb = hp % 2
            hsl = slice(0, 64) if h_ % 2 == 0 else slice(64, 128)
            hoff = 0 if h_ % 2 == 0 else 65
            if h_ % 2 == 0 and hp + 1 < NH // 2:
                emit_pair_loads(hp + 1)
            for qt in range(NQT):
                def emit_pv(kc, pp2, first_w=()):
                    nonlocal t_pv
                    pvw = [(c_ac, t_st_exp[(h_ * NQT + qt) * NKC + kc])] + list(first_w)
                    t_pv = p.add(TE, lambda e, kc=kc, pqb=pqb, hoff=hoff, pp2=pp2: e.matmul(
                        psC[0:65, 0:512],
                        vh2[:, pqb * NKC * 130 + kc * 130 + hoff:
                            pqb * NKC * 130 + kc * 130 + hoff + 65],
                        pt_t[:, pp2 * 512:(pp2 + 1) * 512],
                        start=(kc == 0), stop=(kc == NKC - 1)),
                        pvw, c_pe if kc == NKC - 1 else None)

                for kc in range(NKC):
                    stu = (h_ * NQT + qt) * NKC + kc
                    pp2 = stu % 2
                    sw = [(c_ac, t_st_exp[stu - 2])]
                    if kc == 0:
                        sw = sw + [(c_dv, t_recb_dv.get(
                            (h_, qt - 1), t_recb_dv.get((h_ - 1, NQT - 1), p3_dv0)))]
                        if qt == 0:
                            sw = sw + [(c_ld, pair_tok[hp][0]), (c_ld, pair_tok[hp][1])]
                    t_s1 = p.add(TE, lambda e, kc=kc, qt=qt, pp2=pp2, pqb=pqb, hsl=hsl:
                                 e.matmul(
                        psB[:, pp2 * 512:(pp2 + 1) * 512],
                        kt2[hsl, pqb * S + kc * 128:pqb * S + (kc + 1) * 128],
                        qt2[hsl, pqb * S + qt * 512:pqb * S + (qt + 1) * 512],
                        start=True, stop=True), sw, c_pe)
                    t_ex = p.add(SC, lambda e, pp2=pp2: e.activation(
                        pt_t[:, pp2 * 512:(pp2 + 1) * 512],
                        psB[:, pp2 * 512:(pp2 + 1) * 512], AF.Exp, scale=0.125),
                        [(c_pe, t_s1)], c_ac)
                    t_st_exp[stu] = t_ex
                    if kc > 0:
                        fw = ()
                        if kc == 1:
                            fw = [(c_ld, pair_tok[hp][2]),
                                  (c_dv, t_ao_dv.get((h_, qt - 1),
                                                     t_ao_dv.get((h_ - 1, NQT - 1), p3_dv0)))]
                            emit_pv(0, (h_ * NQT + qt) * NKC % 2, fw)
                        else:
                            emit_pv(kc - 1, ((h_ * NQT + qt) * NKC + kc - 1) % 2)
                emit_pv(NKC - 1, ((h_ * NQT + qt) * NKC + NKC - 1) % 2)
                t_lc = p.add(VE, lambda e: e.tensor_copy(lrow[64:65, :], psC[64:65, 0:512]),
                             [(c_pe, t_pv)], c_dv)
                t_ln2 = p.add(SC, lambda e: e.activation(lrow[64:65, :], lrow[64:65, :],
                                                         AF.Ln), [(c_dv, t_lc)], c_ac)
                t_rc = p.add(SC, lambda e: e.activation(lrow[64:65, :], lrow[64:65, :],
                                                        AF.Exp, scale=-1.0),
                             [(c_ac, t_ln2)], c_ac)
                t_bc = p.add(TE, lambda e: e.matmul(
                    psB[0:64, 0:512], ones_f[64:65, 0:64], lrow[64:65, :],
                    start=True, stop=True),
                    [(c_ac, t_rc),
                     (c_ac, t_st_exp[(h_ * NQT + qt) * NKC + NKC - 1])], c_pe)
                t_rb = p.add(VE, lambda e: e.tensor_copy(recb[0:64, :], psB[0:64, 0:512]),
                             [(c_pe, t_bc)], c_dv)
                t_recb_dv[(h_, qt)] = t_rb
                t_ao = p.add(VE, lambda e: e.tensor_mul(
                    ao_st[0:64, :], psC[0:64, 0:512], recb[0:64, :]),
                    [(c_dv, t_rb), (c_st, t_aost[aou - 1])], c_dv)
                t_ao_dv[(h_, qt)] = t_ao
                t_aost[aou] = p.dma(SY, aot_h[seq, h_ * 64:(h_ + 1) * 64,
                                              qt * 512:(qt + 1) * 512],
                                    ao_st[0:64, :], [(c_dv, t_ao)], c_st)
                aou += 1
            t_pair_pe_done[hp] = t_pv
        p3_last_st = t_aost[aou - 1]

        p4_dv0 = c_dv.v
        t_p4_dv = {-1: 0, -2: 0}
        t_r2tb_dv = {-1: 0}
        tok_wq3 = {}
        t_wq3_prev = 0
        t_logit_dv = {-1: p4_dv0}
        t_hb_st = {-1: 0, -2: 0}
        t_r2_prev = 0
        t_r2s_prev = 0
        t_tr4_prev = 0
        t_gmm = None
        for tc in range(STC):
            pb = tc % 2
            gtc = seq * STC + tc
            r0 = gtc * 128
            src = bass.AP(aot_h.tensor, seq * D * S + tc * 128,
                          [[S, 128], [128 * S, NDC], [1, 128]])
            dsta = aoin[:, pb * NDC * 128:(pb + 1) * NDC * 128]
            t_lao = p.dma(SY, dsta.rearrange("q (dc w) -> q dc w", dc=NDC), src,
                          [(c_st, p3_last_st), (c_dv, t_p4_dv[tc - 2])], c_ld)
            t_lz = p.dma(SY, zin[:, pb * D:(pb + 1) * D], z_h[r0:r0 + 128, :],
                         [(c_st, tok_z_st[gtc])], c_ld)
            t_wo = None
            for et in range(2):
                for dc in range(NDC):
                    t_wo = p.add(TE, lambda e, et=et, dc=dc, pb=pb: e.matmul(
                        psA[:, pb * 1024 + et * 512:pb * 1024 + (et + 1) * 512],
                        aoin[:, pb * NDC * 128 + dc * 128:pb * NDC * 128 + (dc + 1) * 128],
                        wo_w[:, dc * D + et * 512:dc * D + (et + 1) * 512],
                        start=(dc == 0), stop=(dc == NDC - 1)),
                        [(c_ld, t_lao), (c_gw, t_gw_wo), (c_dv, t_p4_dv[tc - 2]),
                         (c_dv, p4_dv0)] if (et == 0 and dc == 0) else (),
                        c_pe if (et == 1 and dc == NDC - 1) else None)
            t_h = p.add(VE, lambda e, pb=pb: e.tensor_add(
                hb[:], psA[:, pb * 1024:(pb + 1) * 1024], zin[:, pb * D:(pb + 1) * D]),
                [(c_pe, t_wo), (c_ld, t_lz), (c_st, t_hb_st[tc - 1]),
                 (c_ac, t_r2s_prev)], c_dv)
            t_p4_dv[tc] = t_h
            t_hb_st[tc] = tok_fh_st[gtc] = p.dma(SY, f_h[r0:r0 + 128, :], hb[:],
                                                 [(c_dv, t_h)], c_st)
            t_rr2, rr2 = rms_stats(hb[:], D, 72 + 4 * pb,
                                   [(c_dv, t_h), (c_ac, t_r2s_prev)], sq4)
            t_r2s = p.add(SC, lambda e, rr=rr2: e.activation(
                r2tmp[:], hb[:], AF.Identity, scale=rr),
                [(c_ac, t_rr2), (c_dv, t_r2_prev)], c_ac)
            t_r2s_prev = t_r2s
            t_r2 = p.add(VE, lambda e: e.tensor_mul(r2f[:], r2tmp[:], l2w_b[:]),
                         [(c_ac, t_r2s), (c_pe, t_tr4_prev)], c_dv)
            t_r2_prev = t_r2
            t_tr4 = None
            for dc in range(NDC):
                t_tr4 = p.add(TE, lambda e, dc=dc: e.transpose(
                    psB[:, dc * 128:(dc + 1) * 128], r2f[:, dc * 128:(dc + 1) * 128],
                    ident_f[:]),
                    [(c_dv, t_r2), (c_dv, t_recb_dv.get((NH - 1, NQT - 1), p4_dv0))]
                    if dc == 0 else (), c_pe if dc == NDC - 1 else None)
            t_tr4_prev = t_tr4
            for dc in range(NDC):
                p.add(VE, lambda e, dc=dc: e.tensor_copy(
                    r2tf[:, dc * 128:(dc + 1) * 128], psB[:, dc * 128:(dc + 1) * 128]),
                    [(c_pe, t_tr4)] if dc == 0 else ())
            t_r2b = p.add(VE, lambda e, pb=pb: e.tensor_copy(
                r2b_st[:, pb * D:(pb + 1) * D], r2f[:]),
                [(c_dv, t_r2), (c_st, tok_r2_st.get(gtc - 2, 0))], c_dv)
            t_r2tb_dv[tc] = t_r2b
            tok_r2_st[gtc] = p.dma(SY, r2tm_h[r0:r0 + 128, :],
                                   r2b_st[:, pb * D:(pb + 1) * D],
                                   [(c_dv, t_r2b)], c_st)
            for dc in range(NDC):
                t_gmm = p.add(TE, lambda e, dc=dc: e.matmul(
                    psC[:, 0:NE], r2tf[:, dc * 128:(dc + 1) * 128],
                    gate_w[:, dc * NE:(dc + 1) * NE],
                    start=(dc == 0), stop=(dc == NDC - 1)),
                    [(c_dv, t_r2b), (c_dv, t_logit_dv[tc - 1]), (c_gw, t_gw_wt)]
                    if dc == 0 else (), c_pe if dc == NDC - 1 else None)
            g0, g1, g2, g3, g4 = (stat[:, 83 + i:84 + i] for i in range(5))
            t_lg = p.add(VE, lambda e: e.tensor_copy(logit[:], psC[:, 0:NE]),
                         [(c_pe, t_gmm)], c_dv)
            t_logit_dv[tc] = t_lg
            tg = p.add(VE, lambda e: e.reduce_max(g0, logit[:], axis=AX.X, negate=True),
                       [(c_dv, t_lg)], c_dv)
            tg = p.add(VE, lambda e: e.tensor_scalar(gs1[:], logit[:], g0, None, op0=ALU.add),
                       [(c_dv, tg)], c_dv)
            tg = p.add(VE, lambda e: e.tensor_scalar(gs2[:], gs1[:], 0.0, None, op0=ALU.is_ge),
                       [(c_dv, tg)], c_dv)
            tg = p.add(VE, lambda e: e.tensor_copy(wq24[:, 8:16], gs2[:]),
                       [(c_dv, tg), (c_st, tok_wq3.get(tc - 1, 0))], c_dv)
            tg = p.add(VE, lambda e: e.scalar_tensor_tensor(gs1[:], gs2[:], -1e30, logit[:],
                                                            op0=ALU.mult, op1=ALU.add),
                       [(c_dv, tg)], c_dv)
            tg = p.add(VE, lambda e: e.reduce_max(g1, gs1[:], axis=AX.X, negate=True),
                       [(c_dv, tg)], c_dv)
            t_d12 = p.add(VE, lambda e: e.tensor_sub(g2, g1, g0), [(c_dv, tg)], c_dv)
            t_sg = p.add(SC, lambda e: e.activation(g3, g2, AF.Sigmoid),
                         [(c_dv, t_d12)], c_ac)
            tg = p.add(VE, lambda e: e.tensor_scalar(g4, g3, -1.0, 1.0, op0=ALU.mult, op1=ALU.add),
                       [(c_ac, t_sg)], c_dv)
            tg = p.add(VE, lambda e: e.tensor_scalar(gs1[:], gs1[:], g1, None, op0=ALU.add),
                       [(c_dv, tg)], c_dv)
            tg = p.add(VE, lambda e: e.tensor_scalar(gs1[:], gs1[:], 0.0, None, op0=ALU.is_ge),
                       [(c_dv, tg)], c_dv)
            tg = p.add(VE, lambda e: e.tensor_copy(wq24[:, 16:24], gs1[:]),
                       [(c_dv, tg)], c_dv)
            tg = p.add(VE, lambda e: e.tensor_scalar(gs2[:], gs2[:], g3, None, op0=ALU.mult),
                       [(c_dv, tg)], c_dv)
            t_we = p.add(VE, lambda e, gtc=gtc: e.scalar_tensor_tensor(
                we_all[:, gtc * NE:(gtc + 1) * NE], gs1[:], g4, gs2[:],
                op0=ALU.mult, op1=ALU.add), [(c_dv, tg)], c_dv)
            tok_we[gtc] = t_we
            if "we_s" in dbg:
                p.dma(SY, we_h[gtc, :, :], we_all[:, gtc * NE:(gtc + 1) * NE],
                      [(c_dv, t_we)], c_st)
            t_wc = p.add(VE, lambda e, gtc=gtc: e.tensor_copy(
                wq24[:, 0:8], we_all[:, gtc * NE:(gtc + 1) * NE]), [(c_dv, t_we)], c_dv)
            t_trq = p.add(TE, lambda e: e.transpose(psC[0:24, 128:256], wq24[:], ident_f[:]),
                          [(c_dv, t_wc), (c_dv, t_wq3_prev)], c_pe)
            t_wq3_prev = p.add(VE, lambda e: e.tensor_copy(wq24t[:], psC[0:24, 128:256]),
                               [(c_pe, t_trq)], c_dv)
            t_logit_dv[tc] = t_wq3_prev
            tok_wq3[tc] = p.dma(SY, wq3_h[:, gtc * 128:(gtc + 1) * 128], wq24t[:],
                                [(c_dv, t_wq3_prev)], c_st)
        p4_last_pe = t_gmm
        p4_last_r2st = tok_r2_st[seq * STC + STC - 1]
        p4_last_wq3 = tok_wq3[STC - 1]

    TS = 512
    NT_SP = (2 * T) // TS + NE
    STRIDE_E = D * I_DIM



    p6_w = [(c_pe, c_pe.v), (c_dv, c_dv.v), (c_ac, c_ac.v), (c_st, c_st.v),
            (c_gw, c_gw.v)]

    t_zr = p.add(VE, lambda e: e.memset(zraw[:], 0.0), p6_w, c_dv)
    t_zr_st = p.dma(SY, r2tm_h[T:T + 128, :], zraw[:], [(c_dv, t_zr)], c_st)
    t_v2a = p.add(VE, lambda e: e.memset(val2[:, 0:1], float(T)), [(c_dv, t_zr)], c_dv)
    t_v2b = p.add(VE, lambda e: e.memset(val2[:, 1:2], 0.0), [(c_dv, t_v2a)], c_dv)
    t_fil = p.dma(SY, flat0_h[0:128, :], val2[:, 0:2], [(c_dv, t_v2b)], c_st)
    n = 128
    while n < NT_SP * TS:
        m = min(n, NT_SP * TS - n)
        t_fil = p.dma(SY, flat0_h[n:n + m, :], flat0_h[0:m, :],
                      [(c_st, t_fil)], c_st)
        n += m
    t_io = p.add(GP, lambda e: e.iota(iota128[:, 0:1], [[1, 1]], channel_multiplier=1,
                                      allow_small_or_imprecise_dtypes=True), p6_w, c_gw)
    t_jr = p.add(GP, lambda e: e.iota(jrow8[:, :], [[1, NT_SP]], channel_multiplier=0,
                                      allow_small_or_imprecise_dtypes=True), (), c_gw)

    if stop_after_postproc == 1:
        p.add(SY, lambda e: e.nop(), [(c_st, c_st.v), (c_ga, c_ga.v), (c_ga1, c_ga1.v), (c_cv, c_cv.v), (c_ld, c_ld.v),
                                      (c_gw, c_gw.v), (c_dv, c_dv.v), (c_pe, c_pe.v),
                                      (c_ac, c_ac.v)])
        p.emit_all()
        return nc
    hw0 = [(c_st, p4_last_wq3), (c_st, t_zr_st)] + p6_w
    t_la = p.dma(SY, prow_a[:, :], wq3_h[0:8, :], hw0, c_ld)
    t_le1 = p.dma(SY, prow_b[:, :], wq3_h[8:16, :], (), c_ld)
    t_le2 = p.dma(SY, prow_c[:, :], wq3_h[16:24, :], (), c_ld)
    t_m = p.add(VE, lambda e: e.tensor_add(prow_c[:, :], prow_b[:, :], prow_c[:, :]),
                [(c_ld, t_le2)] + p6_w, c_dv)
    t_s = p.add(VE, lambda e: e.tensor_tensor_scan(prow_d[:, :], prow_c[:, :],
                                                   prow_c[:, :], 0.0,
                                                   op0=ALU.add, op1=ALU.bypass),
                [(c_dv, t_m)], c_dv)
    t_n = p.add(VE, lambda e: e.tensor_copy(pq[:, 0:1], prow_d[:, T - 1:T]),
                [(c_dv, t_s)], c_dv)
    t_tc = p.add(VE, lambda e: e.tensor_scalar(pq[:, 1:2], pq[:, 0:1], 0.0, None,
                                               op0=ALU.is_gt), [(c_dv, t_n)], c_dv)
    for m in range(1, T // TS):
        t_g5 = p.add(VE, lambda e, m=m: e.tensor_scalar(
            pq[:, 2:3], pq[:, 0:1], float(TS * m), None, op0=ALU.is_gt),
            [(c_dv, t_tc)], c_dv)
        t_tc = p.add(VE, lambda e: e.tensor_add(pq[:, 1:2], pq[:, 1:2], pq[:, 2:3]),
                     [(c_dv, t_g5)], c_dv)
    t_pad = p.add(VE, lambda e: e.tensor_scalar(pq[:, 4:5], pq[:, 1:2], float(TS),
                                                None, op0=ALU.mult),
                  [(c_dv, t_tc)], c_dv)
    t_fl = None
    for ee in range(8):
        t_fl = p.dma(SY, brow[0:1, ee:ee + 1], pq[ee:ee + 1, 4:5],
                     [(c_dv, t_pad)] if ee == 0 else (), c_ld)
    t_inc = p.add(VE, lambda e: e.tensor_tensor_scan(
        brow[0:1, 8:16], brow[0:1, 0:8], brow[0:1, 0:8], 0.0,
        op0=ALU.add, op1=ALU.bypass), [(c_ld, t_fl)], c_dv)
    t_bas = p.add(VE, lambda e: e.tensor_sub(brow[0:1, 16:24], brow[0:1, 8:16],
                                             brow[0:1, 0:8]), [(c_dv, t_inc)], c_dv)
    t_ct = p.add(VE, lambda e: e.tensor_scalar(brow[0:1, 24:32], brow[0:1, 8:16],
                                               1.0 / TS, None, op0=ALU.mult),
                 [(c_dv, t_bas)], c_dv)
    t_bc2 = None
    for ee in range(8):
        p.dma(SY, pq[ee:ee + 1, 5:6], brow[0:1, 16 + ee:17 + ee],
              [(c_dv, t_ct)] if ee == 0 else (), c_ld)
        t_bc2 = p.dma(SY, pq[ee:ee + 1, 6:7], brow[0:1, 24 + ee:25 + ee], (), c_ld)
    t_ge = p.add(VE, lambda e: e.tensor_scalar(jrow8[:, :], jrow8[:, :], pq[:, 6:7],
                                               None, op0=ALU.is_ge),
                 [(c_ld, t_bc2), (c_gw, t_jr)], c_dv)
    t_emm = p.add(TE, lambda e: e.matmul(psC[0:1, 256:256 + NT_SP], ones_f[0:8, 0:1],
                                         jrow8[:, :], start=True, stop=True),
                  [(c_dv, t_ge)] + p6_w, c_pe)
    t_ecp = p.add(VE, lambda e: e.tensor_copy(etab_f[:, :], psC[0:1, 256:256 + NT_SP]),
                  [(c_pe, t_emm)], c_dv)
    t_ecl = p.add(VE, lambda e: e.tensor_scalar(etab_f[:, :], etab_f[:, :],
                                                7.0, None, op0=ALU.min),
                  [(c_dv, t_ecp)], c_dv)
    t_esc = p.add(VE, lambda e: e.tensor_scalar(etab_f[:, :], etab_f[:, :],
                                                128.0, None, op0=ALU.mult),
                  [(c_dv, t_ecl)], c_dv)
    t_eb = p.add(TE, lambda e: e.matmul(psC[:, 288:288 + NT_SP], ones_f[0:1, 0:128],
                                        etab_f[0:1, :], start=True, stop=True),
                 [(c_dv, t_esc)], c_pe)
    t_ei = p.add(VE, lambda e: e.tensor_scalar(
        wcol_f[:, :], psC[:, 288:288 + NT_SP], iota128[:, 0:1], None,
        op0=ALU.add), [(c_pe, t_eb), (c_gw, t_io)], c_dv)
    t_ei = p.add(VE, lambda e: e.tensor_copy(
        wcol_i[:, :], wcol_f[:, :]), [(c_dv, t_ei)], c_dv)
    t_etab = t_ei
    if "wcol_s" in dbg:
        p.dma(SY, wcol_dbg_h[:, 0:NT_SP], wcol_i[:, :], [(c_dv, t_ei)], c_st)
        p.dma(SY, pq_dbg_h[0], pq[:, :], [(c_dv, t_ei)], c_st)
        p.dma(SY, brow_dbg_h[0], brow[:, :], [(c_dv, t_ei)], c_st)
    t_dp = p.add(VE, lambda e: e.tensor_scalar(prow_d[:, :], prow_d[:, :],
                                               pq[:, 5:6], None, op0=ALU.add),
                 [(c_dv, t_ei)], c_dv)
    tok_pos = {}
    tok_wee = {}
    t_mmP = 0
    for lst in range(2):
        rows = slice(8, 16) if lst == 0 else slice(16, 24)
        t_lb = p.dma(SY, prow_b[:, :], wq3_h[rows, :],
                     [(c_dv, t_dp)], c_ld)
        t_c1 = p.add(VE, lambda e: e.tensor_mul(prow_c[:, :], prow_a[:, :],
                                                prow_b[:, :]),
                     [(c_ld, t_lb), (c_pe, t_mmP)], c_dv)
        for seg in range(T // 512):
            t_mmW = p.add(TE, lambda e, seg=seg: e.matmul(
                psC[0:1, 0:512], ones_f[0:8, 0:1],
                prow_c[:, seg * 512:(seg + 1) * 512], start=True, stop=True),
                [(c_dv, t_c1)] if seg == 0 else [(c_dv, t_evW)], c_pe)
            t_evW = p.add(VE, lambda e: e.tensor_copy(srow6[:, :], psC[0:1, 0:512]),
                          [(c_pe, t_mmW)], c_dv)
            tok_wee[(lst, seg)] = p.dma(
                SY, stage_h[lst * 2 + 1, seg * 512:(seg + 1) * 512],
                srow6[:, :], [(c_dv, t_evW)], c_st)
        t_c2 = p.add(VE, lambda e: e.tensor_mul(prow_c[:, :], prow_d[:, :],
                                                prow_b[:, :]),
                     [(c_pe, t_mmW)], c_dv)
        for seg in range(T // 512):
            t_mmP = p.add(TE, lambda e, seg=seg: e.matmul(
                psC[0:1, 0:512], ones_f[0:8, 0:1],
                prow_c[:, seg * 512:(seg + 1) * 512], start=True, stop=True),
                [(c_dv, t_c2)] if seg == 0 else [(c_dv, t_evP)], c_pe)
            t_evP = p.add(VE, lambda e: e.tensor_scalar(
                srow6[:, :], psC[0:1, 0:512], -1.0, None, op0=ALU.add),
                [(c_pe, t_mmP)], c_dv)
            tok_pos[(lst, seg)] = p.dma(
                SY, stage_h[lst * 2, seg * 512:(seg + 1) * 512],
                srow6[:, :], [(c_dv, t_evP)], c_st)
    t_pp_pe = t_mmP
    t_sc = 0
    for lst in range(2):
        for jj in range(T // 128):
            seg = jj // 4
            t_lp = p.dma(SY, lrow[0:128, 0:1].rearrange("q w -> q w"),
                         bass.AP(stage_h.tensor,
                                 (lst * 2) * T + jj * 128,
                                 [[1, 128], [1, 1]]),
                         [(c_st, tok_pos[(lst, seg)]), (c_gw, t_sc)], c_ld)
            t_lw = p.dma(SY, lrow[0:128, 1:2],
                         bass.AP(stage_h.tensor,
                                 (lst * 2 + 1) * T + jj * 128,
                                 [[1, 128], [1, 1]]),
                         [(c_st, tok_wee[(lst, seg)])], c_ld)
            t_pi = p.add(VE, lambda e: e.tensor_copy(posi[:, 0:1], lrow[0:128, 0:1]),
                         [(c_ld, t_lw), (c_gw, t_sc)], c_dv)
            t_vv = p.add(VE, lambda e, base=float(jj * 128): e.tensor_scalar(
                val2[:, 0:1], iota128[:, 0:1], base, None, op0=ALU.add),
                [(c_dv, t_pi), (c_gw, t_io)], c_dv)
            t_vw = p.add(VE, lambda e: e.tensor_copy(val2[:, 1:2], lrow[0:128, 1:2]),
                         [(c_dv, t_vv)], c_dv)
            t_sc = p.add(GP, lambda e: e.indirect_dma_start(
                flat0_h[0:128, :], IndirectOffsetOnAxis(posi[:, 0:1], 0),
                val2[:, 0:2], None),
                [(c_dv, t_vw), (c_st, t_fil)], c_gw, 16)
    t_scat = t_sc

    if stop_after_postproc:
        p.add(SY, lambda e: e.nop(), [(c_st, c_st.v), (c_ga, c_ga.v), (c_ga1, c_ga1.v), (c_cv, c_cv.v), (c_ld, c_ld.v),
                                      (c_gw, c_gw.v), (c_dv, c_dv.v), (c_pe, c_pe.v),
                                      (c_ac, c_ac.v)])
        p.emit_all()
        return nc

    t_uv_dv = {-1: 0, -2: 0}
    t_y_dv = {-1: 0, -2: 0}
    t_add = {-1: None, -2: None}
    t_tile_add_last = {}
    t_posi_dv = {}
    t_tile_y_done = {-1: 0}
    t_tile_uv_done = {-1: 0}
    t_out_st = {-2: 0, -1: 0}
    t_p7_dv = {-2: 0, -1: 0}
    t_add_final = {}
    p7_site = {}
    uvu = 0
    yu = 0
    addu = 0
    glob_j = 0

    def emit_p7(gtc):
        r0 = gtc * 128
        pb = gtc % 2
        k7 = gtc
        t_lf = p.dma(SY, fin[:, pb * D:(pb + 1) * D], f_h[r0:r0 + 128, :],
                     [t_add_fin, (c_dv, t_p7_dv[k7 - 2])], c_ld)
        t_lx = p.dma(SY, x1in[:, pb * D:(pb + 1) * D], x1_h[r0:r0 + 128, :],
                     [(c_st, tok_x1_st[gtc])], c_ld)
        t_nmrf, rstdf, nmrf = ln_stats(fin[:, pb * D:(pb + 1) * D], D, 0 + 8 * pb,
                                       [(c_ld, t_lf), (c_ac, p7_site.get(pb, 0))], sq7)
        t_lnf = p.add(SC, lambda e, r=rstdf, m=nmrf, pb=pb: e.activation(
            oub[:, pb * D:(pb + 1) * D], fin[:, pb * D:(pb + 1) * D], AF.Identity,
            scale=r, bias=m), [(c_dv, t_nmrf), (c_st, t_out_st[k7 - 2])], c_ac)
        p7_site[pb] = t_lnf
        t_o = p.add(VE, lambda e, pb=pb: e.tensor_add(
            oub[:, pb * D:(pb + 1) * D], oub[:, pb * D:(pb + 1) * D],
            x1in[:, pb * D:(pb + 1) * D]), [(c_ac, t_lnf), (c_ld, t_lx)], c_dv)
        t_p7_dv[k7] = t_o
        t_out_st[k7] = p.dma(SY, out_h[r0:r0 + 128, :], oub[:, pb * D:(pb + 1) * D],
                             [(c_dv, t_o)], c_st)

    last_fh = tok_fh_st[T // 128 - 1]
    last_r2 = tok_r2_st[T // 128 - 1]
    for j in range(NT_SP):
        pjb = glob_j % 2
        t_pb = p.dma(SY, pairblk[:, pjb * 8:(pjb + 1) * 8].rearrange(
            "q (k c) -> q k c", k=4),
            bass.AP(flat0_h.tensor, j * TS * 2,
                    [[2, 128], [256, 4], [1, 2]]),
            [(c_gw, t_scat), (c_dv, t_posi_dv.get(glob_j - 2, 0)),
             t_tile_add_last.get(glob_j - 2)], c_ld)
        t_pi6 = None
        for k in range(4):
            t_pi6 = p.add(VE, lambda e, k=k, pjb=pjb: e.tensor_copy(
                posi6[:, pjb * 4 + k:pjb * 4 + k + 1],
                pairblk[:, pjb * 8 + 2 * k:pjb * 8 + 2 * k + 1]),
                [(c_ld, t_pb)] if k == 0 else (), c_dv if k == 3 else None)
        t_posi_dv[glob_j] = t_pi6
        t_g4 = None
        for k in range(4):
            t_g4 = p.add(GP, lambda e, k=k, pjb=pjb: e.indirect_dma_start(
                r2g[:, k * D:(k + 1) * D], None,
                r2tm_h[0:128, :],
                IndirectOffsetOnAxis(posi6[:, pjb * 4 + k:pjb * 4 + k + 1], 0)),
                [(c_dv, t_pi6), (c_st, last_r2), (c_st, t_zr_st),
                 (c_pe, t_tile_uv_done[glob_j - 1])] if k == 0 else (),
                c_gw, 16)
        t_gT = None
        t_trp = None
        for k in range(4):
            for dc in range(NDC):
                t_trp = p.add(TE, lambda e, k=k, dc=dc: e.transpose(
                    psD[:, dc * 128:(dc + 1) * 128],
                    r2g[:, k * D + dc * 128:k * D + (dc + 1) * 128], ident_bf[:]),
                    ([(c_gw, t_g4)] + ([(c_dv, t_gT)] if t_gT else []))
                    if dc == 0 else (), c_pe if dc == NDC - 1 else None)
            for dc in range(NDC):
                t_gT = p.add(VE, lambda e, k=k, dc=dc: e.tensor_copy(
                    r2gT[:, dc * 512 + k * 128:dc * 512 + (k + 1) * 128],
                    psD[:, dc * 128:(dc + 1) * 128]),
                    [(c_pe, t_trp)] if dc == 0 else (),
                    c_dv if dc == NDC - 1 else None)
        t_gT_tok = t_gT
        ew = [(c_dv, t_etab), (c_cv, t_conv_done), (c_pe, t_pp_pe),
              (c_pe, t_tile_uv_done[glob_j - 1])]
        t_wl = {}
        wc = wcol_i[:, j:j + 1]
        t_wl[(0, 1)] = p.add(GP, lambda e, wc=wc: e.indirect_dma_start(
            w1s[:, :], None, w1c_h[0:128, :], IndirectOffsetOnAxis(wc, 0)),
            ew, c_gw, 16)
        t_wl[(1, 1)] = p.add(GP, lambda e, wc=wc: e.indirect_dma_start(
            w3s[:, :], None, w3c_h[0:128, :], IndirectOffsetOnAxis(wc, 0)),
            (), c_gw, 16)
        t_wl[(2, 1)] = p.add(GP, lambda e, wc=wc: e.indirect_dma_start(
            w2s[:, :], None, w2c_h[0:128, :], IndirectOffsetOnAxis(wc, 0)),
            [(c_pe, t_tile_y_done[glob_j - 1])], c_gw, 16)
        t_g6 = None
        t_uvmm = None
        for ic in range(NIC):
            pp = uvu % 2
            ub = psB if pp == 0 else psA
            uw = ()
            if ic == 0:
                uw = [(c_gw, t_wl[(1, 1)]), (c_dv, t_gT_tok),
                      (c_dv, t_uv_dv[uvu - 2])]
            elif ic == 8:
                uw = [(c_gw, t_wl[(1, 1)]), (c_dv, t_uv_dv[uvu - 2])]
            else:
                uw = [(c_dv, t_uv_dv[uvu - 2])]
            for dc in range(NDC):
                p.add(TE, lambda e, ic=ic, dc=dc, ub=ub: e.matmul(
                    ub[:, 0:512],
                    w1s[:, dc * I_DIM + ic * 128:dc * I_DIM + (ic + 1) * 128],
                    r2gT[:, dc * 512:(dc + 1) * 512],
                    start=(dc == 0), stop=(dc == NDC - 1)),
                    uw if dc == 0 else ())
            for dc in range(NDC):
                t_uvmm = p.add(TE, lambda e, ic=ic, dc=dc, ub=ub: e.matmul(
                    ub[:, 512:1024],
                    w3s[:, dc * I_DIM + ic * 128:dc * I_DIM + (ic + 1) * 128],
                    r2gT[:, dc * 512:(dc + 1) * 512],
                    start=(dc == 0), stop=(dc == NDC - 1)),
                    (), c_pe if dc == NDC - 1 else None)
            t_si = p.add(SC, lambda e, pp=pp, ub=ub: e.activation(
                silu_t[:, pp * 512:(pp + 1) * 512], ub[:, 0:512], AF.Silu),
                [(c_pe, t_uvmm)], c_ac)
            gw_ = [(c_ac, t_si)]
            if ic == 0:
                gw_ += [(c_pe, t_tile_y_done[glob_j - 1])]
            t_g6 = p.add(VE, lambda e, pp=pp, ub=ub, ic=ic: e.tensor_mul(
                g_t[:, ic * 512:(ic + 1) * 512],
                silu_t[:, pp * 512:(pp + 1) * 512], ub[:, 512:1024]), gw_, c_dv)
            t_uv_dv[uvu] = t_g6
            uvu += 1
        t_tile_uv_done[glob_j] = t_uvmm
        t_ymm = None
        for tch in range(4):
            t_ye = None
            for ot in range(2):
                yp = yu % 2
                yw = ()
                for ic in range(NIC):
                    if ic == 0:
                        yw = [(c_dv, t_g6), (c_gw, t_wl[(2, 1)]),
                              (c_dv, t_y_dv[yu - 2])]
                    else:
                        yw = ()
                    t_ymm = p.add(TE, lambda e, ic=ic, tch=tch, ot=ot, yp=yp: e.matmul(
                        psA[:, 1024 + yp * 512:1024 + (yp + 1) * 512],
                        g_t[:, ic * 512 + tch * 128:ic * 512 + tch * 128 + 128],
                        w2s[:, ic * D + ot * 512:ic * D + (ot + 1) * 512],
                        start=(ic == 0), stop=(ic == NIC - 1)),
                        yw, c_pe if ic == NIC - 1 else None)
                t_ye = p.add(VE, lambda e, tch=tch, ot=ot, yp=yp, pjb=pjb:
                             e.tensor_scalar(
                    ysc[:, (tch % 2) * D + ot * 512:(tch % 2) * D + (ot + 1) * 512],
                    psA[:, 1024 + yp * 512:1024 + (yp + 1) * 512],
                    pairblk[:, pjb * 8 + 2 * tch + 1:pjb * 8 + 2 * tch + 2],
                    None, op0=ALU.mult),
                    [(c_pe, t_ymm), t_add[addu - 2]], c_dv)
                t_y_dv[yu] = t_ye
                yu += 1
            t_ad = p.add(GP, lambda e, tch=tch, pjb=pjb: e.indirect_dma_start(
                f_h[0:128, :],
                IndirectOffsetOnAxis(posi6[:, pjb * 4 + tch:pjb * 4 + tch + 1], 0),
                ysc[:, (tch % 2) * D:(tch % 2 + 1) * D], None,
                compute_op=ALU.add),
                [(c_dv, t_ye), t_add[addu - 1], (c_st, last_fh)],
                c_ga, 16)
            t_add[addu] = (c_ga, t_ad)
            addu += 1
        t_tile_add_last[glob_j] = t_add[addu - 1]
        t_tile_y_done[glob_j] = t_ymm
        glob_j += 1
    t_add_fin = t_add[addu - 1]

    for gtc in range(T // 128):
        emit_p7(gtc)

    p.add(SY, lambda e: e.nop(), [(c_st, c_st.v), (c_ga, c_ga.v), (c_ga1, c_ga1.v), (c_cv, c_cv.v), (c_ld, c_ld.v),
                                  (c_gw, c_gw.v)])
    p.emit_all()
    return nc

def host_inputs(L0_attn_Z, L0_attn_decay, L0_attn_W, L0_attn_b, L0_attn_gamma,
                L0_attn_beta, L0_ffn_ln1_w, L0_ffn_ln2_w, L0_ffn_wq, L0_ffn_wk,
                L0_ffn_wv, L0_ffn_wo, L0_ffn_gate_w, L0_ffn_w1, L0_ffn_w2, L0_ffn_w3,
                n_cores=8):
    f = np.float32
    x = np.ascontiguousarray(np.asarray(L0_attn_Z, f))
    B, S_, D_ = x.shape
    per = B // n_cores
    common = dict(
        decay=np.asarray(L0_attn_decay, f).reshape(1, 1),
        b_row=np.ascontiguousarray(np.asarray(L0_attn_b, f).reshape(1, D_)),
        gamma_row=np.ascontiguousarray(np.asarray(L0_attn_gamma, f).reshape(1, D_)),
        beta_row=np.ascontiguousarray(np.asarray(L0_attn_beta, f).reshape(1, D_)),
        ln1w_row=np.ascontiguousarray(np.asarray(L0_ffn_ln1_w, f).reshape(1, D_)),
        ln2w_row=np.ascontiguousarray(np.asarray(L0_ffn_ln2_w, f).reshape(1, D_)),
        wT=np.ascontiguousarray(np.asarray(L0_attn_W, f).T),
        wqT=np.ascontiguousarray(np.asarray(L0_ffn_wq, f).T),
        wkT=np.ascontiguousarray(np.asarray(L0_ffn_wk, f).T),
        wvT=np.ascontiguousarray(np.asarray(L0_ffn_wv, f).T),
        woT=np.ascontiguousarray(np.asarray(L0_ffn_wo, f).T),
        gateT=np.ascontiguousarray(np.asarray(L0_ffn_gate_w, f).T),
        w1T=np.ascontiguousarray(np.asarray(L0_ffn_w1, f).transpose(0, 2, 1)),
        w3T=np.ascontiguousarray(np.asarray(L0_ffn_w3, f).transpose(0, 2, 1)),
        w2T=np.ascontiguousarray(np.asarray(L0_ffn_w2, f).transpose(0, 2, 1)),
    )
    in_maps = [dict(common, x0=np.ascontiguousarray(
        x[c * per:(c + 1) * per].reshape(per * S_, D_))) for c in range(n_cores)]
    return in_maps, (B, S_, D_, per)


def kernel(**inputs):
    in_maps, (B, S_, D_, per) = host_inputs(**inputs)
    nc = build_program(T=per * S_, S=S_)
    res = run_bass_kernel_spmd(nc, in_maps, core_ids=list(range(8)))
    out = np.concatenate([res.results[c]["out"] for c in range(8)], axis=0)
    return out.reshape(B, S_, D_)


# version 13
# speedup vs baseline: 1.5274x; 1.0068x over previous
import numpy as np
import concourse.bass as bass
import concourse.mybir as mybir
from concourse.bass_utils import run_bass_kernel_spmd
from concourse.bass_core_rust import IndirectOffsetOnAxis

F32 = mybir.dt.float32
I32 = mybir.dt.int32
BF16 = mybir.dt.bfloat16
AF = mybir.ActivationFunctionType
ALU = mybir.AluOpType
AX = mybir.AxisListType

D = 1024
NDC = 8
NH = 16
HD = 64
I_DIM = 2048
NIC = 16
NE = 8
EPS = 1e-5


class Ctr:
    def __init__(self, nc, name):
        self.h = nc.alloc_semaphore(name)
        self.v = 0
        self.name = name


class Prog:
    ENGINES = ("sync", "gpsimd", "tensor", "vector", "scalar")

    def __init__(self, nc):
        self.nc = nc
        self.ops = {e: [] for e in self.ENGINES}

    def ctr(self, name):
        return Ctr(self.nc, name)

    def add(self, eng, emit, waits=(), ctr=None, n=1):
        self.ops[eng].append((emit, [w for w in waits if w is not None and w[1] is not None and w[1] > 0],
                              ctr, n))
        if ctr is None:
            return None
        ctr.v += n
        return ctr.v

    def dma(self, eng, out, in_, waits=(), ctr=None, **kw):
        return self.add(eng, lambda e, o=out, i=in_, kw=kw: e.dma_start(o, i, **kw),
                        waits, ctr, 16)

    def emit_all(self):
        with self.nc.Block() as block:
            for eng in self.ENGINES:
                ops = self.ops[eng]

                def runner(engobj, ops=ops):
                    last_wait = {}
                    for emit, waits, ctr, n in ops:
                        for (c, v) in waits:
                            if last_wait.get(id(c), 0) >= v:
                                continue
                            engobj.wait_ge(c.h, v)
                            last_wait[id(c)] = v
                        instr = emit(engobj)
                        if ctr is not None:
                            instr.then_inc(ctr.h, n)
                getattr(block, eng)(runner)


def build_program(T=4096, S=2048, dbg=(), stop_after_postproc=0):
    nseq = T // S
    STC = S // 128
    NQT = S // 512
    NKC = S // 128
    TH = T // 2
    NHALF = 2
    HTC = TH // 128
    NTT = TH // 512

    nc = bass.Bass("TRN2", target_bir_lowering=False)
    p = Prog(nc)

    IN = lambda n, s: nc.dram_tensor(n, s, F32, kind="ExternalInput").ap()
    x0_h = IN("x0", [T, D])
    decay_h = IN("decay", [1, 1])
    b_h = IN("b_row", [1, D]); gamma_h = IN("gamma_row", [1, D]); beta_h = IN("beta_row", [1, D])
    ln1w_h = IN("ln1w_row", [1, D]); ln2w_h = IN("ln2w_row", [1, D])
    wT_h = IN("wT", [D, D])
    wqT_h = IN("wqT", [D, D]); wkT_h = IN("wkT", [D, D])
    wvT_h = IN("wvT", [D, D]); woT_h = IN("woT", [D, D])
    gateT_h = IN("gateT", [D, NE])
    w1T_h = IN("w1T", [NE, D, I_DIM]); w3T_h = IN("w3T", [NE, D, I_DIM])
    w2T_h = IN("w2T", [NE, I_DIM, D])
    out_h = nc.dram_tensor("out", [T, D], F32, kind="ExternalOutput").ap()

    def scratch(name, shape, dt):
        kind = "ExternalOutput" if name in dbg else "Internal"
        return nc.dram_tensor(name, shape, dt, kind=kind).ap()

    x1_h = scratch("x1_s", [T, D], F32)
    z_h = scratch("z_s", [T, D], F32)
    f_h = scratch("f_s", [T + 128, D], F32)
    qt_h = scratch("qt_s", [nseq, D, S], BF16)
    kt_h = scratch("kt_s", [nseq, D, S], BF16)
    v_h = scratch("v_s", [T, D], BF16)
    aot_h = scratch("aot_s", [nseq, D, S], BF16)
    r2t_h = scratch("r2t_s", [NDC, 128, T], BF16)
    r2tm_h = scratch("r2tm_s", [T + 128, D], BF16)
    wq3_h = scratch("wq3_s", [24, T], F32)
    NT_SP = (2 * T) // 512 + NE
    flat0_h = scratch("flat0_s", [NT_SP * 512, 2], F32)
    flat_hs = (flat0_h,)
    stage_h = scratch("stage_s", [4, T], F32)
    wcol_dbg_h = nc.dram_tensor("wcol_s", [128, 32], I32,
                                kind="ExternalOutput" if "wcol_s" in dbg
                                else "Internal").ap()
    pq_dbg_h = scratch("pq_s", [2, 8, 16], F32)
    brow_dbg_h = scratch("brow_s", [2, 1, 32], F32)
    w1c_h = scratch("w1c_s", [NE * 128, NDC * I_DIM], BF16)
    w3c_h = scratch("w3c_s", [NE * 128, NDC * I_DIM], BF16)
    w2c_h = scratch("w2c_s", [NE * 128, NIC * D], BF16)
    we_h = scratch("we_s", [T // 128, 128, NE], F32)
    pow_h = scratch("pow_s", [1, 256], F32)
    yln_h = scratch("yln_s", [T, D], BF16)
    st_h = scratch("st_s", [T, D], BF16)
    yb_h = scratch("yb_s", [T, D], F32)

    sb = lambda n, s, d_: nc.alloc_sbuf_tensor(n, s, d_)
    b_b = sb("b_b", [128, D], F32)
    gam_b = sb("gam_b", [128, D], F32)
    bet_b = sb("bet_b", [128, D], F32)
    l1w_b = sb("l1w_b", [128, D], F32)
    l2w_b = sb("l2w_b", [128, D], F32)
    eps_c = sb("eps_c", [128, 1], F32)
    ones_bf = sb("ones_bf", [128, 128], BF16)
    ones_f = sb("ones_f", [128, 128], F32)
    ident_bf = sb("ident_bf", [128, 128], BF16)
    ident_f = sb("ident_f", [128, 128], F32)
    powx = sb("powx", [1, 256], F32)
    a_sc = sb("a_sc", [1, 1], F32)
    apow_b = sb("apow_b", [128, 128], F32)
    a128_c = sb("a128_c", [128, 1], F32)
    ltri = sb("ltri", [128, 128], BF16)
    gate_w = sb("gate_w", [128, NDC * NE], F32)
    we_all = sb("we_all", [128, (T // 128) * NE], F32)
    stat = sb("stat", [128, 96], F32)

    ra0, _ = nc.bump_sbuf(65536)
    AT = nc.alloc_sbuf_tensor_at
    wq_w = AT("wq_w", [128, NDC * D], BF16, offset=ra0)
    wk_w = AT("wk_w", [128, NDC * D], BF16, offset=ra0 + 16384)
    wv_w = AT("wv_w", [128, NDC * D], BF16, offset=ra0 + 32768)
    wo_w = AT("wo_w", [128, NDC * D], BF16, offset=ra0)
    wt_w = AT("wt_w", [128, NDC * D], BF16, offset=ra0 + 49152)
    w1s = AT("w1s", [128, NDC * I_DIM], BF16, offset=ra0)
    w3s = AT("w3s", [128, NDC * I_DIM], BF16, offset=ra0 + 32768)
    rb0, _ = nc.bump_sbuf(NDC * S * 2)
    r1t_s = AT("r1t_s", [128, NDC * S], BF16, offset=rb0)
    g_t = AT("g_t", [128, NIC * 512], BF16, offset=rb0)
    ysc = AT("ysc", [128, 2 * D], F32, offset=rb0 + 16384)
    r2g = AT("r2g", [128, 4 * D], BF16, offset=rb0 + 24576)
    rc0, _ = nc.bump_sbuf(53248)
    off = rc0
    def at(name, shape, dt):
        nonlocal off
        t = AT(name, shape, dt, offset=off)
        nbytes = int(np.prod(shape[1:])) * (2 if dt == BF16 else 4)
        off += (nbytes + 31) & ~31
        return t
    x0_t = at("x0_t", [128, 2 * D], F32)
    yln_t = at("yln_t", [128, D], BF16)
    st_t = at("st_t", [128, 2 * D], BF16)
    carry = at("carry", [128, 2 * NDC], F32)
    ybuf = at("ybuf", [128, D], F32)
    gbuf = at("gbuf", [128, D], F32)
    sqs = at("sqs", [128, D], F32)
    x1b = at("x1b", [128, 2 * D], F32)
    zb = at("zb", [128, 2 * D], F32)
    r1tmp = at("r1tmp", [128, D], F32)
    assert off - rc0 <= 49152, off - rc0
    off = rc0
    aoin = at("aoin", [128, 2 * NDC * 128], BF16)
    zin = at("zin", [128, 2 * D], F32)
    hb = at("hb", [128, D], F32)
    r2tmp = at("r2tmp", [128, D], F32)
    r2f = at("r2f", [128, D], F32)
    r2tf = at("r2tf", [128, NDC * 128], F32)
    r2tb = at("r2tb", [128, NDC * 256], BF16)
    logit = at("logit", [128, NE], F32)
    gs1 = at("gs1", [128, NE], F32)
    gs2 = at("gs2", [128, NE], F32)
    sq4 = at("sq4", [128, D], F32)
    wq24 = at("wq24", [128, 24], F32)
    wq24t = at("wq24t", [24, 128], F32)
    r2b_st = at("r2b_st", [128, 2 * D], BF16)
    assert off - rc0 <= 53248, off - rc0
    off = rc0
    w2s = at("w2s", [128, NIC * D], BF16)
    convst = AT("convst", [128, 2 * 1024], BF16, offset=rc0 + 49152)
    r2gT = at("r2gT", [128, NDC * 512], BF16)
    silu_t = at("silu_t", [128, 2 * 512], BF16)
    pairblk = at("pairblk", [128, 2 * 8], F32)
    posi6 = at("posi6", [128, 2 * 4], I32)
    assert off - rc0 <= 53248, off - rc0
    rd0, _ = nc.bump_sbuf(35712)
    off = rd0
    ybufB = at("ybufB", [128, D], F32)
    gbufB = at("gbufB", [128, D], F32)
    sqsB = at("sqsB", [128, D], F32)
    r1tmpB = at("r1tmpB", [128, D], F32)
    r1bB = AT("r1bB", [128, D], BF16, offset=rd0 + 24704)
    ylnB = AT("ylnB", [128, D], BF16, offset=rd0 + 26752)
    assert off - rd0 <= 35712, off - rd0
    off = rd0
    pstage = at("pstage", [128, 2 * 512], BF16)
    vstage = at("vstage", [128, 2 * 512], BF16)
    assert off - rd0 <= 35712, off - rd0
    off = rd0
    qt2 = at("qt2", [128, 2 * S], BF16)
    kt2 = at("kt2", [128, 2 * S], BF16)
    vh2 = at("vh2", [128, 2 * NKC * 130], BF16)
    mcols = at("mcols", [128, 32], F32)
    mrow_t = at("mrow_t", [16, 128], BF16)
    negm = at("negm", [1, 2048], BF16)
    pt_t = at("pt_t", [128, 2 * 512], BF16)
    recb = at("recb", [128, 512], F32)
    lrow = AT("lrow", [128, 512], F32, offset=off - 2048)
    ao_st = at("ao_st", [128, 512], BF16)
    assert off - rd0 <= 35712, off - rd0
    prow_a = AT("prow_a", [8, T], F32, offset=ra0)
    prow_b = AT("prow_b", [8, T], F32, offset=ra0 + 16384)
    prow_c = AT("prow_c", [8, T], F32, offset=ra0 + 32768)
    prow_d = AT("prow_d", [8, T], F32, offset=rb0)
    jrow8 = AT("jrow8", [8, NT_SP], F32, offset=rd0 + 28672)
    etab_f = AT("etab_f", [1, NT_SP], F32, offset=rd0 + 28768)
    iota128 = AT("iota128", [128, 1], F32, offset=rd0 + 28864)
    srow6 = AT("srow6", [1, 512], F32, offset=rd0 + 28896)
    pq = AT("pq", [8, 16], F32, offset=rd0 + 30944)
    brow = AT("brow", [1, 32], F32, offset=rd0 + 31008)
    val2 = AT("val2", [128, 2], F32, offset=rd0 + 31200)
    posi = AT("posi", [128, 1], I32, offset=rd0 + 31232)
    zraw = AT("zraw", [128, D], BF16, offset=rd0 + 31264)
    wcol_f = AT("wcol_f", [128, NT_SP], F32, offset=rd0 + 33312)
    wcol_i = AT("wcol_i", [128, NT_SP], I32, offset=rd0 + 33408)
    r1b = AT("r1b", [128, D], BF16, offset=rd0 + 33504)
    off = rd0
    fin = at("fin", [128, 2 * D], F32)
    x1in = at("x1in", [128, 2 * D], F32)
    oub = at("oub", [128, 2 * D], F32)
    sq7 = at("sq7", [128, D], F32)
    assert off - rd0 <= 35712, off - rd0

    psA = nc.alloc_psum_tensor("psA", [128, 2048], F32)
    psB = nc.alloc_psum_tensor("psB", [128, 1024], F32)
    psC = nc.alloc_psum_tensor("psC", [128, 512], F32)
    psD = nc.alloc_psum_tensor("psD", [128, 1024], BF16)

    c_ld = p.ctr("c_ld")
    c_st = p.ctr("c_st")
    c_gw = p.ctr("c_gw")
    c_ga = p.ctr("c_ga")
    c_ga1 = p.ctr("c_ga1")
    c_cv = p.ctr("c_cv")
    c_pe = p.ctr("c_pe")
    c_dv = p.ctr("c_dv")
    c_ac = p.ctr("c_ac")

    SY, GP, TE, VE, SC = "sync", "gpsimd", "tensor", "vector", "scalar"


    def ln_stats(src, n_feat, cols, waits, sq_scratch):
        sq_scratch = sq_scratch[:]
        s0, s1, mu, var, rstd, nmr = (stat[:, cols + i:cols + i + 1] for i in range(6))
        t0 = p.add(VE, lambda e, s=src, o=s0: e.reduce_sum(o, s, axis=AX.X), waits, c_dv)
        t_sq = p.add(SC, lambda e, s=src, o=sq_scratch, a=s1: e.activation(
            o, s, AF.Square, accum_out=a), waits, c_ac)
        t1 = p.add(VE, lambda e, i=s0, o=mu: e.tensor_scalar(o, i, 1.0 / n_feat, None, op0=ALU.mult),
                   [(c_ac, t_sq), (c_dv, t0)], c_dv)
        t2 = p.add(VE, lambda e, i=mu, o=nmr: e.tensor_scalar(o, i, i, None, op0=ALU.mult),
                   [(c_dv, t1)], c_dv)
        t_var = p.add(VE, lambda e, i=s1, o=var, m=nmr: e.tensor_scalar(
            o, i, 1.0 / n_feat, m, op0=ALU.mult, op1=ALU.subtract), [(c_dv, t2)], c_dv)
        t_ln = p.add(SC, lambda e, i=var, o=rstd: e.activation(o, i, AF.Ln, bias=eps_c[:, 0:1]),
                     [(c_dv, t_var)], c_ac)
        t_ex = p.add(SC, lambda e, i=rstd, o=rstd: e.activation(o, i, AF.Exp, scale=-0.5),
                     [(c_ac, t_ln)], c_ac)
        t_nmr = p.add(VE, lambda e, i=mu, r=rstd, o=nmr: e.scalar_tensor_tensor(
            o, i, -1.0, r, op0=ALU.mult, op1=ALU.mult), [(c_ac, t_ex), (c_dv, t_var)], c_dv)
        return t_nmr, rstd, nmr

    def ln_stats_gen(src, n_feat, cols, waits, sq_scratch, res):
        sq_scratch = sq_scratch[:]
        s0, s1, mu, var, rstd, nmr = (stat[:, cols + i:cols + i + 1] for i in range(6))
        t0 = p.add(VE, lambda e, s=src, o=s0: e.reduce_sum(o, s, axis=AX.X), waits, c_dv)
        yield
        t_sq = p.add(SC, lambda e, s=src, o=sq_scratch, a=s1: e.activation(
            o, s, AF.Square, accum_out=a), waits, c_ac)
        yield
        t1 = p.add(VE, lambda e, i=s0, o=mu: e.tensor_scalar(o, i, 1.0 / n_feat, None, op0=ALU.mult),
                   [(c_ac, t_sq), (c_dv, t0)], c_dv)
        yield
        t2 = p.add(VE, lambda e, i=mu, o=nmr: e.tensor_scalar(o, i, i, None, op0=ALU.mult),
                   [(c_dv, t1)], c_dv)
        yield
        t_var = p.add(VE, lambda e, i=s1, o=var, m=nmr: e.tensor_scalar(
            o, i, 1.0 / n_feat, m, op0=ALU.mult, op1=ALU.subtract), [(c_dv, t2)], c_dv)
        yield
        t_ln = p.add(SC, lambda e, i=var, o=rstd: e.activation(o, i, AF.Ln, bias=eps_c[:, 0:1]),
                     [(c_dv, t_var)], c_ac)
        yield
        t_ex = p.add(SC, lambda e, i=rstd, o=rstd: e.activation(o, i, AF.Exp, scale=-0.5),
                     [(c_ac, t_ln)], c_ac)
        yield
        t_nmr = p.add(VE, lambda e, i=mu, r=rstd, o=nmr: e.scalar_tensor_tensor(
            o, i, -1.0, r, op0=ALU.mult, op1=ALU.mult), [(c_ac, t_ex), (c_dv, t_var)], c_dv)
        res["t"], res["rstd"], res["nmr"] = t_nmr, rstd, nmr
        yield

    def rms_stats_gen(src, n_feat, cols, waits, sq_scratch, res):
        sq_scratch = sq_scratch[:]
        s1, msq, rr = (stat[:, cols + i:cols + i + 1] for i in range(3))
        t_sq = p.add(SC, lambda e, s=src, o=sq_scratch, a=s1: e.activation(
            o, s, AF.Square, accum_out=a), waits, c_ac)
        yield
        t_m = p.add(VE, lambda e, i=s1, o=msq: e.tensor_scalar(o, i, 1.0 / n_feat, None,
                                                               op0=ALU.mult),
                    [(c_ac, t_sq)], c_dv)
        yield
        t_ln = p.add(SC, lambda e, i=msq, o=rr: e.activation(o, i, AF.Ln, bias=eps_c[:, 0:1]),
                     [(c_dv, t_m)], c_ac)
        yield
        t_rr = p.add(SC, lambda e, i=rr, o=rr: e.activation(o, i, AF.Exp, scale=-0.5),
                     [(c_ac, t_ln)], c_ac)
        res["t"], res["rr"] = t_rr, rr
        yield

    def rms_stats(src, n_feat, cols, waits, sq_scratch):
        sq_scratch = sq_scratch[:]
        s1, msq, rr = (stat[:, cols + i:cols + i + 1] for i in range(3))
        t_sq = p.add(SC, lambda e, s=src, o=sq_scratch, a=s1: e.activation(
            o, s, AF.Square, accum_out=a), waits, c_ac)
        t_m = p.add(VE, lambda e, i=s1, o=msq: e.tensor_scalar(o, i, 1.0 / n_feat, None,
                                                               op0=ALU.mult),
                    [(c_ac, t_sq)], c_dv)
        t_ln = p.add(SC, lambda e, i=msq, o=rr: e.activation(o, i, AF.Ln, bias=eps_c[:, 0:1]),
                     [(c_dv, t_m)], c_ac)
        t_rr = p.add(SC, lambda e, i=rr, o=rr: e.activation(o, i, AF.Exp, scale=-0.5),
                     [(c_ac, t_ln)], c_ac)
        return t_rr, rr

    for dst, srch in ((b_b, b_h), (gam_b, gamma_h), (bet_b, beta_h),
                      (l1w_b, ln1w_h), (l2w_b, ln2w_h)):
        p.dma(SY, dst[:], bass.AP(srch.tensor, 0, [[0, 128], [1, D]]), (), c_ld)
    t_ld_a = p.dma(SY, a_sc[:], decay_h, (), c_ld)

    for dc in range(NDC):
        p.dma(GP, wt_w[:, dc * D:(dc + 1) * D], wT_h[dc * 128:(dc + 1) * 128, :], (), c_gw)
    for dc in range(NDC):
        t_gw_wt = p.dma(GP, gate_w[:, dc * NE:(dc + 1) * NE],
                        gateT_h[dc * 128:(dc + 1) * 128, :], (), c_gw)

    p.add(GP, lambda e: e.memset(ident_bf[:], 0.0))
    p.add(GP, lambda e: e.affine_select(out=ident_bf[:], in_=ident_bf[:],
                                        compare_op=ALU.not_equal, fill=1.0, base=0,
                                        pattern=[[-1, 128]], channel_multiplier=1))
    p.add(GP, lambda e: e.memset(ident_f[:], 0.0))
    p.add(GP, lambda e: e.affine_select(out=ident_f[:], in_=ident_f[:],
                                        compare_op=ALU.not_equal, fill=1.0, base=0,
                                        pattern=[[-1, 128]], channel_multiplier=1))
    p.add(GP, lambda e: e.memset(ones_bf[:], 1.0))
    for pb in range(2):
        for kc in range(NKC):
            for hoff in (64, 129):
                p.add(GP, lambda e, c=pb * (NKC * 130) + kc * 130 + hoff:
                      e.memset(vh2[:, c:c + 1], 1.0))
    t_gp_done = p.add(GP, lambda e: e.memset(ones_f[:], 1.0), (), c_gw)

    t_a = p.add(SC, lambda e: e.activation(a_sc[:], a_sc[:], AF.Sigmoid),
                [(c_ld, t_ld_a)], c_ac)
    p.add(VE, lambda e: e.memset(eps_c[:], EPS))

    p.add(VE, lambda e: e.memset(powx[:], 0.0), [(c_ac, t_a)])
    t_ms1 = p.add(VE, lambda e: e.memset(powx[0:1, 127:128], 1.0), (), c_dv)
    t_pow = t_ms1
    t_pow = None
    for k in range(1, 129):
        t_pow = p.add(VE, lambda e, k=k: e.tensor_scalar(
            powx[0:1, 127 + k:128 + k], powx[0:1, 126 + k:127 + k], a_sc[0:1, 0:1],
            None, op0=ALU.mult), [(c_dv, t_pow)], c_dv)
    t_pwst = p.dma(SY, pow_h, powx[0:1, :], [(c_dv, t_pow)], c_st)
    p.dma(SY, apow_b[:], bass.AP(pow_h.tensor, 128, [[0, 128], [1, 128]]),
          [(c_st, t_pwst)], c_ld)
    t_a128 = p.dma(SY, a128_c[:], bass.AP(pow_h.tensor, 255, [[0, 128], [1, 1]]), (), c_ld)
    t_ltri = None
    for srow in range(128):
        t_ltri = p.dma(GP, ltri[srow:srow + 1, :],
                       bass.AP(pow_h.tensor, 127 - srow, [[1, 1], [1, 128]]),
                       [(c_st, t_pwst)] if srow == 0 else (), c_gw)
    t_gw_qkv0 = None
    for w_sbuf, w_ap in ((wq_w, wqT_h), (wk_w, wkT_h), (wv_w, wvT_h)):
        for dc in range(NDC):
            t_gw_qkv0 = p.dma(GP, w_sbuf[:, dc * D:(dc + 1) * D],
                              w_ap[dc * 128:(dc + 1) * 128, :], (), c_gw)
    t_cv_in = {}
    cvu = 0
    t_cv_st = {-1: 0, -2: 0}
    for e_ in range(NE):
        for dc in range(NDC):
            for wsrc, wdst in ((w1T_h, w1c_h), (w3T_h, w3c_h)):
                for hc in range(2):
                    pbv = cvu % 2
                    t_ci = p.dma(GP, convst[:, pbv * 1024:(pbv + 1) * 1024],
                                 wsrc[e_, dc * 128:(dc + 1) * 128,
                                      hc * 1024:(hc + 1) * 1024],
                                 [(c_gw, t_gw_wt), (c_cv, t_cv_st[cvu - 2])], c_cv)
                    t_cv_st[cvu] = p.dma(GP,
                        bass.AP(wdst.tensor,
                                e_ * 128 * NDC * I_DIM + dc * I_DIM + hc * 1024,
                                [[NDC * I_DIM, 128], [1, 1024]]),
                        convst[:, pbv * 1024:(pbv + 1) * 1024],
                        [(c_cv, t_ci)], c_cv)
                    cvu += 1
        for ic in range(NIC):
            pbv = cvu % 2
            t_ci = p.dma(GP, convst[:, pbv * 1024:(pbv + 1) * 1024],
                         w2T_h[e_, ic * 128:(ic + 1) * 128, :],
                         [(c_cv, t_cv_st[cvu - 2])], c_cv)
            t_cv_st[cvu] = p.dma(GP,
                bass.AP(w2c_h.tensor, e_ * 128 * NIC * D + ic * D,
                        [[NIC * D, 128], [1, D]]),
                convst[:, pbv * 1024:(pbv + 1) * 1024],
                [(c_cv, t_ci)], c_cv)
            cvu += 1
    t_conv_done = t_cv_st[cvu - 1]
    p.conv_done_token = t_conv_done

    tok_x1_st = [0] * (T // 128)
    tok_z_st = [0] * (T // 128)
    tok_fh_st = [0] * (T // 128)
    tok_r2_st = {}
    tok_we = [0] * (T // 128)
    p4_last_pe = 0
    p4_last_r2st = 0

    for seq in range(nseq):
        w_wait = [(c_pe, p4_last_pe)] if seq > 0 else []
        if seq == 0:
            t_gw_qkv = t_gw_qkv0
        else:
            t_gw_qkv = None
            for w_sbuf, w_ap in ((wq_w, wqT_h), (wk_w, wkT_h), (wv_w, wvT_h)):
                for dc in range(NDC):
                    t_gw_qkv = p.dma(GP, w_sbuf[:, dc * D:(dc + 1) * D],
                                     w_ap[dc * 128:(dc + 1) * 128, :],
                                     w_wait if dc == 0 else (), c_gw)

        p1_w = [(c_pe, c_pe.v), (c_dv, c_dv.v), (c_ac, c_ac.v), (c_st, c_st.v)]
        t_carry0 = p.add(VE, lambda e: e.memset(carry[:, NDC:2 * NDC], 0.0),
                         [(c_ld, t_a128), (c_gw, t_ltri)], c_dv)

        dbg_tok = {}
        t_dv_evac = {-1: t_carry0}
        t_pe_scan = {-1: 0, -2: 0}
        t_dv_yb = {-1: 0}
        t_dv_x1add = {-1: 0, -2: 0}
        t_pe_r1t = {-1: 0, -2: 0}
        t_dv_r1t = {-1: 0, -2: 0}
        t_ac_r1s = {-1: 0}

        site_last = {}

        def p1_gen(tc):
            pb = tc % 2
            gtc = seq * STC + tc
            r0 = gtc * 128
            ybuf_p = ybuf if pb == 0 else ybufB
            gbuf_p = gbuf if pb == 0 else gbufB
            sqs_p = sqs if pb == 0 else sqsB
            r1tmp_p = r1tmp if pb == 0 else r1tmpB
            r1b_p = r1b if pb == 0 else r1bB
            yln_p = yln_t if pb == 0 else ylnB
            xo = x0_t[:, pb * D:(pb + 1) * D]
            t_ldx = p.dma(SY, xo, x0_h[r0:r0 + 128, :],
                          [(c_dv, t_dv_x1add[tc - 2])] + (p1_w if tc < 2 else []), c_ld)
            yield
            res = {}
            yield from ln_stats_gen(xo, D, 0 + 8 * pb,
                                    [(c_ld, t_ldx), (c_ac, site_last.get(pb, 0))] +
                                    (p1_w if tc < 2 else []), sqs_p, res)
            t_nmr, rstd, nmr = res["t"], res["rstd"], res["nmr"]
            t_yln = p.add(SC, lambda e, x=xo, r=rstd, m=nmr, o=yln_p: e.activation(
                o[:], x, AF.Identity, scale=r, bias=m),
                [(c_dv, t_nmr), (c_pe, t_pe_scan[tc - 2])], c_ac)
            site_last[pb] = t_yln
            yield
            t_scan = None
            for dc in range(NDC):
                t_scan = p.add(TE, lambda e, dc=dc, o=yln_p: e.matmul(
                    psA[:, dc * 128:(dc + 1) * 128], o[:, dc * 128:(dc + 1) * 128],
                    ltri[:], start=True, stop=True),
                    [(c_ac, t_yln), (c_dv, t_dv_evac[tc - 1]), (c_gw, t_ltri)]
                    if dc == 0 else (), c_pe if dc == NDC - 1 else None)
            t_pe_scan[tc] = t_scan
            yield
            po, pn = (1 - pb) * NDC, pb * NDC
            t_ev = None
            for dc in range(NDC):
                p.add(VE, lambda e, dc=dc, po=po, pn=pn: e.scalar_tensor_tensor(
                    carry[:, pn + dc:pn + dc + 1], a128_c[:], carry[:, po + dc:po + dc + 1],
                    psA[:, dc * 128 + 127:dc * 128 + 128], op0=ALU.mult, op1=ALU.add),
                    [(c_pe, t_scan)] if dc == 0 else ())
                t_ev = p.add(VE, lambda e, dc=dc, po=po, pb=pb: e.scalar_tensor_tensor(
                    st_t[:, pb * D + dc * 128:pb * D + (dc + 1) * 128], apow_b[:],
                    carry[:, po + dc:po + dc + 1], psA[:, dc * 128:(dc + 1) * 128],
                    op0=ALU.mult, op1=ALU.add), (), c_dv if dc == NDC - 1 else None)
                if dc == 3:
                    yield
            t_dv_evac[tc] = t_ev
            yield
            t_y = None
            for et in range(2):
                for dc in range(NDC):
                    t_y = p.add(TE, lambda e, et=et, dc=dc, pb=pb: e.matmul(
                        psA[:, 1024 + et * 512:1024 + (et + 1) * 512],
                        st_t[:, pb * D + dc * 128:pb * D + (dc + 1) * 128],
                        wt_w[:, dc * D + et * 512:dc * D + (et + 1) * 512],
                        start=(dc == 0), stop=(dc == NDC - 1)),
                        [(c_dv, t_ev), (c_dv, t_dv_yb[tc - 1]), (c_gw, t_gw_wt)]
                        if (et == 0 and dc == 0) else (),
                        c_pe if (et == 1 and dc == NDC - 1) else None)
            yield
            t_yb = p.add(VE, lambda e, o=ybuf_p: e.tensor_add(o[:], psA[:, 1024:2048], b_b[:]),
                         [(c_pe, t_y)], c_dv)
            t_dv_yb[tc] = t_yb
            yield
            res1 = {}
            yield from ln_stats_gen(ybuf_p[:], D, 16 + 8 * pb,
                                    [(c_dv, t_yb), (c_ac, site_last.get(16 + pb, 0))],
                                    sqs_p, res1)
            t_l1 = p.add(SC, lambda e, r=res1["rstd"], m=res1["nmr"], i=ybuf_p, o=sqs_p:
                         e.activation(o[:], i[:], AF.Identity, scale=r, bias=m),
                         [(c_dv, res1["t"])], c_ac)
            site_last[16 + pb] = t_l1
            yield
            t_g1 = p.add(VE, lambda e, i=sqs_p, o=gbuf_p: e.tensor_mul(o[:], i[:], gam_b[:]),
                         [(c_ac, t_l1)], c_dv)
            yield
            t_g2 = p.add(VE, lambda e, o=gbuf_p: e.tensor_add(o[:], o[:], bet_b[:]),
                         [(c_dv, t_g1)], c_dv)
            yield
            res2 = {}
            yield from ln_stats_gen(gbuf_p[:], D, 32 + 8 * pb,
                                    [(c_dv, t_g2), (c_ac, site_last.get(32 + pb, 0))],
                                    sqs_p, res2)
            t_l2 = p.add(SC, lambda e, r=res2["rstd"], m=res2["nmr"], i=gbuf_p, o=ybuf_p:
                         e.activation(o[:], i[:], AF.Identity, scale=r, bias=m),
                         [(c_dv, res2["t"])], c_ac)
            site_last[32 + pb] = t_l2
            yield
            t_x1a = p.add(VE, lambda e, x=xo, pb=pb, i=ybuf_p: e.tensor_add(
                x1b[:, pb * D:(pb + 1) * D], x, i[:]),
                [(c_ac, t_l2), (c_st, tok_x1_st[gtc - 2] if gtc >= 2 else 0)], c_dv)
            t_dv_x1add[tc] = t_x1a
            tok_x1_st[gtc] = p.dma(SY, x1_h[r0:r0 + 128, :], x1b[:, pb * D:(pb + 1) * D],
                                   [(c_dv, t_x1a)], c_st)
            yield
            resz = {}
            yield from ln_stats_gen(x1b[:, pb * D:(pb + 1) * D], D, 48 + 8 * pb,
                                    [(c_dv, t_x1a), (c_ac, site_last.get(48 + pb, 0))],
                                    sqs_p, resz)
            t_z = p.add(SC, lambda e, r=resz["rstd"], m=resz["nmr"], pb=pb: e.activation(
                zb[:, pb * D:(pb + 1) * D], x1b[:, pb * D:(pb + 1) * D], AF.Identity,
                scale=r, bias=m),
                [(c_dv, resz["t"]), (c_st, tok_z_st[gtc - 2] if gtc >= 2 else 0)], c_ac)
            site_last[48 + pb] = t_z
            tok_z_st[gtc] = p.dma(SY, z_h[r0:r0 + 128, :], zb[:, pb * D:(pb + 1) * D],
                                  [(c_ac, t_z)], c_st)
            yield
            resr = {}
            yield from rms_stats_gen(zb[:, pb * D:(pb + 1) * D], D, 64 + 4 * pb,
                                     [(c_ac, t_z), (c_ac, site_last.get(64 + pb, 0))],
                                     sqs_p, resr)
            t_r1s = p.add(SC, lambda e, rr=resr["rr"], pb=pb, o=r1tmp_p: e.activation(
                o[:], zb[:, pb * D:(pb + 1) * D], AF.Identity, scale=rr),
                [(c_ac, resr["t"]), (c_dv, t_dv_r1t[tc - 2])], c_ac)
            site_last[64 + pb] = t_r1s
            yield
            t_r1 = p.add(VE, lambda e, i=r1tmp_p, o=r1b_p: e.tensor_mul(o[:], i[:], l1w_b[:]),
                         [(c_ac, t_r1s), (c_pe, t_pe_r1t[tc - 2])], c_dv)
            yield
            t_tr = None
            for dc in range(NDC):
                t_tr = p.add(TE, lambda e, dc=dc, i=r1b_p: e.transpose(
                    psD[:, dc * 128:(dc + 1) * 128], i[:, dc * 128:(dc + 1) * 128],
                    ident_bf[:]),
                    [(c_dv, t_r1), (c_dv, t_dv_r1t[tc - 1]), (c_gw, t_gp_done)]
                    if dc == 0 else (), c_pe if dc == NDC - 1 else None)
            t_pe_r1t[tc] = t_tr
            yield
            t_r1t = None
            for dc in range(NDC):
                t_r1t = p.add(VE, lambda e, dc=dc, tc=tc: e.tensor_copy(
                    r1t_s[:, dc * S + tc * 128:dc * S + (tc + 1) * 128],
                    psD[:, dc * 128:(dc + 1) * 128]),
                    [(c_pe, t_tr)] if dc == 0 else (), c_dv if dc == NDC - 1 else None)
            t_dv_r1t[tc] = t_r1t
            yield

        LAG = 8
        act_g = []
        next_tc = 0
        steps_newest = 0
        while act_g or next_tc < STC:
            if next_tc < STC and (not act_g or (len(act_g) < 2 and steps_newest >= LAG)):
                act_g.append(p1_gen(next_tc))
                next_tc += 1
                steps_newest = 0
            for g in list(act_g):
                try:
                    next(g)
                    if g is act_g[-1]:
                        steps_newest += 1
                except StopIteration:
                    was_newest = g is act_g[-1]
                    act_g.remove(g)
                    if not was_newest:
                        pass

        p1_last_dv = t_dv_r1t[STC - 1]

        t_p2_dv = {-1: 0, -2: 0}
        t_p2_st = {-1: 0, -2: 0}
        unit = 0
        t_mm = None
        for w_sbuf, dst_h in ((wq_w, qt_h), (wk_w, kt_h)):
            for ec in range(NDC):
                for tt4 in range(NQT):
                    pp = unit % 2
                    for dc in range(NDC):
                        t_mm = p.add(TE, lambda e, dc=dc, ec=ec, tt4=tt4, pp=pp, w=w_sbuf: e.matmul(
                            psB[:, pp * 512:(pp + 1) * 512],
                            w[:, dc * D + ec * 128:dc * D + ec * 128 + 128],
                            r1t_s[:, dc * S + tt4 * 512:dc * S + (tt4 + 1) * 512],
                            start=(dc == 0), stop=(dc == NDC - 1)),
                            [(c_dv, p1_last_dv), (c_gw, t_gw_qkv),
                             (c_dv, t_p2_dv[unit - 2])] if dc == 0 else (),
                            c_pe if dc == NDC - 1 else None)
                    t_ev2 = p.add(VE, lambda e, pp=pp: e.tensor_copy(
                        pstage[:, pp * 512:(pp + 1) * 512], psB[:, pp * 512:(pp + 1) * 512]),
                        [(c_pe, t_mm), (c_st, t_p2_st[unit - 2])], c_dv)
                    t_p2_dv[unit] = t_ev2
                    t_p2_st[unit] = p.dma(
                        SY, dst_h[seq, ec * 128:(ec + 1) * 128, tt4 * 512:(tt4 + 1) * 512],
                        pstage[:, pp * 512:(pp + 1) * 512], [(c_dv, t_ev2)], c_st)
                    unit += 1
        last_qk_st = t_p2_st[unit - 1]
        vunit = 0
        t_v_dv = {-1: 0, -2: 0}
        t_v_st = {-1: 0, -2: 0}
        for tcq in range(STC):
            for et in range(2):
                pp = vunit % 2
                for dc in range(NDC):
                    t_mm = p.add(TE, lambda e, dc=dc, tcq=tcq, et=et, pp=pp: e.matmul(
                        psA[:, pp * 512:(pp + 1) * 512],
                        r1t_s[:, dc * S + tcq * 128:dc * S + (tcq + 1) * 128],
                        wv_w[:, dc * D + et * 512:dc * D + (et + 1) * 512],
                        start=(dc == 0), stop=(dc == NDC - 1)),
                        [(c_dv, t_v_dv[vunit - 2]), (c_dv, p1_last_dv)] if dc == 0 else (),
                        c_pe if dc == NDC - 1 else None)
                t_ev3 = p.add(VE, lambda e, pp=pp: e.tensor_copy(
                    vstage[:, pp * 512:(pp + 1) * 512], psA[:, pp * 512:(pp + 1) * 512]),
                    [(c_pe, t_mm), (c_st, t_v_st[vunit - 2])], c_dv)
                t_v_dv[vunit] = t_ev3
                t_v_st[vunit] = p.dma(
                    SY, v_h[seq * S + tcq * 128:seq * S + (tcq + 1) * 128,
                            et * 512:(et + 1) * 512],
                    vstage[:, pp * 512:(pp + 1) * 512], [(c_dv, t_ev3)], c_st)
                vunit += 1
        p2_last_pe = t_mm
        last_v_st = t_v_st[vunit - 1]
        t_gw_wo = None
        for dc in range(NDC):
            t_gw_wo = p.dma(GP, wo_w[:, dc * D:(dc + 1) * D],
                            woT_h[dc * 128:(dc + 1) * 128, :],
                            [(c_pe, p2_last_pe)] if dc == 0 else (), c_gw)

        p3_dv0 = c_dv.v
        t_pair_pe_done = {-1: 0, -2: 0}
        t_ao_dv = {}
        t_st_exp = {-1: 0, -2: 0}
        t_aost = {-1: 0}
        pair_tok = {}
        pv_tok = {}
        aou = 0
        t_pv = None
        t_bc_prev = 0
        t_rb_prev = 0
        t_rc_by = {}

        def emit_pair_loads(hp):
            pqb = hp % 2
            wv_ = [(c_pe, t_pair_pe_done[hp - 2]), (c_st, last_qk_st), (c_st, last_v_st)]
            t_lq = p.dma(SY, qt2[:, pqb * S:(pqb + 1) * S],
                         qt_h[seq, hp * 128:(hp + 1) * 128, :], wv_, c_ld)
            t_lk = p.dma(SY, kt2[:, pqb * S:(pqb + 1) * S],
                         kt_h[seq, hp * 128:(hp + 1) * 128, :], (), c_ld)
            t_lv = None
            for sub in range(2):
                h2 = 2 * hp + sub
                hoff = 0 if sub == 0 else 65
                srcv = bass.AP(v_h.tensor, (seq * S) * D + h2 * HD,
                               [[D, 128], [128 * D, NKC], [1, HD]])
                dstv = vh2[:, pqb * NKC * 130:(pqb + 1) * NKC * 130].rearrange(
                    "q (kc w) -> q kc w", kc=NKC)[:, :, hoff:hoff + HD]
                t_lv = p.dma(SY, dstv, srcv, (), c_ld)
            pair_tok[hp] = (t_lq, t_lk, t_lv)

        def lchain_pre(iq):
            nonlocal t_bc_prev
            bank = iq % 2
            t_lc = p.add(VE, lambda e, bank=bank: e.tensor_copy(
                lrow[64:65, :], psA[64:65, bank * 512:(bank + 1) * 512]),
                [(c_pe, pv_tok[iq]), (c_pe, t_bc_prev)], c_dv)
            t_ln2 = p.add(SC, lambda e: e.activation(lrow[64:65, :], lrow[64:65, :],
                                                     AF.Ln), [(c_dv, t_lc)], c_ac)
            t_rc_by[iq] = p.add(SC, lambda e: e.activation(
                lrow[64:65, :], lrow[64:65, :], AF.Exp, scale=-1.0),
                [(c_ac, t_ln2)], c_ac)

        def lchain_post(iq, h_p, qt_p):
            nonlocal t_bc_prev, t_rb_prev, aou
            bank = iq % 2
            t_bc = p.add(TE, lambda e: e.matmul(
                psA[0:64, 1024:1536], ones_f[64:65, 0:64], lrow[64:65, :],
                start=True, stop=True),
                [(c_ac, t_rc_by[iq]), (c_dv, t_rb_prev)], c_pe)
            t_bc_prev = t_bc
            t_rb = p.add(VE, lambda e: e.tensor_copy(recb[0:64, :],
                                                     psA[0:64, 1024:1536]),
                         [(c_pe, t_bc)], c_dv)
            t_rb_prev = t_rb
            t_ao = p.add(VE, lambda e, bank=bank: e.tensor_mul(
                ao_st[0:64, :], psA[0:64, bank * 512:(bank + 1) * 512],
                recb[0:64, :]),
                [(c_dv, t_rb), (c_st, t_aost[aou - 1])], c_dv)
            t_ao_dv[iq] = t_ao
            t_aost[aou] = p.dma(SY, aot_h[seq, h_p * 64:(h_p + 1) * 64,
                                          qt_p * 512:(qt_p + 1) * 512],
                                ao_st[0:64, :], [(c_dv, t_ao)], c_st)
            aou += 1

        emit_pair_loads(0)
        for h_ in range(NH):
            hp = h_ // 2
            pqb = hp % 2
            hsl = slice(0, 64) if h_ % 2 == 0 else slice(64, 128)
            hoff = 0 if h_ % 2 == 0 else 65
            if h_ % 2 == 0 and hp + 1 < NH // 2:
                emit_pair_loads(hp + 1)
            for qt in range(NQT):
                iq = h_ * NQT + qt
                bank = iq % 2

                def emit_pv(kc, pp2, first_w=()):
                    nonlocal t_pv
                    pvw = [(c_ac, t_st_exp[iq * NKC + kc])] + list(first_w)
                    t_pv = p.add(TE, lambda e, kc=kc, pqb=pqb, hoff=hoff, pp2=pp2,
                                 bank=bank: e.matmul(
                        psA[0:65, bank * 512:(bank + 1) * 512],
                        vh2[:, pqb * NKC * 130 + kc * 130 + hoff:
                            pqb * NKC * 130 + kc * 130 + hoff + 65],
                        pt_t[:, pp2 * 512:(pp2 + 1) * 512],
                        start=(kc == 0), stop=(kc == NKC - 1)),
                        pvw, c_pe if kc == NKC - 1 else None)

                for kc in range(NKC):
                    stu = iq * NKC + kc
                    pp2 = stu % 2
                    sw = [(c_ac, t_st_exp[stu - 2])]
                    if kc == 0 and qt == 0:
                        sw = sw + [(c_ld, pair_tok[hp][0]), (c_ld, pair_tok[hp][1])]
                    t_s1 = p.add(TE, lambda e, kc=kc, qt=qt, pp2=pp2, pqb=pqb, hsl=hsl:
                                 e.matmul(
                        psB[:, pp2 * 512:(pp2 + 1) * 512],
                        kt2[hsl, pqb * S + kc * 128:pqb * S + (kc + 1) * 128],
                        qt2[hsl, pqb * S + qt * 512:pqb * S + (qt + 1) * 512],
                        start=True, stop=True), sw, c_pe)
                    t_ex = p.add(SC, lambda e, pp2=pp2: e.activation(
                        pt_t[:, pp2 * 512:(pp2 + 1) * 512],
                        psB[:, pp2 * 512:(pp2 + 1) * 512], AF.Exp, scale=0.125),
                        [(c_pe, t_s1)], c_ac)
                    t_st_exp[stu] = t_ex
                    if kc == 0 and iq > 0:
                        lchain_pre(iq - 1)
                    if kc == 3 and iq > 0:
                        lchain_post(iq - 1, (iq - 1) // NQT, (iq - 1) % NQT)
                    if kc > 0:
                        fw = ()
                        if kc == 1:
                            fw = [(c_ld, pair_tok[hp][2]),
                                  (c_dv, t_ao_dv.get(iq - 2, p3_dv0))]
                            emit_pv(0, iq * NKC % 2, fw)
                        else:
                            emit_pv(kc - 1, (iq * NKC + kc - 1) % 2)
                emit_pv(NKC - 1, (iq * NKC + NKC - 1) % 2)
                pv_tok[iq] = t_pv
            t_pair_pe_done[hp] = t_pv
        iq_last = NH * NQT - 1
        lchain_pre(iq_last)
        lchain_post(iq_last, NH - 1, NQT - 1)
        p3_last_st = t_aost[aou - 1]

        p4_dv0 = c_dv.v
        p4_ac0 = c_ac.v
        t_p4_dv = {-1: 0, -2: 0}
        t_r2tb_dv = {-1: 0}
        tok_wq3 = {}
        t_wq3_prev = 0
        t_logit_dv = {-1: p4_dv0}
        t_hb_st = {-1: 0, -2: 0}
        t_r2_prev = 0
        t_r2s_prev = 0
        t_tr4_prev = 0
        t_gmm = None
        for tc in range(STC):
            pb = tc % 2
            gtc = seq * STC + tc
            r0 = gtc * 128
            src = bass.AP(aot_h.tensor, seq * D * S + tc * 128,
                          [[S, 128], [128 * S, NDC], [1, 128]])
            dsta = aoin[:, pb * NDC * 128:(pb + 1) * NDC * 128]
            t_lao = p.dma(SY, dsta.rearrange("q (dc w) -> q dc w", dc=NDC), src,
                          [(c_st, p3_last_st), (c_dv, t_p4_dv[tc - 2])], c_ld)
            t_lz = p.dma(SY, zin[:, pb * D:(pb + 1) * D], z_h[r0:r0 + 128, :],
                         [(c_st, tok_z_st[gtc])], c_ld)
            t_wo = None
            for et in range(2):
                for dc in range(NDC):
                    t_wo = p.add(TE, lambda e, et=et, dc=dc, pb=pb: e.matmul(
                        psA[:, pb * 1024 + et * 512:pb * 1024 + (et + 1) * 512],
                        aoin[:, pb * NDC * 128 + dc * 128:pb * NDC * 128 + (dc + 1) * 128],
                        wo_w[:, dc * D + et * 512:dc * D + (et + 1) * 512],
                        start=(dc == 0), stop=(dc == NDC - 1)),
                        [(c_ld, t_lao), (c_gw, t_gw_wo), (c_dv, t_p4_dv[tc - 2]),
                         (c_dv, p4_dv0)] if (et == 0 and dc == 0) else (),
                        c_pe if (et == 1 and dc == NDC - 1) else None)
            t_h = p.add(VE, lambda e, pb=pb: e.tensor_add(
                hb[:], psA[:, pb * 1024:(pb + 1) * 1024], zin[:, pb * D:(pb + 1) * D]),
                [(c_pe, t_wo), (c_ld, t_lz), (c_st, t_hb_st[tc - 1]),
                 (c_ac, t_r2s_prev)], c_dv)
            t_p4_dv[tc] = t_h
            t_hb_st[tc] = tok_fh_st[gtc] = p.dma(SY, f_h[r0:r0 + 128, :], hb[:],
                                                 [(c_dv, t_h)], c_st)
            t_rr2, rr2 = rms_stats(hb[:], D, 72 + 4 * pb,
                                   [(c_dv, t_h), (c_ac, t_r2s_prev)], sq4)
            t_r2s = p.add(SC, lambda e, rr=rr2: e.activation(
                r2tmp[:], hb[:], AF.Identity, scale=rr),
                [(c_ac, t_rr2), (c_dv, t_r2_prev)], c_ac)
            t_r2s_prev = t_r2s
            t_r2 = p.add(VE, lambda e: e.tensor_mul(r2f[:], r2tmp[:], l2w_b[:]),
                         [(c_ac, t_r2s), (c_pe, t_tr4_prev)], c_dv)
            t_r2_prev = t_r2
            t_tr4 = None
            for dc in range(NDC):
                t_tr4 = p.add(TE, lambda e, dc=dc: e.transpose(
                    psB[:, dc * 128:(dc + 1) * 128], r2f[:, dc * 128:(dc + 1) * 128],
                    ident_f[:]),
                    [(c_dv, t_r2), (c_ac, p4_ac0)]
                    if dc == 0 else (), c_pe if dc == NDC - 1 else None)
            t_tr4_prev = t_tr4
            for dc in range(NDC):
                p.add(VE, lambda e, dc=dc: e.tensor_copy(
                    r2tf[:, dc * 128:(dc + 1) * 128], psB[:, dc * 128:(dc + 1) * 128]),
                    [(c_pe, t_tr4)] if dc == 0 else ())
            t_r2b = p.add(VE, lambda e, pb=pb: e.tensor_copy(
                r2b_st[:, pb * D:(pb + 1) * D], r2f[:]),
                [(c_dv, t_r2), (c_st, tok_r2_st.get(gtc - 2, 0))], c_dv)
            t_r2tb_dv[tc] = t_r2b
            tok_r2_st[gtc] = p.dma(SY, r2tm_h[r0:r0 + 128, :],
                                   r2b_st[:, pb * D:(pb + 1) * D],
                                   [(c_dv, t_r2b)], c_st)
            for dc in range(NDC):
                t_gmm = p.add(TE, lambda e, dc=dc: e.matmul(
                    psC[:, 0:NE], r2tf[:, dc * 128:(dc + 1) * 128],
                    gate_w[:, dc * NE:(dc + 1) * NE],
                    start=(dc == 0), stop=(dc == NDC - 1)),
                    [(c_dv, t_r2b), (c_dv, t_logit_dv[tc - 1]), (c_gw, t_gw_wt)]
                    if dc == 0 else (), c_pe if dc == NDC - 1 else None)
            g0, g1, g2, g3, g4 = (stat[:, 83 + i:84 + i] for i in range(5))
            t_lg = p.add(VE, lambda e: e.tensor_copy(logit[:], psC[:, 0:NE]),
                         [(c_pe, t_gmm)], c_dv)
            t_logit_dv[tc] = t_lg
            tg = p.add(VE, lambda e: e.reduce_max(g0, logit[:], axis=AX.X, negate=True),
                       [(c_dv, t_lg)], c_dv)
            tg = p.add(VE, lambda e: e.tensor_scalar(gs1[:], logit[:], g0, None, op0=ALU.add),
                       [(c_dv, tg)], c_dv)
            tg = p.add(VE, lambda e: e.tensor_scalar(gs2[:], gs1[:], 0.0, None, op0=ALU.is_ge),
                       [(c_dv, tg)], c_dv)
            tg = p.add(VE, lambda e: e.tensor_copy(wq24[:, 8:16], gs2[:]),
                       [(c_dv, tg), (c_st, tok_wq3.get(tc - 1, 0))], c_dv)
            tg = p.add(VE, lambda e: e.scalar_tensor_tensor(gs1[:], gs2[:], -1e30, logit[:],
                                                            op0=ALU.mult, op1=ALU.add),
                       [(c_dv, tg)], c_dv)
            tg = p.add(VE, lambda e: e.reduce_max(g1, gs1[:], axis=AX.X, negate=True),
                       [(c_dv, tg)], c_dv)
            t_d12 = p.add(VE, lambda e: e.tensor_sub(g2, g1, g0), [(c_dv, tg)], c_dv)
            t_sg = p.add(SC, lambda e: e.activation(g3, g2, AF.Sigmoid),
                         [(c_dv, t_d12)], c_ac)
            tg = p.add(VE, lambda e: e.tensor_scalar(g4, g3, -1.0, 1.0, op0=ALU.mult, op1=ALU.add),
                       [(c_ac, t_sg)], c_dv)
            tg = p.add(VE, lambda e: e.tensor_scalar(gs1[:], gs1[:], g1, None, op0=ALU.add),
                       [(c_dv, tg)], c_dv)
            tg = p.add(VE, lambda e: e.tensor_scalar(gs1[:], gs1[:], 0.0, None, op0=ALU.is_ge),
                       [(c_dv, tg)], c_dv)
            tg = p.add(VE, lambda e: e.tensor_copy(wq24[:, 16:24], gs1[:]),
                       [(c_dv, tg)], c_dv)
            tg = p.add(VE, lambda e: e.tensor_scalar(gs2[:], gs2[:], g3, None, op0=ALU.mult),
                       [(c_dv, tg)], c_dv)
            t_we = p.add(VE, lambda e, gtc=gtc: e.scalar_tensor_tensor(
                we_all[:, gtc * NE:(gtc + 1) * NE], gs1[:], g4, gs2[:],
                op0=ALU.mult, op1=ALU.add), [(c_dv, tg)], c_dv)
            tok_we[gtc] = t_we
            if "we_s" in dbg:
                p.dma(SY, we_h[gtc, :, :], we_all[:, gtc * NE:(gtc + 1) * NE],
                      [(c_dv, t_we)], c_st)
            t_wc = p.add(VE, lambda e, gtc=gtc: e.tensor_copy(
                wq24[:, 0:8], we_all[:, gtc * NE:(gtc + 1) * NE]), [(c_dv, t_we)], c_dv)
            t_trq = p.add(TE, lambda e: e.transpose(psC[0:24, 128:256], wq24[:], ident_f[:]),
                          [(c_dv, t_wc), (c_dv, t_wq3_prev)], c_pe)
            t_wq3_prev = p.add(VE, lambda e: e.tensor_copy(wq24t[:], psC[0:24, 128:256]),
                               [(c_pe, t_trq)], c_dv)
            t_logit_dv[tc] = t_wq3_prev
            tok_wq3[tc] = p.dma(SY, wq3_h[:, gtc * 128:(gtc + 1) * 128], wq24t[:],
                                [(c_dv, t_wq3_prev)], c_st)
        p4_last_pe = t_gmm
        p4_last_r2st = tok_r2_st[seq * STC + STC - 1]
        p4_last_wq3 = tok_wq3[STC - 1]

    TS = 512
    NT_SP = (2 * T) // TS + NE
    STRIDE_E = D * I_DIM



    p6_w = [(c_pe, c_pe.v), (c_dv, c_dv.v), (c_ac, c_ac.v), (c_st, c_st.v),
            (c_gw, c_gw.v)]

    t_zr = p.add(VE, lambda e: e.memset(zraw[:], 0.0), p6_w, c_dv)
    t_zr_st = p.dma(SY, r2tm_h[T:T + 128, :], zraw[:], [(c_dv, t_zr)], c_st)
    t_v2a = p.add(VE, lambda e: e.memset(val2[:, 0:1], float(T)), [(c_dv, t_zr)], c_dv)
    t_v2b = p.add(VE, lambda e: e.memset(val2[:, 1:2], 0.0), [(c_dv, t_v2a)], c_dv)
    t_fil = p.dma(SY, flat0_h[0:128, :], val2[:, 0:2], [(c_dv, t_v2b)], c_st)
    n = 128
    while n < NT_SP * TS:
        m = min(n, NT_SP * TS - n)
        t_fil = p.dma(SY, flat0_h[n:n + m, :], flat0_h[0:m, :],
                      [(c_st, t_fil)], c_st)
        n += m
    t_io = p.add(GP, lambda e: e.iota(iota128[:, 0:1], [[1, 1]], channel_multiplier=1,
                                      allow_small_or_imprecise_dtypes=True), p6_w, c_gw)
    t_jr = p.add(GP, lambda e: e.iota(jrow8[:, :], [[1, NT_SP]], channel_multiplier=0,
                                      allow_small_or_imprecise_dtypes=True), (), c_gw)

    if stop_after_postproc == 1:
        p.add(SY, lambda e: e.nop(), [(c_st, c_st.v), (c_ga, c_ga.v), (c_ga1, c_ga1.v), (c_cv, c_cv.v), (c_ld, c_ld.v),
                                      (c_gw, c_gw.v), (c_dv, c_dv.v), (c_pe, c_pe.v),
                                      (c_ac, c_ac.v)])
        p.emit_all()
        return nc
    hw0 = [(c_st, p4_last_wq3), (c_st, t_zr_st)] + p6_w
    t_la = p.dma(SY, prow_a[:, :], wq3_h[0:8, :], hw0, c_ld)
    t_le1 = p.dma(SY, prow_b[:, :], wq3_h[8:16, :], (), c_ld)
    t_le2 = p.dma(SY, prow_c[:, :], wq3_h[16:24, :], (), c_ld)
    t_m = p.add(VE, lambda e: e.tensor_add(prow_c[:, :], prow_b[:, :], prow_c[:, :]),
                [(c_ld, t_le2)] + p6_w, c_dv)
    t_s = p.add(VE, lambda e: e.tensor_tensor_scan(prow_d[:, :], prow_c[:, :],
                                                   prow_c[:, :], 0.0,
                                                   op0=ALU.add, op1=ALU.bypass),
                [(c_dv, t_m)], c_dv)
    t_n = p.add(VE, lambda e: e.tensor_copy(pq[:, 0:1], prow_d[:, T - 1:T]),
                [(c_dv, t_s)], c_dv)
    t_tc = p.add(VE, lambda e: e.tensor_scalar(pq[:, 1:2], pq[:, 0:1], 0.0, None,
                                               op0=ALU.is_gt), [(c_dv, t_n)], c_dv)
    for m in range(1, T // TS):
        t_g5 = p.add(VE, lambda e, m=m: e.tensor_scalar(
            pq[:, 2:3], pq[:, 0:1], float(TS * m), None, op0=ALU.is_gt),
            [(c_dv, t_tc)], c_dv)
        t_tc = p.add(VE, lambda e: e.tensor_add(pq[:, 1:2], pq[:, 1:2], pq[:, 2:3]),
                     [(c_dv, t_g5)], c_dv)
    t_pad = p.add(VE, lambda e: e.tensor_scalar(pq[:, 4:5], pq[:, 1:2], float(TS),
                                                None, op0=ALU.mult),
                  [(c_dv, t_tc)], c_dv)
    t_fl = None
    for ee in range(8):
        t_fl = p.dma(SY, brow[0:1, ee:ee + 1], pq[ee:ee + 1, 4:5],
                     [(c_dv, t_pad)] if ee == 0 else (), c_ld)
    t_inc = p.add(VE, lambda e: e.tensor_tensor_scan(
        brow[0:1, 8:16], brow[0:1, 0:8], brow[0:1, 0:8], 0.0,
        op0=ALU.add, op1=ALU.bypass), [(c_ld, t_fl)], c_dv)
    t_bas = p.add(VE, lambda e: e.tensor_sub(brow[0:1, 16:24], brow[0:1, 8:16],
                                             brow[0:1, 0:8]), [(c_dv, t_inc)], c_dv)
    t_ct = p.add(VE, lambda e: e.tensor_scalar(brow[0:1, 24:32], brow[0:1, 8:16],
                                               1.0 / TS, None, op0=ALU.mult),
                 [(c_dv, t_bas)], c_dv)
    t_bc2 = None
    for ee in range(8):
        p.dma(SY, pq[ee:ee + 1, 5:6], brow[0:1, 16 + ee:17 + ee],
              [(c_dv, t_ct)] if ee == 0 else (), c_ld)
        t_bc2 = p.dma(SY, pq[ee:ee + 1, 6:7], brow[0:1, 24 + ee:25 + ee], (), c_ld)
    t_ge = p.add(VE, lambda e: e.tensor_scalar(jrow8[:, :], jrow8[:, :], pq[:, 6:7],
                                               None, op0=ALU.is_ge),
                 [(c_ld, t_bc2), (c_gw, t_jr)], c_dv)
    t_emm = p.add(TE, lambda e: e.matmul(psC[0:1, 256:256 + NT_SP], ones_f[0:8, 0:1],
                                         jrow8[:, :], start=True, stop=True),
                  [(c_dv, t_ge)] + p6_w, c_pe)
    t_ecp = p.add(VE, lambda e: e.tensor_copy(etab_f[:, :], psC[0:1, 256:256 + NT_SP]),
                  [(c_pe, t_emm)], c_dv)
    t_ecl = p.add(VE, lambda e: e.tensor_scalar(etab_f[:, :], etab_f[:, :],
                                                7.0, None, op0=ALU.min),
                  [(c_dv, t_ecp)], c_dv)
    t_esc = p.add(VE, lambda e: e.tensor_scalar(etab_f[:, :], etab_f[:, :],
                                                128.0, None, op0=ALU.mult),
                  [(c_dv, t_ecl)], c_dv)
    t_eb = p.add(TE, lambda e: e.matmul(psC[:, 288:288 + NT_SP], ones_f[0:1, 0:128],
                                        etab_f[0:1, :], start=True, stop=True),
                 [(c_dv, t_esc)], c_pe)
    t_ei = p.add(VE, lambda e: e.tensor_scalar(
        wcol_f[:, :], psC[:, 288:288 + NT_SP], iota128[:, 0:1], None,
        op0=ALU.add), [(c_pe, t_eb), (c_gw, t_io)], c_dv)
    t_ei = p.add(VE, lambda e: e.tensor_copy(
        wcol_i[:, :], wcol_f[:, :]), [(c_dv, t_ei)], c_dv)
    t_etab = t_ei
    if "wcol_s" in dbg:
        p.dma(SY, wcol_dbg_h[:, 0:NT_SP], wcol_i[:, :], [(c_dv, t_ei)], c_st)
        p.dma(SY, pq_dbg_h[0], pq[:, :], [(c_dv, t_ei)], c_st)
        p.dma(SY, brow_dbg_h[0], brow[:, :], [(c_dv, t_ei)], c_st)
    t_dp = p.add(VE, lambda e: e.tensor_scalar(prow_d[:, :], prow_d[:, :],
                                               pq[:, 5:6], None, op0=ALU.add),
                 [(c_dv, t_ei)], c_dv)
    tok_pos = {}
    tok_wee = {}
    t_mmP = 0
    for lst in range(2):
        rows = slice(8, 16) if lst == 0 else slice(16, 24)
        t_lb = p.dma(SY, prow_b[:, :], wq3_h[rows, :],
                     [(c_dv, t_dp)], c_ld)
        t_c1 = p.add(VE, lambda e: e.tensor_mul(prow_c[:, :], prow_a[:, :],
                                                prow_b[:, :]),
                     [(c_ld, t_lb), (c_pe, t_mmP)], c_dv)
        for seg in range(T // 512):
            t_mmW = p.add(TE, lambda e, seg=seg: e.matmul(
                psC[0:1, 0:512], ones_f[0:8, 0:1],
                prow_c[:, seg * 512:(seg + 1) * 512], start=True, stop=True),
                [(c_dv, t_c1)] if seg == 0 else [(c_dv, t_evW)], c_pe)
            t_evW = p.add(VE, lambda e: e.tensor_copy(srow6[:, :], psC[0:1, 0:512]),
                          [(c_pe, t_mmW)], c_dv)
            tok_wee[(lst, seg)] = p.dma(
                SY, stage_h[lst * 2 + 1, seg * 512:(seg + 1) * 512],
                srow6[:, :], [(c_dv, t_evW)], c_st)
        t_c2 = p.add(VE, lambda e: e.tensor_mul(prow_c[:, :], prow_d[:, :],
                                                prow_b[:, :]),
                     [(c_pe, t_mmW)], c_dv)
        for seg in range(T // 512):
            t_mmP = p.add(TE, lambda e, seg=seg: e.matmul(
                psC[0:1, 0:512], ones_f[0:8, 0:1],
                prow_c[:, seg * 512:(seg + 1) * 512], start=True, stop=True),
                [(c_dv, t_c2)] if seg == 0 else [(c_dv, t_evP)], c_pe)
            t_evP = p.add(VE, lambda e: e.tensor_scalar(
                srow6[:, :], psC[0:1, 0:512], -1.0, None, op0=ALU.add),
                [(c_pe, t_mmP)], c_dv)
            tok_pos[(lst, seg)] = p.dma(
                SY, stage_h[lst * 2, seg * 512:(seg + 1) * 512],
                srow6[:, :], [(c_dv, t_evP)], c_st)
    t_pp_pe = t_mmP
    t_sc = 0
    for lst in range(2):
        for jj in range(T // 128):
            seg = jj // 4
            t_lp = p.dma(SY, lrow[0:128, 0:1].rearrange("q w -> q w"),
                         bass.AP(stage_h.tensor,
                                 (lst * 2) * T + jj * 128,
                                 [[1, 128], [1, 1]]),
                         [(c_st, tok_pos[(lst, seg)]), (c_gw, t_sc)], c_ld)
            t_lw = p.dma(SY, lrow[0:128, 1:2],
                         bass.AP(stage_h.tensor,
                                 (lst * 2 + 1) * T + jj * 128,
                                 [[1, 128], [1, 1]]),
                         [(c_st, tok_wee[(lst, seg)])], c_ld)
            t_pi = p.add(VE, lambda e: e.tensor_copy(posi[:, 0:1], lrow[0:128, 0:1]),
                         [(c_ld, t_lw), (c_gw, t_sc)], c_dv)
            t_vv = p.add(VE, lambda e, base=float(jj * 128): e.tensor_scalar(
                val2[:, 0:1], iota128[:, 0:1], base, None, op0=ALU.add),
                [(c_dv, t_pi), (c_gw, t_io)], c_dv)
            t_vw = p.add(VE, lambda e: e.tensor_copy(val2[:, 1:2], lrow[0:128, 1:2]),
                         [(c_dv, t_vv)], c_dv)
            t_sc = p.add(GP, lambda e: e.indirect_dma_start(
                flat0_h[0:128, :], IndirectOffsetOnAxis(posi[:, 0:1], 0),
                val2[:, 0:2], None),
                [(c_dv, t_vw), (c_st, t_fil)], c_gw, 16)
    t_scat = t_sc

    if stop_after_postproc:
        p.add(SY, lambda e: e.nop(), [(c_st, c_st.v), (c_ga, c_ga.v), (c_ga1, c_ga1.v), (c_cv, c_cv.v), (c_ld, c_ld.v),
                                      (c_gw, c_gw.v), (c_dv, c_dv.v), (c_pe, c_pe.v),
                                      (c_ac, c_ac.v)])
        p.emit_all()
        return nc

    t_uv_dv = {-1: 0, -2: 0}
    t_y_dv = {-1: 0, -2: 0}
    t_add = {-1: None, -2: None}
    t_tile_add_last = {}
    t_posi_dv = {}
    t_tile_y_done = {-1: 0}
    t_tile_uv_done = {-1: 0}
    t_out_st = {-2: 0, -1: 0}
    t_p7_dv = {-2: 0, -1: 0}
    t_add_final = {}
    p7_site = {}
    uvu = 0
    yu = 0
    addu = 0
    glob_j = 0

    def emit_p7(gtc):
        r0 = gtc * 128
        pb = gtc % 2
        k7 = gtc
        t_lf = p.dma(SY, fin[:, pb * D:(pb + 1) * D], f_h[r0:r0 + 128, :],
                     [t_add_fin, (c_dv, t_p7_dv[k7 - 2])], c_ld)
        t_lx = p.dma(SY, x1in[:, pb * D:(pb + 1) * D], x1_h[r0:r0 + 128, :],
                     [(c_st, tok_x1_st[gtc])], c_ld)
        t_nmrf, rstdf, nmrf = ln_stats(fin[:, pb * D:(pb + 1) * D], D, 0 + 8 * pb,
                                       [(c_ld, t_lf), (c_ac, p7_site.get(pb, 0))], sq7)
        t_lnf = p.add(SC, lambda e, r=rstdf, m=nmrf, pb=pb: e.activation(
            oub[:, pb * D:(pb + 1) * D], fin[:, pb * D:(pb + 1) * D], AF.Identity,
            scale=r, bias=m), [(c_dv, t_nmrf), (c_st, t_out_st[k7 - 2])], c_ac)
        p7_site[pb] = t_lnf
        t_o = p.add(VE, lambda e, pb=pb: e.tensor_add(
            oub[:, pb * D:(pb + 1) * D], oub[:, pb * D:(pb + 1) * D],
            x1in[:, pb * D:(pb + 1) * D]), [(c_ac, t_lnf), (c_ld, t_lx)], c_dv)
        t_p7_dv[k7] = t_o
        t_out_st[k7] = p.dma(SY, out_h[r0:r0 + 128, :], oub[:, pb * D:(pb + 1) * D],
                             [(c_dv, t_o)], c_st)

    last_fh = tok_fh_st[T // 128 - 1]
    last_r2 = tok_r2_st[T // 128 - 1]
    for j in range(NT_SP):
        pjb = glob_j % 2
        t_pb = p.dma(SY, pairblk[:, pjb * 8:(pjb + 1) * 8].rearrange(
            "q (k c) -> q k c", k=4),
            bass.AP(flat0_h.tensor, j * TS * 2,
                    [[2, 128], [256, 4], [1, 2]]),
            [(c_gw, t_scat), (c_dv, t_posi_dv.get(glob_j - 2, 0)),
             t_tile_add_last.get(glob_j - 2)], c_ld)
        t_pi6 = None
        for k in range(4):
            t_pi6 = p.add(VE, lambda e, k=k, pjb=pjb: e.tensor_copy(
                posi6[:, pjb * 4 + k:pjb * 4 + k + 1],
                pairblk[:, pjb * 8 + 2 * k:pjb * 8 + 2 * k + 1]),
                [(c_ld, t_pb)] if k == 0 else (), c_dv if k == 3 else None)
        t_posi_dv[glob_j] = t_pi6
        t_g4 = None
        for k in range(4):
            t_g4 = p.add(GP, lambda e, k=k, pjb=pjb: e.indirect_dma_start(
                r2g[:, k * D:(k + 1) * D], None,
                r2tm_h[0:128, :],
                IndirectOffsetOnAxis(posi6[:, pjb * 4 + k:pjb * 4 + k + 1], 0)),
                [(c_dv, t_pi6), (c_st, last_r2), (c_st, t_zr_st),
                 (c_pe, t_tile_uv_done[glob_j - 1])] if k == 0 else (),
                c_gw, 16)
        t_gT = None
        t_trp = None
        for k in range(4):
            for dc in range(NDC):
                t_trp = p.add(TE, lambda e, k=k, dc=dc: e.transpose(
                    psD[:, dc * 128:(dc + 1) * 128],
                    r2g[:, k * D + dc * 128:k * D + (dc + 1) * 128], ident_bf[:]),
                    ([(c_gw, t_g4)] + ([(c_dv, t_gT)] if t_gT else []))
                    if dc == 0 else (), c_pe if dc == NDC - 1 else None)
            for dc in range(NDC):
                t_gT = p.add(VE, lambda e, k=k, dc=dc: e.tensor_copy(
                    r2gT[:, dc * 512 + k * 128:dc * 512 + (k + 1) * 128],
                    psD[:, dc * 128:(dc + 1) * 128]),
                    [(c_pe, t_trp)] if dc == 0 else (),
                    c_dv if dc == NDC - 1 else None)
        t_gT_tok = t_gT
        ew = [(c_dv, t_etab), (c_cv, t_conv_done), (c_pe, t_pp_pe),
              (c_pe, t_tile_uv_done[glob_j - 1])]
        t_wl = {}
        wc = wcol_i[:, j:j + 1]
        t_wl[(0, 1)] = p.add(GP, lambda e, wc=wc: e.indirect_dma_start(
            w1s[:, :], None, w1c_h[0:128, :], IndirectOffsetOnAxis(wc, 0)),
            ew, c_gw, 16)
        t_wl[(1, 1)] = p.add(GP, lambda e, wc=wc: e.indirect_dma_start(
            w3s[:, :], None, w3c_h[0:128, :], IndirectOffsetOnAxis(wc, 0)),
            (), c_gw, 16)
        t_wl[(2, 1)] = p.add(GP, lambda e, wc=wc: e.indirect_dma_start(
            w2s[:, :], None, w2c_h[0:128, :], IndirectOffsetOnAxis(wc, 0)),
            [(c_pe, t_tile_y_done[glob_j - 1])], c_gw, 16)
        t_g6 = None
        t_uvmm = None
        for ic in range(NIC):
            pp = uvu % 2
            ub = psB if pp == 0 else psA
            uw = ()
            if ic == 0:
                uw = [(c_gw, t_wl[(1, 1)]), (c_dv, t_gT_tok),
                      (c_dv, t_uv_dv[uvu - 2])]
            elif ic == 8:
                uw = [(c_gw, t_wl[(1, 1)]), (c_dv, t_uv_dv[uvu - 2])]
            else:
                uw = [(c_dv, t_uv_dv[uvu - 2])]
            for dc in range(NDC):
                p.add(TE, lambda e, ic=ic, dc=dc, ub=ub: e.matmul(
                    ub[:, 0:512],
                    w1s[:, dc * I_DIM + ic * 128:dc * I_DIM + (ic + 1) * 128],
                    r2gT[:, dc * 512:(dc + 1) * 512],
                    start=(dc == 0), stop=(dc == NDC - 1)),
                    uw if dc == 0 else ())
            for dc in range(NDC):
                t_uvmm = p.add(TE, lambda e, ic=ic, dc=dc, ub=ub: e.matmul(
                    ub[:, 512:1024],
                    w3s[:, dc * I_DIM + ic * 128:dc * I_DIM + (ic + 1) * 128],
                    r2gT[:, dc * 512:(dc + 1) * 512],
                    start=(dc == 0), stop=(dc == NDC - 1)),
                    (), c_pe if dc == NDC - 1 else None)
            t_si = p.add(SC, lambda e, pp=pp, ub=ub: e.activation(
                silu_t[:, pp * 512:(pp + 1) * 512], ub[:, 0:512], AF.Silu),
                [(c_pe, t_uvmm)], c_ac)
            gw_ = [(c_ac, t_si)]
            if ic == 0:
                gw_ += [(c_pe, t_tile_y_done[glob_j - 1])]
            t_g6 = p.add(VE, lambda e, pp=pp, ub=ub, ic=ic: e.tensor_mul(
                g_t[:, ic * 512:(ic + 1) * 512],
                silu_t[:, pp * 512:(pp + 1) * 512], ub[:, 512:1024]), gw_, c_dv)
            t_uv_dv[uvu] = t_g6
            uvu += 1
        t_tile_uv_done[glob_j] = t_uvmm
        t_ymm = None
        for tch in range(4):
            t_ye = None
            for ot in range(2):
                yp = yu % 2
                yw = ()
                for ic in range(NIC):
                    if ic == 0:
                        yw = [(c_dv, t_g6), (c_gw, t_wl[(2, 1)]),
                              (c_dv, t_y_dv[yu - 2])]
                    else:
                        yw = ()
                    t_ymm = p.add(TE, lambda e, ic=ic, tch=tch, ot=ot, yp=yp: e.matmul(
                        psA[:, 1024 + yp * 512:1024 + (yp + 1) * 512],
                        g_t[:, ic * 512 + tch * 128:ic * 512 + tch * 128 + 128],
                        w2s[:, ic * D + ot * 512:ic * D + (ot + 1) * 512],
                        start=(ic == 0), stop=(ic == NIC - 1)),
                        yw, c_pe if ic == NIC - 1 else None)
                t_ye = p.add(VE, lambda e, tch=tch, ot=ot, yp=yp, pjb=pjb:
                             e.tensor_scalar(
                    ysc[:, (tch % 2) * D + ot * 512:(tch % 2) * D + (ot + 1) * 512],
                    psA[:, 1024 + yp * 512:1024 + (yp + 1) * 512],
                    pairblk[:, pjb * 8 + 2 * tch + 1:pjb * 8 + 2 * tch + 2],
                    None, op0=ALU.mult),
                    [(c_pe, t_ymm), t_add[addu - 2]], c_dv)
                t_y_dv[yu] = t_ye
                yu += 1
            t_ad = p.add(GP, lambda e, tch=tch, pjb=pjb: e.indirect_dma_start(
                f_h[0:128, :],
                IndirectOffsetOnAxis(posi6[:, pjb * 4 + tch:pjb * 4 + tch + 1], 0),
                ysc[:, (tch % 2) * D:(tch % 2 + 1) * D], None,
                compute_op=ALU.add),
                [(c_dv, t_ye), t_add[addu - 1], (c_st, last_fh)],
                c_ga, 16)
            t_add[addu] = (c_ga, t_ad)
            addu += 1
        t_tile_add_last[glob_j] = t_add[addu - 1]
        t_tile_y_done[glob_j] = t_ymm
        glob_j += 1
    t_add_fin = t_add[addu - 1]

    for gtc in range(T // 128):
        emit_p7(gtc)

    p.add(SY, lambda e: e.nop(), [(c_st, c_st.v), (c_ga, c_ga.v), (c_ga1, c_ga1.v), (c_cv, c_cv.v), (c_ld, c_ld.v),
                                  (c_gw, c_gw.v)])
    p.emit_all()
    return nc

def host_inputs(L0_attn_Z, L0_attn_decay, L0_attn_W, L0_attn_b, L0_attn_gamma,
                L0_attn_beta, L0_ffn_ln1_w, L0_ffn_ln2_w, L0_ffn_wq, L0_ffn_wk,
                L0_ffn_wv, L0_ffn_wo, L0_ffn_gate_w, L0_ffn_w1, L0_ffn_w2, L0_ffn_w3,
                n_cores=8):
    f = np.float32
    x = np.ascontiguousarray(np.asarray(L0_attn_Z, f))
    B, S_, D_ = x.shape
    per = B // n_cores
    common = dict(
        decay=np.asarray(L0_attn_decay, f).reshape(1, 1),
        b_row=np.ascontiguousarray(np.asarray(L0_attn_b, f).reshape(1, D_)),
        gamma_row=np.ascontiguousarray(np.asarray(L0_attn_gamma, f).reshape(1, D_)),
        beta_row=np.ascontiguousarray(np.asarray(L0_attn_beta, f).reshape(1, D_)),
        ln1w_row=np.ascontiguousarray(np.asarray(L0_ffn_ln1_w, f).reshape(1, D_)),
        ln2w_row=np.ascontiguousarray(np.asarray(L0_ffn_ln2_w, f).reshape(1, D_)),
        wT=np.ascontiguousarray(np.asarray(L0_attn_W, f).T),
        wqT=np.ascontiguousarray(np.asarray(L0_ffn_wq, f).T),
        wkT=np.ascontiguousarray(np.asarray(L0_ffn_wk, f).T),
        wvT=np.ascontiguousarray(np.asarray(L0_ffn_wv, f).T),
        woT=np.ascontiguousarray(np.asarray(L0_ffn_wo, f).T),
        gateT=np.ascontiguousarray(np.asarray(L0_ffn_gate_w, f).T),
        w1T=np.ascontiguousarray(np.asarray(L0_ffn_w1, f).transpose(0, 2, 1)),
        w3T=np.ascontiguousarray(np.asarray(L0_ffn_w3, f).transpose(0, 2, 1)),
        w2T=np.ascontiguousarray(np.asarray(L0_ffn_w2, f).transpose(0, 2, 1)),
    )
    in_maps = [dict(common, x0=np.ascontiguousarray(
        x[c * per:(c + 1) * per].reshape(per * S_, D_))) for c in range(n_cores)]
    return in_maps, (B, S_, D_, per)


def kernel(**inputs):
    in_maps, (B, S_, D_, per) = host_inputs(**inputs)
    nc = build_program(T=per * S_, S=S_)
    res = run_bass_kernel_spmd(nc, in_maps, core_ids=list(range(8)))
    out = np.concatenate([res.results[c]["out"] for c in range(8)], axis=0)
    return out.reshape(B, S_, D_)
